# version 02
# baseline (speedup 1.0000x reference)
import numpy as np
import concourse.bass as bass
import concourse.mybir as mybir
from concourse.bass_utils import run_bass_kernel_spmd
from contextlib import ExitStack

F32 = mybir.dt.float32
BF16 = mybir.dt.bfloat16
AF = mybir.ActivationFunctionType
ALU = mybir.AluOpType
AX = mybir.AxisListType

ENGINES = ("pe", "act", "dve", "pool", "sync")
BLOCK_ATTR = {"pe": "tensor", "act": "scalar", "dve": "vector", "pool": "gpsimd", "sync": "sync"}

NCORE = 8
TOKTOT = 16384
TOK = TOKTOT // NCORE
TT = TOK // 128
D = 1024
KB = D // 128
T = 4096
B = 4
H = 16
HD = 64
NEG = -30000.0
NEG8 = -240000.0


class Buf:
    __slots__ = ("writer", "readers", "name")

    def __init__(self, name=""):
        self.writer = None
        self.readers = []
        self.name = name


class Prog:
    def __init__(self, nc):
        self.nc = nc
        self.plan = {e: [] for e in ENGINES}
        self.counts = {e: 0 for e in ENGINES}
        self.dma_counts = {e: 0 for e in ENGINES}
        self.waited = {e: {} for e in ENGINES}
        self.pending = {e: {"r": [], "w": []} for e in ENGINES}
        self.sems = {}

    def op(self, eng, fn, reads=(), writes=(), dma=False, quiet=False):
        toks = []
        for b in reads:
            if b.writer:
                toks.append(b.writer)
        for b in writes:
            if b.writer:
                toks.append(b.writer)
            toks.extend(b.readers)
        need = {}
        for (s, v) in toks:
            if eng == "pe" and s == "pe":
                continue
            need[s] = max(need.get(s, 0), v)
        waits = []
        for s, v in sorted(need.items()):
            if self.waited[eng].get(s, 0) >= v:
                continue
            self.waited[eng][s] = v
            waits.append((s, v))
        if quiet:
            self.pending[eng]["r"].extend(reads)
            self.pending[eng]["w"].extend(writes)
            self.plan[eng].append((waits, fn, None))
            return None
        if dma:
            key = "d_" + (writes[0].name if writes else eng)
            self.dma_counts[key] = self.dma_counts.get(key, 0) + 1
            tok = (key, self.dma_counts[key] * 16)
        else:
            self.counts[eng] += 1
            tok = (eng, self.counts[eng])
        wl = list(writes) + self.pending[eng]["w"]
        rl = list(reads) + self.pending[eng]["r"]
        self.pending[eng]["r"] = []
        self.pending[eng]["w"] = []
        for b in wl:
            b.writer = tok
            b.readers = []
        for b in rl:
            b.readers.append(tok)
        self.plan[eng].append((waits, fn, tok))
        return tok

    def build(self, stack):
        nc = self.nc
        used = set()
        for e in ENGINES:
            for (waits, fn, tok) in self.plan[e]:
                if tok is not None:
                    used.add(tok[0])
                for (s, v) in waits:
                    used.add(s)
        for s in sorted(used):
            self.sems[s] = stack.enter_context(nc.semaphore(name="s_" + s))
        nums = sorted(h.num for h in self.sems.values())
        if nums:
            nc.gpsimd.dma_reset(range(nums[0], nums[-1] + 1))
            for h in self.sems.values():
                nc.gpsimd.sem_clear(h)
        nc.all_engine_barrier()
        block = stack.enter_context(nc.Block())
        for ename in ENGINES:
            items = self.plan[ename]
            if not items:
                continue

            def make(f_items):
                def run(eng):
                    for waits, fn, tok in f_items:
                        for (s, v) in waits:
                            eng.wait_ge(self.sems[s], v)
                        inst = fn(eng)
                        if tok is not None:
                            inst.then_inc(self.sems[tok[0]], 16 if tok[0].startswith("d_") else 1)
                return run

            getattr(block, BLOCK_ATTR[ename])(make(items))


class Ker:

    def __init__(self):
        self.nc = bass.Bass("TRN2", target_bir_lowering=False)
        self.stack = ExitStack()
        self.p = Prog(self.nc)
        self.bufs = {}

    def dram_in(self, name, shape, dt=F32):
        ap = self.nc.dram_tensor(name, list(shape), dt, kind="ExternalInput").ap()
        self.bufs[name] = Buf(name)
        return ap

    def dram_out(self, name, shape, dt=F32):
        ap = self.nc.dram_tensor(name, list(shape), dt, kind="ExternalOutput").ap()
        self.bufs[name] = Buf(name)
        return ap

    def dram_tmp(self, name, shape, dt=F32):
        ap = self.nc.dram_tensor(name, list(shape), dt, kind="Internal").ap()
        self.bufs[name] = Buf(name)
        return ap

    def sb(self, name, shape, dt=F32):
        t = self.stack.enter_context(self.nc.sbuf_tensor(name, list(shape), dt))
        self.bufs[name] = Buf(name)
        return t

    def ps(self, name, shape, dt=F32):
        t = self.stack.enter_context(self.nc.psum_tensor(name, list(shape), dt))
        self.bufs[name] = Buf(name)
        return t

    def b(self, name):
        return self.bufs[name]

    def prelude_ident(self):
        k, p = self, self.p
        self.identd = k.dram_in("ident", (128, 128), F32)
        self.id_f32 = k.sb("id_f32", (128, 128), F32)
        self.id_bf = k.sb("id_bf", (128, 128), BF16)
        self.ones1 = k.sb("ones1", (1, 128), F32)
        p.op("sync", lambda e: e.dma_start(out=self.id_f32[:, :], in_=self.identd[:, :]),
             writes=[k.b("id_f32")], dma=True)
        p.op("dve", lambda e: e.tensor_copy(out=self.id_bf[:, :], in_=self.id_f32[:, :]),
             reads=[k.b("id_f32")], writes=[k.b("id_bf")])
        p.op("pool", lambda e: e.memset(self.ones1[:, :], 1.0), writes=[k.b("ones1")])
        self.epscol = k.sb("epscol", (128, 1), F32)
        self.onecol = k.sb("onecol", (128, 1), F32)
        p.op("pool", lambda e: e.memset(self.epscol[:, :], 1e-5), writes=[k.b("epscol")])
        p.op("pool", lambda e: e.memset(self.onecol[:, :], 1.0), writes=[k.b("onecol")])

    def rank1_rows(self, name, src_dram, n, ps_buf, ps_name, out_dt=F32):
        k, p = self, self.p
        row = k.sb(name + "_row", (1, n), F32)
        out = k.sb(name + "_bc", (128, n), out_dt)
        p.op("sync", lambda e: e.dma_start(out=row[:, :], in_=src_dram[None, :]),
             writes=[k.b(name + "_row")], dma=True)
        nchunk = (n + 511) // 512
        for c in range(nchunk):
            j0, j1 = c * 512, min(n, (c + 1) * 512)
            p.op("pe", lambda e, j0=j0, j1=j1: e.matmul(ps_buf[:, j0:j1], self.ones1[:, :],
                                                        row[:, j0:j1], start=True, stop=True),
                 reads=[k.b("ones1"), k.b(name + "_row")], writes=[k.b(ps_name)],
                 quiet=(c != nchunk - 1))
        p.op("act", lambda e: e.copy(out=out[:, :], in_=ps_buf[:, 0:n]),
             reads=[k.b(ps_name)], writes=[k.b(name + "_bc")])
        return out

    def load_w_bf(self, wname, w_dram, kb, ncols, stag, stag_name, dst, dst_name, dst_off=0,
                  dst_stride=None, col0=0):
        k, p = self, self.p
        if dst_stride is None:
            dst_stride = ncols
        for i in range(kb):
            p.op("sync", lambda e, i=i: e.dma_start(out=stag[:, 0:ncols],
                                                    in_=w_dram[i * 128:(i + 1) * 128, col0:col0 + ncols]),
                 writes=[k.b(stag_name)], dma=True)
            p.op("dve", lambda e, i=i: e.tensor_copy(out=dst[:, dst_off + i * dst_stride: dst_off + i * dst_stride + ncols],
                                                     in_=stag[:, 0:ncols]),
                 reads=[k.b(stag_name)], writes=[k.b(dst_name)])

    def ln_plain(self, x_sb, x_name, out_sb, out_name, tmp, tmp_name, col, col_name, n=D, eps=1e-5):
        k, p = self, self.p
        rx, rc, rt, ro = k.b(x_name), k.b(col_name), k.b(tmp_name), k.b(out_name)
        p.op("dve", lambda e: e.reduce_sum(out=col[:, 0:1], in_=x_sb[:, 0:n], axis=AX.X),
             reads=[rx], writes=[rc])
        p.op("dve", lambda e: e.tensor_scalar_mul(out=col[:, 0:1], in0=col[:, 0:1], scalar1=1.0 / n),
             reads=[rc], writes=[rc])
        p.op("dve", lambda e: e.tensor_scalar_sub(out=out_sb[:, 0:n], in0=x_sb[:, 0:n], scalar1=col[:, 0:1]),
             reads=[rx, rc], writes=[ro])
        p.op("act", lambda e: e.activation(out=tmp[:, 0:n], in_=out_sb[:, 0:n], func=AF.Square),
             reads=[ro], writes=[rt])
        p.op("dve", lambda e: e.reduce_sum(out=col[:, 1:2], in_=tmp[:, 0:n], axis=AX.X),
             reads=[rt], writes=[rc])
        p.op("dve", lambda e: e.tensor_scalar_mul(out=col[:, 1:2], in0=col[:, 1:2], scalar1=1.0 / n),
             reads=[rc], writes=[rc])
        p.op("act", lambda e: e.activation(out=col[:, 2:3], in_=col[:, 1:2], func=AF.Ln,
                                           bias=self.epscol[:, 0:1]),
             reads=[rc, self.b("epscol")], writes=[rc])
        p.op("act", lambda e: e.activation(out=col[:, 3:4], in_=col[:, 2:3], func=AF.Exp, scale=-0.5),
             reads=[rc], writes=[rc])
        p.op("dve", lambda e: e.tensor_scalar_mul(out=out_sb[:, 0:n], in0=out_sb[:, 0:n], scalar1=col[:, 3:4]),
             reads=[ro, rc], writes=[ro])

    def tp_blocks(self, src_sb, src_name, tp_ps, tp_name, dst_sb, dst_name, nblk, dst_off=0,
                  ident=None, ident_name=None, src_off=0):
        k, p = self, self.p
        if ident is None:
            ident, ident_name = self.id_bf, "id_bf"
        for i in range(nblk):
            p.op("pe", lambda e, i=i: e.transpose(out=tp_ps[:, :],
                                                  in_=src_sb[:, src_off + i * 128: src_off + (i + 1) * 128],
                                                  identity=ident[:, :]),
                 reads=[k.b(src_name), k.b(ident_name)], writes=[k.b(tp_name)])
            p.op("dve", lambda e, i=i: e.tensor_copy(out=dst_sb[:, dst_off + i * 128: dst_off + (i + 1) * 128],
                                                     in_=tp_ps[:, :]),
                 reads=[k.b(tp_name)], writes=[k.b(dst_name)])

    def gemm(self, statT, stat_name, w_bf, w_name, out_ps, out_ps_name, ncols, wcols,
             stat_off=0, w_off=0, extra=None):
        k, p = self, self.p
        for kb in range(KB):
            p.op("pe", lambda e, kb=kb: e.matmul(out_ps[:, 0:ncols],
                                                 statT[:, stat_off + kb * 128: stat_off + (kb + 1) * 128],
                                                 w_bf[:, w_off + kb * wcols: w_off + kb * wcols + ncols],
                                                 start=(kb == 0), stop=(kb == KB - 1 and extra is None)),
                 reads=[k.b(stat_name), k.b(w_name)], writes=[k.b(out_ps_name)],
                 quiet=not (kb == KB - 1 and extra is None))
        if extra is not None:
            fn, rbufs = extra
            p.op("pe", fn, reads=rbufs, writes=[k.b(out_ps_name)])



def build_pre(with_proj):
    k = Ker()
    nc, p = k.nc, k.p
    with k.stack:
        if with_proj:
            ctx_d = k.dram_in("ctx", (TOK, D))
            xbase_d = k.dram_in("xbase", (TOK, D))
            wp_d = k.dram_in("wproj", (D, D))
            bp_d = k.dram_in("bproj", (D,))
            xout_d = k.dram_out("xout", (TOK, D))
        else:
            xin_d = k.dram_in("xin", (TOK, D))
        ln1s_d = k.dram_in("ln1s", (D,))
        ln1b_d = k.dram_in("ln1b", (D,))
        wq_d = k.dram_in("wq", (D, D))
        wk_d = k.dram_in("wk", (D, D))
        wv_d = k.dram_in("wv", (D, D))
        u_d = k.dram_out("u", (TOK, D))
        q_d = k.dram_out("q", (TOK, D))
        kk_d = k.dram_out("k", (TOK, D))
        v_d = k.dram_out("v", (TOK, D))
        outs = [q_d, kk_d, v_d]
        outnames = ["q", "k", "v"]

        k.prelude_ident()
        ps_r1 = k.ps("ps_r1", (128, 1024), F32)
        ps_tp = k.ps("ps_tp", (128, 128), BF16)
        ps_g0 = k.ps("ps_g0", (128, 512), F32)
        ps_g1 = k.ps("ps_g1", (128, 512), F32)

        scale_b = k.rank1_rows("lns", ln1s_d, D, ps_r1, "ps_r1")
        shift_b = k.rank1_rows("lnb", ln1b_d, D, ps_r1, "ps_r1")

        stag = k.sb("stag", (128, D), F32)
        w_bf = {}
        for nm, wd in (("wq", wq_d), ("wk", wk_d), ("wv", wv_d)):
            w_bf[nm] = k.sb(nm + "_bf", (128, KB * D), BF16)
            k.load_w_bf(nm, wd, KB, D, stag, "stag", w_bf[nm], nm + "_bf")
        if with_proj:
            wp_bf = k.sb("wp_bf", (128, KB * D), BF16)
            k.load_w_bf("wp", wp_d, KB, D, stag, "stag", wp_bf, "wp_bf")
            bp_row = k.sb("bp_row", (1, D), F32)
            p.op("sync", lambda e: e.dma_start(out=bp_row[:, :], in_=bp_d[None, :]),
                 writes=[k.b("bp_row")], dma=True)

        x_t = k.sb("x_t", (128, D), F32)
        u_t = k.sb("u_t", (128, D), F32)
        tmp = k.sb("tmp", (128, D), F32)
        col = k.sb("col", (128, 4), F32)
        h_bf = k.sb("h_bf", (128, D), BF16)
        hT = k.sb("hT", (128, KB * 128), BF16)
        out_sb = k.sb("out_sb", (128, 512), F32)
        if with_proj:
            ctx_t = k.sb("ctx_t", (128, D), F32)
            ctx_bf = k.sb("ctx_bf", (128, D), BF16)
            cT = k.sb("cT", (128, KB * 128), BF16)

        for tt in range(TT):
            r0 = tt * 128
            if with_proj:
                p.op("sync", lambda e, r0=r0: e.dma_start(out=ctx_t[:, :], in_=ctx_d[r0:r0 + 128, :]),
                     writes=[k.b("ctx_t")], dma=True)
                p.op("dve", lambda e: e.tensor_copy(out=ctx_bf[:, :], in_=ctx_t[:, :]),
                     reads=[k.b("ctx_t")], writes=[k.b("ctx_bf")])
                k.tp_blocks(ctx_bf, "ctx_bf", ps_tp, "ps_tp", cT, "cT", KB)
                p.op("sync", lambda e, r0=r0: e.dma_start(out=x_t[:, :], in_=xbase_d[r0:r0 + 128, :]),
                     writes=[k.b("x_t")], dma=True)
                for jt in range(2):
                    pg, pgn = (ps_g0, "ps_g0") if jt == 0 else (ps_g1, "ps_g1")
                    ex = (lambda e, jt=jt, pg=pg: e.matmul(pg[:, 0:512], k.ones1[:, :],
                                                           bp_row[:, jt * 512:(jt + 1) * 512],
                                                           start=False, stop=True),
                          [k.b("ones1"), k.b("bp_row")])
                    k.gemm(cT, "cT", wp_bf, "wp_bf", pg, pgn, 512, D, w_off=jt * 512, extra=ex)
                    p.op("dve", lambda e, jt=jt, pg=pg: e.tensor_add(out=x_t[:, jt * 512:(jt + 1) * 512],
                                                                     in0=x_t[:, jt * 512:(jt + 1) * 512],
                                                                     in1=pg[:, 0:512]),
                         reads=[k.b("x_t"), k.b(pgn)], writes=[k.b("x_t")])
                p.op("pool", lambda e, r0=r0: e.dma_start(out=xout_d[r0:r0 + 128, :], in_=x_t[:, :]),
                     reads=[k.b("x_t")], writes=[k.b("xout")], dma=True)
            else:
                p.op("sync", lambda e, r0=r0: e.dma_start(out=x_t[:, :], in_=xin_d[r0:r0 + 128, :]),
                     writes=[k.b("x_t")], dma=True)
            k.ln_plain(x_t, "x_t", u_t, "u_t", tmp, "tmp", col, "col")
            p.op("pool", lambda e, r0=r0: e.dma_start(out=u_d[r0:r0 + 128, :], in_=u_t[:, :]),
                 reads=[k.b("u_t")], writes=[k.b("u")], dma=True)
            p.op("dve", lambda e: e.tensor_mul(out=tmp[:, :], in0=u_t[:, :], in1=scale_b[:, :]),
                 reads=[k.b("u_t"), k.b("lns_bc")], writes=[k.b("tmp")])
            p.op("dve", lambda e: e.tensor_add(out=h_bf[:, :], in0=tmp[:, :], in1=shift_b[:, :]),
                 reads=[k.b("tmp"), k.b("lnb_bc")], writes=[k.b("h_bf")])
            k.tp_blocks(h_bf, "h_bf", ps_tp, "ps_tp", hT, "hT", KB)
            for wi, nm in enumerate(outnames):
                for jt in range(2):
                    pg, pgn = (ps_g0, "ps_g0") if jt == 0 else (ps_g1, "ps_g1")
                    k.gemm(hT, "hT", w_bf["w" + nm], "w" + nm + "_bf", pg, pgn, 512, D, w_off=jt * 512)
                    p.op("act", lambda e, pg=pg: e.copy(out=out_sb[:, :], in_=pg[:, 0:512]),
                         reads=[k.b(pgn)], writes=[k.b("out_sb")])
                    p.op("pool", lambda e, r0=r0, wi=wi, jt=jt: e.dma_start(
                        out=outs[wi][r0:r0 + 128, jt * 512:(jt + 1) * 512], in_=out_sb[:, :]),
                        reads=[k.b("out_sb")], writes=[k.b(outnames[wi])], dma=True)
        fin = [k.b(n) for n in outnames] + [k.b("u")] + ([k.b("xout")] if with_proj else [])
        p.op("dve", lambda e: e.engine_nop(), reads=fin)
        p.build(k.stack)
    return k



def build_attn(with_bias):
    k = Ker()
    nc, p = k.nc, k.p
    NP = 8
    with k.stack:
        qT_d = k.dram_in("qT", (NP, HD, T))
        kT_d = k.dram_in("kT", (NP, HD, T))
        v_d = k.dram_in("v", (NP, T, HD))
        lt_d = k.dram_in("ltmask", (128, 128))
        negu_d = k.dram_in("negu", (128, 128))
        if with_bias:
            bias_d = k.dram_in("bias", (2, T, T), BF16)
        ctx_d = k.dram_out("ctxo", (NP, T, HD))

        k.prelude_ident()
        ps_s = k.ps("ps_s", (128, 2048), F32)
        ps_tp = k.ps("ps_tp", (128, 128), BF16)
        ps_c = k.ps("ps_c", (128, 64), F32)

        lt_sb = k.sb("lt_sb", (128, 128), F32)
        negu_sb = k.sb("negu_sb", (128, 128), F32)
        p.op("sync", lambda e: e.dma_start(out=lt_sb[:, :], in_=lt_d[:, :]), writes=[k.b("lt_sb")], dma=True)
        p.op("sync", lambda e: e.dma_start(out=negu_sb[:, :], in_=negu_d[:, :]), writes=[k.b("negu_sb")], dma=True)

        stag1 = k.sb("stag1", (HD, T), F32)
        stag2 = k.sb("stag2", (128, TT * 2 * HD), F32)
        qT_bf, kT_bf, v_bf = [], [], []
        for b in range(B):
            qT_bf.append(k.sb("qT%d" % b, (HD, T), BF16))
            kT_bf.append(k.sb("kT%d" % b, (HD, T), BF16))
            v_bf.append(k.sb("v%d" % b, (128, (T // 128) * HD), BF16))
        scores = k.sb("scores", (128, T), F32)
        p_sb = k.sb("p_sb", (128, T), BF16)
        pT_sb = k.sb("pT_sb", (128, 128), BF16)
        mcol = k.sb("mcol", (128, 4), F32)
        ctx_sb = k.sb("ctx_sb", (128, HD), F32)
        if with_bias:
            bias_bf = k.sb("bias_bf", (128, T), BF16)
            bias_f = k.sb("bias_f", (128, T), F32)

        for hi in range(2):
            for b in range(B):
                pair = hi * 4 + b
                for src_d, dst, dstn in ((qT_d, qT_bf[b], "qT%d" % b), (kT_d, kT_bf[b], "kT%d" % b)):
                    p.op("sync", lambda e, pair=pair, src_d=src_d: e.dma_start(out=stag1[:, :], in_=src_d[pair, :, :]),
                         writes=[k.b("stag1")], dma=True)
                    p.op("dve", lambda e, dst=dst: e.tensor_copy(out=dst[:, :], in_=stag1[:, :]),
                         reads=[k.b("stag1")], writes=[k.b(dstn)])
                for half in range(4):
                    nt0 = half * (T // 512)
                    p.op("sync", lambda e, pair=pair, half=half: e.dma_start(
                        out=stag2.rearrange("p (t d) -> p t d", d=HD)[:, half * 8:(half + 1) * 8, :],
                        in_=v_d[pair].rearrange("(t p) d -> p t d", p=128)[:, half * 8:(half + 1) * 8, :]),
                        writes=[k.b("stag2")], dma=True)
                p.op("dve", lambda e, b=b: e.tensor_copy(out=v_bf[b][:, :], in_=stag2[:, :]),
                     reads=[k.b("stag2")], writes=[k.b("v%d" % b)])
            for qt in range(32):
                cov = (qt // 4 + 1) * 512
                if with_bias:
                    p.op("sync", lambda e, hi=hi, qt=qt, cov=cov: e.dma_start(
                        out=bias_bf[:, 0:cov], in_=bias_d[hi, qt * 128:(qt + 1) * 128, 0:cov]),
                        writes=[k.b("bias_bf")], dma=True)
                    p.op("dve", lambda e, cov=cov: e.tensor_copy(out=bias_f[:, 0:cov], in_=bias_bf[:, 0:cov]),
                         reads=[k.b("bias_bf")], writes=[k.b("bias_f")])
                for b in range(B):
                    pair = hi * 4 + b
                    qbn, kbn, vbn = "qT%d" % b, "kT%d" % b, "v%d" % b
                    nmb = (cov + 2047) // 2048
                    for mb in range(nmb):
                        c0 = mb * 2048
                        c1 = min(cov, c0 + 2048)
                        nch = (c1 - c0 + 511) // 512
                        for ch in range(nch):
                            s0 = c0 + ch * 512
                            p.op("pe", lambda e, qt=qt, s0=s0, c0=c0, b=b: e.matmul(
                                ps_s[:, s0 - c0:s0 - c0 + 512],
                                qT_bf[b][:, qt * 128:(qt + 1) * 128],
                                kT_bf[b][:, s0:s0 + 512], start=True, stop=True),
                                reads=[k.b(qbn), k.b(kbn)], writes=[k.b("ps_s")],
                                quiet=(ch != nch - 1))
                        for ch in range(nch):
                            s0 = c0 + ch * 512
                            if with_bias:
                                p.op("dve", lambda e, s0=s0, c0=c0: e.tensor_add(
                                    out=scores[:, s0:s0 + 512], in0=ps_s[:, s0 - c0:s0 - c0 + 512],
                                    in1=bias_f[:, s0:s0 + 512]),
                                    reads=[k.b("ps_s"), k.b("bias_f")], writes=[k.b("scores")])
                            else:
                                p.op("act", lambda e, s0=s0, c0=c0: e.copy(
                                    out=scores[:, s0:s0 + 512], in_=ps_s[:, s0 - c0:s0 - c0 + 512]),
                                    reads=[k.b("ps_s")], writes=[k.b("scores")])
                    if not with_bias:
                        dc = qt * 128
                        p.op("dve", lambda e, dc=dc: e.tensor_mul(out=scores[:, dc:dc + 128],
                                                                  in0=scores[:, dc:dc + 128], in1=lt_sb[:, :]),
                             reads=[k.b("scores"), k.b("lt_sb")], writes=[k.b("scores")])
                        p.op("dve", lambda e, dc=dc: e.tensor_add(out=scores[:, dc:dc + 128],
                                                                  in0=scores[:, dc:dc + 128], in1=negu_sb[:, :]),
                             reads=[k.b("scores"), k.b("negu_sb")], writes=[k.b("scores")])
                        if cov > dc + 128:
                            p.op("pool", lambda e, dc=dc, cov=cov: e.memset(scores[:, dc + 128:cov], NEG),
                                 writes=[k.b("scores")])
                    p.op("dve", lambda e, cov=cov: e.reduce_max(out=mcol[:, 0:1], in_=scores[:, 0:cov], axis=AX.X),
                         reads=[k.b("scores")], writes=[k.b("mcol")])
                    p.op("dve", lambda e: e.tensor_scalar_mul(out=mcol[:, 1:2], in0=mcol[:, 0:1], scalar1=-0.125),
                         reads=[k.b("mcol")], writes=[k.b("mcol")])
                    p.op("act", lambda e, cov=cov: e.activation(out=p_sb[:, 0:cov], in_=scores[:, 0:cov],
                                                                func=AF.Exp, bias=mcol[:, 1:2], scale=0.125),
                         reads=[k.b("scores"), k.b("mcol")], writes=[k.b("p_sb")])
                    p.op("dve", lambda e, cov=cov: e.reduce_sum(out=mcol[:, 2:3], in_=p_sb[:, 0:cov], axis=AX.X),
                         reads=[k.b("p_sb")], writes=[k.b("mcol")])
                    p.op("dve", lambda e: e.reciprocal(out=mcol[:, 3:4], in_=mcol[:, 2:3]),
                         reads=[k.b("mcol")], writes=[k.b("mcol")])
                    nblk = qt + 1
                    for sb in range(nblk):
                        p.op("pe", lambda e, sb=sb: e.transpose(out=ps_tp[:, :], in_=p_sb[:, sb * 128:(sb + 1) * 128],
                                                                identity=k.id_bf[:, :]),
                             reads=[k.b("p_sb"), k.b("id_bf")], writes=[k.b("ps_tp")])
                        p.op("act", lambda e: e.copy(out=pT_sb[:, :], in_=ps_tp[:, :]),
                             reads=[k.b("ps_tp")], writes=[k.b("pT_sb")])
                        p.op("pe", lambda e, sb=sb, b=b: e.matmul(ps_c[:, :], pT_sb[:, :],
                                                                  v_bf[b][:, sb * HD:(sb + 1) * HD],
                                                                  start=(sb == 0), stop=(sb == nblk - 1)),
                             reads=[k.b("pT_sb"), k.b(vbn)], writes=[k.b("ps_c")],
                             quiet=(sb != nblk - 1))
                    p.op("dve", lambda e: e.tensor_scalar_mul(out=ctx_sb[:, :], in0=ps_c[:, :], scalar1=mcol[:, 3:4]),
                         reads=[k.b("ps_c"), k.b("mcol")], writes=[k.b("ctx_sb")])
                    p.op("pool", lambda e, pair=pair, qt=qt: e.dma_start(
                        out=ctx_d[pair, qt * 128:(qt + 1) * 128, :], in_=ctx_sb[:, :]),
                        reads=[k.b("ctx_sb")], writes=[k.b("ctxo")], dma=True)
        p.op("dve", lambda e: e.engine_nop(), reads=[k.b("ctxo")])
        p.build(k.stack)
    return k



def build_big(tail):
    k = Ker()
    nc, p = k.nc, k.p
    HID = 2048
    JT = HID // 128
    NTC = 4
    with k.stack:
        ctx_d = k.dram_in("ctx", (TOK, D))
        u_d = k.dram_in("u", (TOK, D))
        xres_d = k.dram_in("xres", (TOK, D))
        wo_d = k.dram_in("wo", (D, D))
        bo_d = k.dram_in("bo", (D,))
        ln2s_d = k.dram_in("ln2s", (D,))
        ln2b_d = k.dram_in("ln2b", (D,))
        gw_d = k.dram_in("gw", (D, 8))
        w1_d = k.dram_in("w1", (8, D, HID))
        w2_d = k.dram_in("w2", (8, D, HID))
        w3_d = k.dram_in("w3", (8, HID, D))
        xnew_d = k.dram_out("xnew", (TOK, D))
        if tail == "qkv":
            wqkv_d = k.dram_in("wqkv", (D, 3 * D))
            bqkv_d = k.dram_in("bqkv", (3 * D,))
            qkv_d = k.dram_out("qkv", (TOK, 3 * D))
        else:
            un_d = k.dram_out("unorm", (TOK, D))
        x1_t_d = k.dram_tmp("x1scr", (TOK, D))
        hT_dram = k.dram_tmp("hTscr", (KB, 128, TOK), BF16)
        wt_dram = k.dram_tmp("wtscr", (8, TOK))

        k.prelude_ident()
        ps_r1 = k.ps("ps_r1", (128, 1024), F32)
        ps_a = k.ps("ps_a", (128, 512), F32)
        ps_b = k.ps("ps_b", (128, 512), F32)
        ps_tp = k.ps("ps_tp", (128, 128), BF16)
        ps_tf = k.ps("ps_tf", (128, 128), F32)
        ps_lg = k.ps("ps_lg", (128, 8), F32)

        scale_b = k.rank1_rows("lns", ln2s_d, D, ps_r1, "ps_r1")
        shift_b = k.rank1_rows("lnb", ln2b_d, D, ps_r1, "ps_r1")
        bo_row = k.sb("bo_row", (1, D), F32)
        p.op("sync", lambda e: e.dma_start(out=bo_row[:, :], in_=bo_d[None, :]), writes=[k.b("bo_row")], dma=True)

        stag = k.sb("stag", (128, 1024), F32)
        w1h_bf = k.sb("w1h_bf", (128, KB * 1024), BF16)
        wo_bf, _won = w1h_bf, "w1h_bf"
        k.load_w_bf("wo", wo_d, KB, D, stag, "stag", w1h_bf, "w1h_bf")
        gw_sb = k.sb("gw_sb", (128, KB * 8), F32)
        for kb in range(KB):
            p.op("sync", lambda e, kb=kb: e.dma_start(out=gw_sb[:, kb * 8:(kb + 1) * 8],
                                                      in_=gw_d[kb * 128:(kb + 1) * 128, :]),
                 writes=[k.b("gw_sb")], dma=True)

        x_t = k.sb("x_t", (128, D), F32)
        u_t = k.sb("u_t", (128, D), F32)
        tmp = k.sb("tmp", (128, D), F32)
        col = k.sb("col", (128, 8), F32)
        h_f = k.sb("h_f", (128, D), F32)
        h_bf = k.sb("h_bf", (128, D), BF16)
        hT = k.sb("hT", (128, KB * 128), BF16)
        ctx_t = k.sb("ctx_t", (128, D), F32)
        ctx_bf = k.sb("ctx_bf", (128, D), BF16)
        cT = k.sb("cT", (128, KB * 128), BF16)
        statf = k.sb("statf", (128, 128), F32)
        gE = k.sb("gE", (128, 8), F32)
        gC = k.sb("gC", (128, 8), F32)
        wfull = k.sb("wfull", (128, TT * 8), F32)
        wt_sb = k.sb("wt_sb", (8, TOK), F32)
        wrow = k.sb("wrow", (1, 1024), F32)
        wb_bf = k.sb("wb_bf", (128, 1024), BF16)

        for tt in range(TT):
            r0 = tt * 128
            p.op("sync", lambda e, r0=r0: e.dma_start(out=ctx_t[:, :], in_=ctx_d[r0:r0 + 128, :]),
                 writes=[k.b("ctx_t")], dma=True)
            p.op("dve", lambda e: e.tensor_copy(out=ctx_bf[:, :], in_=ctx_t[:, :]),
                 reads=[k.b("ctx_t")], writes=[k.b("ctx_bf")])
            k.tp_blocks(ctx_bf, "ctx_bf", ps_tp, "ps_tp", cT, "cT", KB)
            p.op("sync", lambda e, r0=r0: e.dma_start(out=x_t[:, :], in_=u_d[r0:r0 + 128, :]),
                 writes=[k.b("x_t")], dma=True)
            for jt in range(2):
                pg, pgn = (ps_a, "ps_a") if jt == 0 else (ps_b, "ps_b")
                ex = (lambda e, jt=jt, pg=pg: e.matmul(pg[:, 0:512], k.ones1[:, :],
                                                       bo_row[:, jt * 512:(jt + 1) * 512], start=False, stop=True),
                      [k.b("ones1"), k.b("bo_row")])
                k.gemm(cT, "cT", wo_bf, "w1h_bf", pg, pgn, 512, D, w_off=jt * 512, extra=ex)
                p.op("dve", lambda e, jt=jt, pg=pg: e.tensor_add(out=x_t[:, jt * 512:(jt + 1) * 512],
                                                                 in0=x_t[:, jt * 512:(jt + 1) * 512],
                                                                 in1=pg[:, 0:512]),
                     reads=[k.b("x_t"), k.b(pgn)], writes=[k.b("x_t")])
            p.op("pool", lambda e, r0=r0: e.dma_start(out=x1_t_d[r0:r0 + 128, :], in_=x_t[:, :]),
                 reads=[k.b("x_t")], writes=[k.b("x1scr")], dma=True)
            k.ln_plain(x_t, "x_t", u_t, "u_t", tmp, "tmp", col, "col")
            p.op("dve", lambda e: e.tensor_mul(out=h_f[:, :], in0=u_t[:, :], in1=scale_b[:, :]),
                 reads=[k.b("u_t"), k.b("lns_bc")], writes=[k.b("h_f")])
            p.op("dve", lambda e: e.tensor_add(out=h_f[:, :], in0=h_f[:, :], in1=shift_b[:, :]),
                 reads=[k.b("h_f"), k.b("lnb_bc")], writes=[k.b("h_f")])
            p.op("act", lambda e: e.copy(out=h_bf[:, :], in_=h_f[:, :]),
                 reads=[k.b("h_f")], writes=[k.b("h_bf")])
            k.tp_blocks(h_bf, "h_bf", ps_tp, "ps_tp", hT, "hT", KB)
            for kb in range(KB):
                p.op("pool", lambda e, kb=kb, r0=r0: e.dma_start(out=hT_dram[kb, :, r0:r0 + 128],
                                                                 in_=hT[:, kb * 128:(kb + 1) * 128]),
                     reads=[k.b("hT")], writes=[k.b("hTscr")], dma=True)
            for kb in range(KB):
                p.op("pe", lambda e, kb=kb: e.transpose(out=ps_tf[:, :], in_=h_f[:, kb * 128:(kb + 1) * 128],
                                                        identity=k.id_f32[:, :]),
                     reads=[k.b("h_f"), k.b("id_f32")], writes=[k.b("ps_tf")])
                p.op("dve", lambda e: e.tensor_copy(out=statf[:, :], in_=ps_tf[:, :]),
                     reads=[k.b("ps_tf")], writes=[k.b("statf")])
                p.op("pe", lambda e, kb=kb: e.matmul(ps_lg[:, :], statf[:, :], gw_sb[:, kb * 8:(kb + 1) * 8],
                                                     start=(kb == 0), stop=(kb == KB - 1)),
                     reads=[k.b("statf"), k.b("gw_sb")], writes=[k.b("ps_lg")],
                     quiet=(kb != KB - 1))
            p.op("dve", lambda e: e.reduce_max(out=col[:, 0:1], in_=ps_lg[:, :], axis=AX.X),
                 reads=[k.b("ps_lg")], writes=[k.b("col")])
            p.op("dve", lambda e: e.tensor_scalar(out=gC[:, :], in0=ps_lg[:, :], scalar1=col[:, 0:1],
                                                  scalar2=None, op0=ALU.is_ge),
                 reads=[k.b("ps_lg"), k.b("col")], writes=[k.b("gC")])
            p.op("dve", lambda e: e.scalar_tensor_tensor(out=gE[:, :], in0=gC[:, :], scalar=-1e30,
                                                         in1=ps_lg[:, :], op0=ALU.mult, op1=ALU.add),
                 reads=[k.b("gC"), k.b("ps_lg")], writes=[k.b("gE")])
            p.op("dve", lambda e: e.reduce_max(out=col[:, 2:3], in_=gE[:, :], axis=AX.X),
                 reads=[k.b("gE")], writes=[k.b("col")])
            p.op("dve", lambda e: e.tensor_scalar_mul(out=col[:, 1:2], in0=col[:, 0:1], scalar1=-1.0),
                 reads=[k.b("col")], writes=[k.b("col")])
            p.op("act", lambda e: e.activation(out=col[:, 3:4], in_=col[:, 2:3], func=AF.Exp, bias=col[:, 1:2]),
                 reads=[k.b("col")], writes=[k.b("col")])
            p.op("dve", lambda e: e.tensor_scalar_add(out=col[:, 4:5], in0=col[:, 3:4], scalar1=1.0),
                 reads=[k.b("col")], writes=[k.b("col")])
            p.op("dve", lambda e: e.reciprocal(out=col[:, 4:5], in_=col[:, 4:5]),
                 reads=[k.b("col")], writes=[k.b("col")])
            p.op("act", lambda e: e.activation(out=gE[:, :], in_=ps_lg[:, :], func=AF.Exp, bias=col[:, 1:2]),
                 reads=[k.b("ps_lg"), k.b("col")], writes=[k.b("gE")])
            p.op("dve", lambda e: e.tensor_scalar(out=gC[:, :], in0=ps_lg[:, :], scalar1=col[:, 2:3],
                                                  scalar2=None, op0=ALU.is_ge),
                 reads=[k.b("ps_lg"), k.b("col")], writes=[k.b("gC")])
            p.op("dve", lambda e: e.tensor_mul(out=gE[:, :], in0=gE[:, :], in1=gC[:, :]),
                 reads=[k.b("gE"), k.b("gC")], writes=[k.b("gE")])
            p.op("dve", lambda e, tt=tt: e.tensor_scalar_mul(out=wfull[:, tt * 8:(tt + 1) * 8], in0=gE[:, :],
                                                             scalar1=col[:, 4:5]),
                 reads=[k.b("gE"), k.b("col")], writes=[k.b("wfull")])
        for tt in range(TT):
            p.op("pe", lambda e, tt=tt: e.transpose(out=ps_tf[0:8, 0:128], in_=wfull[:, tt * 8:(tt + 1) * 8],
                                                    identity=k.id_f32[:, :]),
                 reads=[k.b("wfull"), k.b("id_f32")], writes=[k.b("ps_tf")])
            p.op("dve", lambda e, tt=tt: e.tensor_copy(out=wt_sb[0:8, tt * 128:(tt + 1) * 128],
                                                       in_=ps_tf[0:8, 0:128]),
                 reads=[k.b("ps_tf")], writes=[k.b("wt_sb")])
        p.op("pool", lambda e: e.dma_start(out=wt_dram[:, :], in_=wt_sb[:, :]),
             reads=[k.b("wt_sb")], writes=[k.b("wtscr")], dma=True)

        w2h_bf = k.sb("w2h_bf", (128, KB * 1024), BF16)
        w3h_bf = k.sb("w3h_bf", (128, KB * D), BF16)
        hT_c2 = k.sb("hT_c2", (128, 2 * KB * 512), BF16)
        G_sb = k.sb("G_sb", (128, 8 * 512), BF16)
        acc2 = k.sb("acc2", (128, 2 * KB * 512), F32)
        silu_t = k.sb("silu_t", (128, 512), F32)
        for pp in range(2):
            p0 = pp * 1024
            for tci in range(2):
                t0 = p0 + tci * 512
                for kb in range(KB):
                    p.op("sync", lambda e, kb=kb, t0=t0, tci=tci: e.dma_start(
                        out=hT_c2[:, (tci * KB + kb) * 512:(tci * KB + kb + 1) * 512],
                        in_=hT_dram[kb, :, t0:t0 + 512]),
                        reads=[k.b("hTscr")], writes=[k.b("hT_c2")], dma=True)
            for e_i in range(8):
                p.op("sync", lambda e, e_i=e_i, p0=p0: e.dma_start(out=wrow[:, :],
                                                                   in_=wt_dram[e_i, p0:p0 + 1024][None, :]),
                     reads=[k.b("wtscr")], writes=[k.b("wrow")], dma=True)
                for tci in range(2):
                    p.op("pe", lambda e, tci=tci: e.matmul(ps_r1[:, 512:1024], k.ones1[:, :],
                                                           wrow[:, tci * 512:(tci + 1) * 512],
                                                           start=True, stop=True),
                         reads=[k.b("ones1"), k.b("wrow")], writes=[k.b("ps_r1")])
                    p.op("act", lambda e, tci=tci: e.copy(out=wb_bf[:, tci * 512:(tci + 1) * 512],
                                                          in_=ps_r1[:, 512:1024]),
                         reads=[k.b("ps_r1")], writes=[k.b("wb_bf")])
                for hh in range(2):
                    k.load_w_bf("w1e", w1_d[e_i][:, hh * 1024:(hh + 1) * 1024], KB, 1024, stag, "stag",
                                w1h_bf, "w1h_bf")
                    k.load_w_bf("w2e", w2_d[e_i][:, hh * 1024:(hh + 1) * 1024], KB, 1024, stag, "stag",
                                w2h_bf, "w2h_bf")
                    k.load_w_bf("w3e", w3_d[e_i][hh * 1024:(hh + 1) * 1024], KB, D, stag, "stag",
                                w3h_bf, "w3h_bf")
                    for tci in range(2):
                        for jt in range(8):
                            for kb in range(KB):
                                p.op("pe", lambda e, jt=jt, kb=kb, tci=tci: e.matmul(
                                    ps_a[:, :], w1h_bf[:, kb * 1024 + jt * 128: kb * 1024 + (jt + 1) * 128],
                                    hT_c2[:, (tci * KB + kb) * 512:(tci * KB + kb + 1) * 512],
                                    start=(kb == 0), stop=(kb == KB - 1)),
                                    reads=[k.b("w1h_bf"), k.b("hT_c2")], writes=[k.b("ps_a")],
                                    quiet=(kb != KB - 1))
                            for kb in range(KB):
                                p.op("pe", lambda e, jt=jt, kb=kb, tci=tci: e.matmul(
                                    ps_b[:, :], w2h_bf[:, kb * 1024 + jt * 128: kb * 1024 + (jt + 1) * 128],
                                    hT_c2[:, (tci * KB + kb) * 512:(tci * KB + kb + 1) * 512],
                                    start=(kb == 0), stop=(kb == KB - 1)),
                                    reads=[k.b("w2h_bf"), k.b("hT_c2")], writes=[k.b("ps_b")],
                                    quiet=(kb != KB - 1))
                            p.op("act", lambda e: e.activation(out=silu_t[:, :], in_=ps_a[:, :], func=AF.Silu),
                                 reads=[k.b("ps_a")], writes=[k.b("silu_t")])
                            p.op("dve", lambda e, jt=jt: e.tensor_mul(out=G_sb[:, jt * 512:(jt + 1) * 512],
                                                                      in0=silu_t[:, :], in1=ps_b[:, :]),
                                 reads=[k.b("silu_t"), k.b("ps_b")], writes=[k.b("G_sb")])
                            p.op("dve", lambda e, jt=jt, tci=tci: e.tensor_mul(
                                out=G_sb[:, jt * 512:(jt + 1) * 512],
                                in0=G_sb[:, jt * 512:(jt + 1) * 512],
                                in1=wb_bf[:, tci * 512:(tci + 1) * 512]),
                                reads=[k.b("G_sb"), k.b("wb_bf")], writes=[k.b("G_sb")])
                        for dt in range(KB):
                            for jb in range(KB):
                                p.op("pe", lambda e, dt=dt, jb=jb: e.matmul(
                                    ps_r1[:, 0:512], w3h_bf[:, jb * D + dt * 128: jb * D + (dt + 1) * 128],
                                    G_sb[:, jb * 512:(jb + 1) * 512], start=(jb == 0), stop=(jb == KB - 1)),
                                    reads=[k.b("w3h_bf"), k.b("G_sb")], writes=[k.b("ps_r1")],
                                    quiet=(jb != KB - 1))
                            ai = (tci * KB + dt) * 512
                            if e_i == 0 and hh == 0:
                                p.op("dve", lambda e, ai=ai: e.tensor_copy(out=acc2[:, ai:ai + 512],
                                                                           in_=ps_r1[:, 0:512]),
                                     reads=[k.b("ps_r1")], writes=[k.b("acc2")])
                            else:
                                p.op("dve", lambda e, ai=ai: e.tensor_add(out=acc2[:, ai:ai + 512],
                                                                          in0=acc2[:, ai:ai + 512],
                                                                          in1=ps_r1[:, 0:512]),
                                     reads=[k.b("acc2"), k.b("ps_r1")], writes=[k.b("acc2")])
            for tci in range(2):
                for ti in range(4):
                    r0 = p0 + tci * 512 + ti * 128
                    p.op("sync", lambda e, r0=r0: e.dma_start(out=x_t[:, :], in_=x1_t_d[r0:r0 + 128, :]),
                         reads=[k.b("x1scr")], writes=[k.b("x_t")], dma=True)
                    p.op("sync", lambda e, r0=r0: e.dma_start(out=u_t[:, :], in_=xres_d[r0:r0 + 128, :]),
                         writes=[k.b("u_t")], dma=True)
                    for dt in range(KB):
                        ai = (tci * KB + dt) * 512 + ti * 128
                        p.op("pe", lambda e, ai=ai: e.transpose(
                            out=ps_tf[:, :], in_=acc2[:, ai:ai + 128],
                            identity=k.id_f32[:, :]),
                            reads=[k.b("acc2"), k.b("id_f32")], writes=[k.b("ps_tf")])
                        p.op("dve", lambda e, dt=dt: e.tensor_add(out=x_t[:, dt * 128:(dt + 1) * 128],
                                                                  in0=x_t[:, dt * 128:(dt + 1) * 128],
                                                                  in1=ps_tf[:, :]),
                             reads=[k.b("x_t"), k.b("ps_tf")], writes=[k.b("x_t")])
                    p.op("dve", lambda e: e.scalar_tensor_tensor(out=x_t[:, :], in0=x_t[:, :], scalar=0.5,
                                                                 in1=u_t[:, :], op0=ALU.mult, op1=ALU.add),
                         reads=[k.b("x_t"), k.b("u_t")], writes=[k.b("x_t")])
                    p.op("pool", lambda e, r0=r0: e.dma_start(out=xnew_d[r0:r0 + 128, :], in_=x_t[:, :]),
                         reads=[k.b("x_t")], writes=[k.b("xnew")], dma=True)

        qkv_arenas = [(w1h_bf, "w1h_bf"), (w2h_bf, "w2h_bf"), (w3h_bf, "w3h_bf")]
        if tail == "qkv":
            for part in range(3):
                ar, an = qkv_arenas[part]
                k.load_w_bf("wqkv" + str(part), wqkv_d, KB, 1024, stag, "stag", ar, an,
                            col0=part * 1024)
        for tt in range(TT):
            r0 = tt * 128
            p.op("sync", lambda e, r0=r0: e.dma_start(out=x_t[:, :], in_=xnew_d[r0:r0 + 128, :]),
                 reads=[k.b("xnew")], writes=[k.b("x_t")], dma=True)
            k.ln_plain(x_t, "x_t", u_t, "u_t", tmp, "tmp", col, "col")
            if tail == "ln":
                p.op("pool", lambda e, r0=r0: e.dma_start(out=un_d[r0:r0 + 128, :], in_=u_t[:, :]),
                     reads=[k.b("u_t")], writes=[k.b("unorm")], dma=True)
            else:
                p.op("act", lambda e: e.copy(out=h_bf[:, :], in_=u_t[:, :]),
                     reads=[k.b("u_t")], writes=[k.b("h_bf")])
                k.tp_blocks(h_bf, "h_bf", ps_tp, "ps_tp", hT, "hT", KB)
                for jt in range(6):
                    pg, pgn = (ps_a, "ps_a") if jt % 2 == 0 else (ps_b, "ps_b")
                    p.op("sync", lambda e, jt=jt: e.dma_start(out=wrow[:, 0:512],
                                                              in_=bqkv_d[jt * 512:(jt + 1) * 512][None, :]),
                         writes=[k.b("wrow")], dma=True)
                    ex = (lambda e, jt=jt, pg=pg: e.matmul(
                        pg[:, 0:512], k.ones1[:, :], wrow[:, 0:512],
                        start=False, stop=True), [k.b("ones1"), k.b("wrow")])
                    ar, an = qkv_arenas[jt // 2]
                    k.gemm(hT, "hT", ar, an, pg, pgn, 512, D, w_off=(jt % 2) * 512, extra=ex)
                    p.op("act", lambda e, pg=pg: e.copy(out=silu_t[:, :], in_=pg[:, 0:512]),
                         reads=[k.b(pgn)], writes=[k.b("silu_t")])
                    p.op("pool", lambda e, r0=r0, jt=jt: e.dma_start(
                        out=qkv_d[r0:r0 + 128, jt * 512:(jt + 1) * 512], in_=silu_t[:, :]),
                        reads=[k.b("silu_t")], writes=[k.b("qkv")], dma=True)
        fin = [k.b("xnew")] + ([k.b("qkv")] if tail == "qkv" else [k.b("unorm")])
        p.op("dve", lambda e: e.engine_nop(), reads=fin)
        p.build(k.stack)
    return k


def build_bias():
    k = Ker()
    nc, p = k.nc, k.p
    with k.stack:
        iota_d = k.dram_in("iota", (128, T))
        qcols_d = k.dram_in("qcols", (128, 128))
        qcols64_d = k.dram_in("qcols64", (64, 128))
        w1t_d = k.dram_in("w1t", (128, 1))
        b1t_d = k.dram_in("b1t", (128, 1))
        w2bd_d = k.dram_in("w2bd", (128, 64))
        b2t_d = k.dram_in("b2t", (64, 1))
        cpar_d = k.dram_in("cpar", (1, 1))
        lmul_d = k.dram_in("lmul", (1, 1))
        bias_d = k.dram_out("bias8", (128, 4, 16, T), BF16)

        k.prelude_ident()
        ps_r1 = k.ps("ps_r1", (128, 8), F32)
        ps_bb = k.ps("ps_bb", (64, 512), F32)

        iota = k.sb("iota_s", (128, T), F32)
        qcols = k.sb("qcols_s", (128, 128), F32)
        qcols64 = k.sb("qcols64_s", (64, 128), F32)
        w1t = k.sb("w1t_s", (128, 1), F32)
        b1t = k.sb("b1t_s", (128, 1), F32)
        w2bd_f = k.sb("w2bd_f", (128, 64), F32)
        w2bd = k.sb("w2bd_s", (128, 64), BF16)
        b2t = k.sb("b2t_s", (64, 1), F32)
        cpar = k.sb("cpar_s", (1, 1), F32)
        lmul = k.sb("lmul_s", (1, 1), F32)
        for nm, dst, src in (("iota_s", iota, iota_d), ("qcols_s", qcols, qcols_d), ("qcols64_s", qcols64, qcols64_d),
                             ("w1t_s", w1t, w1t_d), ("b1t_s", b1t, b1t_d), ("w2bd_f", w2bd_f, w2bd_d),
                             ("b2t_s", b2t, b2t_d), ("cpar_s", cpar, cpar_d), ("lmul_s", lmul, lmul_d)):
            p.op("sync", lambda e, dst=dst, src=src: e.dma_start(out=dst[:, :], in_=src[:, :]),
                 writes=[k.b(nm)], dma=True)
        p.op("dve", lambda e: e.tensor_copy(out=w2bd[:, :], in_=w2bd_f[:, :]),
             reads=[k.b("w2bd_f")], writes=[k.b("w2bd_s")])
        cc = k.sb("cc", (128, 4), F32)
        p.op("pe", lambda e: e.matmul(ps_r1[:, 0:1], k.ones1[:, :], cpar[:, :], start=True, stop=True),
             reads=[k.b("ones1"), k.b("cpar_s")], writes=[k.b("ps_r1")])
        p.op("act", lambda e: e.copy(out=cc[:, 0:1], in_=ps_r1[:, 0:1]), reads=[k.b("ps_r1")], writes=[k.b("cc")])
        p.op("pe", lambda e: e.matmul(ps_r1[:, 1:2], k.ones1[:, :], lmul[:, :], start=True, stop=True),
             reads=[k.b("ones1"), k.b("lmul_s")], writes=[k.b("ps_r1")])
        p.op("act", lambda e: e.activation(out=cc[:, 1:2], in_=ps_r1[:, 1:2], func=AF.Abs, scale=512.0),
             reads=[k.b("ps_r1")], writes=[k.b("cc")])
        b2x8 = k.sb("b2x8", (64, 1), F32)
        p.op("act", lambda e: e.activation(out=b2x8[:, :], in_=b2t[:, :], func=AF.Copy, scale=8.0),
             reads=[k.b("b2t_s")], writes=[k.b("b2x8")])

        ic = k.sb("ic", (128, 4), F32)
        rel_t = k.sb("rel_t", (128, 512), F32)
        ft = k.sb("ft", (128, 512), BF16)
        bf_f = k.sb("bf_f", (64, 512), F32)
        cmp_t = k.sb("cmp_t", (64, 512), F32)
        out_bf = k.sb("out_bf", (64, 512), BF16)

        for g in range(128):
            p.op("dve", lambda e, g=g: e.tensor_max(out=ic[:, 0:1], in0=qcols[:, g:g + 1], in1=cc[:, 1:2]),
                 reads=[k.b("qcols_s"), k.b("cc")], writes=[k.b("ic")])
            p.op("act", lambda e: e.activation(out=ic[:, 1:2], in_=ic[:, 0:1], func=AF.Abs, scale=cc[:, 0:1]),
                 reads=[k.b("ic"), k.b("cc")], writes=[k.b("ic")])
            p.op("act", lambda e: e.activation(out=ic[:, 2:3], in_=ic[:, 1:2], func=AF.Ln,
                                               bias=k.onecol[:, 0:1]),
                 reads=[k.b("ic"), k.b("onecol")], writes=[k.b("ic")])
            p.op("dve", lambda e: e.tensor_scalar_add(out=ic[:, 2:3], in0=ic[:, 2:3], scalar1=1e-6),
                 reads=[k.b("ic")], writes=[k.b("ic")])
            p.op("dve", lambda e: e.reciprocal(out=ic[:, 3:4], in_=ic[:, 2:3]),
                 reads=[k.b("ic")], writes=[k.b("ic")])
            trips = min(8, (g * 32 + 31) // 512 + 1)
            for st in range(trips):
                s0 = st * 512
                p.op("dve", lambda e, s0=s0, g=g: e.tensor_scalar_sub(out=rel_t[:, :], in0=iota[:, s0:s0 + 512],
                                                                      scalar1=qcols[:, g:g + 1]),
                     reads=[k.b("iota_s"), k.b("qcols_s")], writes=[k.b("rel_t")])
                p.op("act", lambda e: e.activation(out=rel_t[:, :], in_=rel_t[:, :], func=AF.Abs, scale=cc[:, 0:1]),
                     reads=[k.b("rel_t"), k.b("cc")], writes=[k.b("rel_t")])
                p.op("act", lambda e: e.activation(out=rel_t[:, :], in_=rel_t[:, :], func=AF.Ln,
                                                   bias=k.onecol[:, 0:1]),
                     reads=[k.b("rel_t"), k.b("onecol")], writes=[k.b("rel_t")])
                p.op("dve", lambda e: e.tensor_scalar_mul(out=rel_t[:, :], in0=rel_t[:, :], scalar1=ic[:, 3:4]),
                     reads=[k.b("rel_t"), k.b("ic")], writes=[k.b("rel_t")])
                p.op("act", lambda e: e.activation(out=ft[:, :], in_=rel_t[:, :], func=AF.Relu,
                                                   scale=w1t[:, 0:1], bias=b1t[:, 0:1]),
                     reads=[k.b("rel_t"), k.b("w1t_s"), k.b("b1t_s")], writes=[k.b("ft")])
                p.op("pe", lambda e: e.matmul(ps_bb[:, :], w2bd[:, :], ft[:, :], start=True, stop=True),
                     reads=[k.b("w2bd_s"), k.b("ft")], writes=[k.b("ps_bb")])
                p.op("act", lambda e: e.activation(out=bf_f[:, :], in_=ps_bb[:, :], func=AF.Copy,
                                                   scale=8.0),
                     reads=[k.b("ps_bb")], writes=[k.b("bf_f")])
                p.op("dve", lambda e: e.tensor_scalar_add(out=bf_f[:, :], in0=bf_f[:, :],
                                                          scalar1=b2x8[:, 0:1]),
                     reads=[k.b("bf_f"), k.b("b2x8")], writes=[k.b("bf_f")])
                if st >= (g * 32) // 512:
                    p.op("dve", lambda e, s0=s0, g=g: e.tensor_scalar(out=cmp_t[:, :], in0=iota[0:64, s0:s0 + 512],
                                                                      scalar1=qcols64[:, g:g + 1], scalar2=None,
                                                                      op0=ALU.is_gt),
                         reads=[k.b("iota_s"), k.b("qcols64_s")], writes=[k.b("cmp_t")])
                    p.op("dve", lambda e: e.tensor_scalar_sub(out=bf_f[:, :], in0=bf_f[:, :], scalar1=NEG8),
                         reads=[k.b("bf_f")], writes=[k.b("bf_f")])
                    p.op("dve", lambda e: e.scalar_tensor_tensor(out=cmp_t[:, :], in0=cmp_t[:, :], scalar=-1.0,
                                                                 in1=bf_f[:, :], op0=ALU.mult, op1=ALU.mult),
                         reads=[k.b("cmp_t"), k.b("bf_f")], writes=[k.b("cmp_t")])
                    p.op("dve", lambda e: e.tensor_add(out=bf_f[:, :], in0=bf_f[:, :], in1=cmp_t[:, :]),
                         reads=[k.b("bf_f"), k.b("cmp_t")], writes=[k.b("bf_f")])
                    p.op("dve", lambda e: e.tensor_scalar_add(out=bf_f[:, :], in0=bf_f[:, :], scalar1=NEG8),
                         reads=[k.b("bf_f")], writes=[k.b("bf_f")])
                p.op("dve", lambda e: e.tensor_copy(out=out_bf[:, :], in_=bf_f[:, :]),
                     reads=[k.b("bf_f")], writes=[k.b("out_bf")])
                p.op("pool", lambda e, g=g, s0=s0: e.dma_start(out=bias_d[g, :, :, s0:s0 + 512], in_=out_bf[:, :]),
                     reads=[k.b("out_bf")], writes=[k.b("bias8")], dma=True)
        p.op("dve", lambda e: e.engine_nop(), reads=[k.b("bias8")])
        p.build(k.stack)
    return k



def build_gate():
    k = Ker()
    nc, p = k.nc, k.p
    NT = TOKTOT // 128
    CAP = 4096.0
    with k.stack:
        uT_d = k.dram_in("uT", (D, TOKTOT))
        wg_d = k.dram_in("wg", (D, 8))
        uts_d = k.dram_in("utstrict", (128, 128))
        wtab_d = k.dram_out("wtab", (TOKTOT, 8))
        tots_d = k.dram_tmp("totscr", (2, 128, 8))
        pre_d = k.dram_tmp("prescr", (2, 128, 8))

        k.prelude_ident()
        ps_lg = k.ps("ps_lg", (128, 8), F32)
        ps_cs = k.ps("ps_cs", (128, 8), F32)
        ps_tot = k.ps("ps_tot", (1, 8), F32)
        ps_pre = k.ps("ps_pre", (128, 8), F32)

        uts = k.sb("uts", (128, 128), F32)
        p.op("sync", lambda e: e.dma_start(out=uts[:, :], in_=uts_d[:, :]), writes=[k.b("uts")], dma=True)
        onesc = k.sb("onesc", (128, 1), F32)
        p.op("pool", lambda e: e.memset(onesc[:, :], 1.0), writes=[k.b("onesc")])
        wg_sb = k.sb("wg_sb", (128, KB * 8), F32)
        for kb in range(KB):
            p.op("sync", lambda e, kb=kb: e.dma_start(out=wg_sb[:, kb * 8:(kb + 1) * 8],
                                                      in_=wg_d[kb * 128:(kb + 1) * 128, :]),
                 writes=[k.b("wg_sb")], dma=True)
        statq = k.sb("statq", (128, KB * 128), F32)
        p_sb = k.sb("p_sb", (128, 8), F32)
        t8 = k.sb("t8", (128, 8), F32)
        cols = k.sb("cols", (128, 8), F32)
        M0 = k.sb("M0", (128, NT * 8), F32)
        M1 = k.sb("M1", (128, NT * 8), F32)
        EX0 = k.sb("EX0", (128, NT * 8), F32)
        EX1 = k.sb("EX1", (128, NT * 8), F32)
        G0c = k.sb("G0c", (128, NT), F32)
        G1c = k.sb("G1c", (128, NT), F32)
        tsm = k.sb("tsm", (1, 8), F32)
        t0_sb = k.sb("t0_sb", (128, 8), F32)
        t1_sb = k.sb("t1_sb", (128, 8), F32)
        tot0 = k.sb("tot0", (1, 8), F32)
        pre_sb = k.sb("pre_sb", (128, 8), F32)
        pre0r = k.sb("pre0r", (1, NT * 8), F32)
        pre1r = k.sb("pre1r", (1, NT * 8), F32)
        cf = k.sb("cf", (128, 8), F32)
        wt_t = k.sb("wt_t", (128, 8), F32)

        for i in range(NT):
            c0 = i * 128
            for kb in range(KB):
                p.op("sync", lambda e, kb=kb, c0=c0: e.dma_start(out=statq[:, kb * 128:(kb + 1) * 128],
                                                                 in_=uT_d[kb * 128:(kb + 1) * 128, c0:c0 + 128]),
                     writes=[k.b("statq")], dma=True)
            for kb in range(KB):
                p.op("pe", lambda e, kb=kb: e.matmul(ps_lg[:, :], statq[:, kb * 128:(kb + 1) * 128],
                                                     wg_sb[:, kb * 8:(kb + 1) * 8],
                                                     start=(kb == 0), stop=(kb == KB - 1)),
                     reads=[k.b("statq"), k.b("wg_sb")], writes=[k.b("ps_lg")], quiet=(kb != KB - 1))
            p.op("dve", lambda e: e.reduce_max(out=cols[:, 0:1], in_=ps_lg[:, :], axis=AX.X),
                 reads=[k.b("ps_lg")], writes=[k.b("cols")])
            p.op("dve", lambda e: e.tensor_scalar_mul(out=cols[:, 1:2], in0=cols[:, 0:1], scalar1=-1.0),
                 reads=[k.b("cols")], writes=[k.b("cols")])
            p.op("act", lambda e: e.activation(out=p_sb[:, :], in_=ps_lg[:, :], func=AF.Exp, bias=cols[:, 1:2]),
                 reads=[k.b("ps_lg"), k.b("cols")], writes=[k.b("p_sb")])
            p.op("dve", lambda e: e.reduce_sum(out=cols[:, 2:3], in_=p_sb[:, :], axis=AX.X),
                 reads=[k.b("p_sb")], writes=[k.b("cols")])
            p.op("dve", lambda e: e.reciprocal(out=cols[:, 3:4], in_=cols[:, 2:3]),
                 reads=[k.b("cols")], writes=[k.b("cols")])
            p.op("dve", lambda e: e.tensor_scalar_mul(out=p_sb[:, :], in0=p_sb[:, :], scalar1=cols[:, 3:4]),
                 reads=[k.b("p_sb"), k.b("cols")], writes=[k.b("p_sb")])
            p.op("dve", lambda e: e.reduce_max(out=cols[:, 4:5], in_=p_sb[:, :], axis=AX.X),
                 reads=[k.b("p_sb")], writes=[k.b("cols")])
            p.op("dve", lambda e, i=i: e.tensor_scalar(out=M0[:, i * 8:(i + 1) * 8], in0=p_sb[:, :],
                                                       scalar1=cols[:, 4:5], scalar2=None, op0=ALU.is_ge),
                 reads=[k.b("p_sb"), k.b("cols")], writes=[k.b("M0")])
            p.op("dve", lambda e, i=i: e.scalar_tensor_tensor(out=t8[:, :], in0=M0[:, i * 8:(i + 1) * 8],
                                                              scalar=-1e30, in1=p_sb[:, :],
                                                              op0=ALU.mult, op1=ALU.add),
                 reads=[k.b("M0"), k.b("p_sb")], writes=[k.b("t8")])
            p.op("dve", lambda e: e.reduce_max(out=cols[:, 5:6], in_=t8[:, :], axis=AX.X),
                 reads=[k.b("t8")], writes=[k.b("cols")])
            p.op("dve", lambda e: e.tensor_scalar(out=t8[:, :], in0=p_sb[:, :], scalar1=cols[:, 5:6],
                                                  scalar2=None, op0=ALU.is_ge),
                 reads=[k.b("p_sb"), k.b("cols")], writes=[k.b("t8")])
            p.op("dve", lambda e, i=i: e.tensor_sub(out=M1[:, i * 8:(i + 1) * 8], in0=t8[:, :],
                                                    in1=M0[:, i * 8:(i + 1) * 8]),
                 reads=[k.b("t8"), k.b("M0")], writes=[k.b("M1")])
            p.op("dve", lambda e: e.tensor_add(out=cols[:, 6:7], in0=cols[:, 4:5], in1=cols[:, 5:6]),
                 reads=[k.b("cols")], writes=[k.b("cols")])
            p.op("dve", lambda e: e.reciprocal(out=cols[:, 6:7], in_=cols[:, 6:7]),
                 reads=[k.b("cols")], writes=[k.b("cols")])
            p.op("dve", lambda e, i=i: e.tensor_mul(out=G0c[:, i:i + 1], in0=cols[:, 4:5], in1=cols[:, 6:7]),
                 reads=[k.b("cols")], writes=[k.b("G0c")])
            p.op("dve", lambda e, i=i: e.tensor_mul(out=G1c[:, i:i + 1], in0=cols[:, 5:6], in1=cols[:, 6:7]),
                 reads=[k.b("cols")], writes=[k.b("G1c")])
            for mi, (Mb, EXb, Mn, EXn) in enumerate(((M0, EX0, "M0", "EX0"), (M1, EX1, "M1", "EX1"))):
                p.op("pe", lambda e, i=i, Mb=Mb: e.matmul(ps_cs[:, :], uts[:, :], Mb[:, i * 8:(i + 1) * 8],
                                                          start=True, stop=True),
                     reads=[k.b("uts"), k.b(Mn)], writes=[k.b("ps_cs")])
                p.op("act", lambda e, i=i, EXb=EXb: e.copy(out=EXb[:, i * 8:(i + 1) * 8], in_=ps_cs[:, :]),
                     reads=[k.b("ps_cs")], writes=[k.b(EXn)])
                p.op("pe", lambda e, i=i, Mb=Mb: e.matmul(ps_tot[:, :], onesc[:, :], Mb[:, i * 8:(i + 1) * 8],
                                                          start=True, stop=True),
                     reads=[k.b("onesc"), k.b(Mn)], writes=[k.b("ps_tot")])
                p.op("act", lambda e: e.copy(out=tsm[:, :], in_=ps_tot[:, :]),
                     reads=[k.b("ps_tot")], writes=[k.b("tsm")])
                p.op("pool", lambda e, mi=mi, i=i: e.dma_start(out=tots_d[mi, i, :][None, :], in_=tsm[:, :]),
                     reads=[k.b("tsm")], writes=[k.b("totscr")], dma=True)
        p.op("sync", lambda e: e.dma_start(out=t0_sb[:, :], in_=tots_d[0]), reads=[k.b("totscr")],
             writes=[k.b("t0_sb")], dma=True)
        p.op("sync", lambda e: e.dma_start(out=t1_sb[:, :], in_=tots_d[1]), reads=[k.b("totscr")],
             writes=[k.b("t1_sb")], dma=True)
        p.op("pe", lambda e: e.matmul(ps_pre[:, :], uts[:, :], t0_sb[:, :], start=True, stop=True),
             reads=[k.b("uts"), k.b("t0_sb")], writes=[k.b("ps_pre")])
        p.op("act", lambda e: e.copy(out=pre_sb[:, :], in_=ps_pre[:, :]), reads=[k.b("ps_pre")], writes=[k.b("pre_sb")])
        p.op("pool", lambda e: e.dma_start(out=pre_d[0], in_=pre_sb[:, :]), reads=[k.b("pre_sb")],
             writes=[k.b("prescr")], dma=True)
        p.op("pe", lambda e: e.matmul(ps_tot[:, :], onesc[:, :], t0_sb[:, :], start=True, stop=True),
             reads=[k.b("onesc"), k.b("t0_sb")], writes=[k.b("ps_tot")])
        p.op("act", lambda e: e.copy(out=tot0[:, :], in_=ps_tot[:, :]), reads=[k.b("ps_tot")], writes=[k.b("tot0")])
        p.op("pe", lambda e: e.matmul(ps_pre[:, :], uts[:, :], t1_sb[:, :], start=True, stop=False),
             reads=[k.b("uts"), k.b("t1_sb")], writes=[k.b("ps_pre")], quiet=True)
        p.op("pe", lambda e: e.matmul(ps_pre[:, :], k.ones1[:, :], tot0[:, :], start=False, stop=True),
             reads=[k.b("ones1"), k.b("tot0")], writes=[k.b("ps_pre")])
        p.op("act", lambda e: e.copy(out=pre_sb[:, :], in_=ps_pre[:, :]), reads=[k.b("ps_pre")], writes=[k.b("pre_sb")])
        p.op("pool", lambda e: e.dma_start(out=pre_d[1], in_=pre_sb[:, :]), reads=[k.b("pre_sb")],
             writes=[k.b("prescr")], dma=True)
        p.op("sync", lambda e: e.dma_start(out=pre0r[:, :], in_=pre_d[0].rearrange("a b -> (a b)")[None, :]),
             reads=[k.b("prescr")], writes=[k.b("pre0r")], dma=True)
        p.op("sync", lambda e: e.dma_start(out=pre1r[:, :], in_=pre_d[1].rearrange("a b -> (a b)")[None, :]),
             reads=[k.b("prescr")], writes=[k.b("pre1r")], dma=True)
        for i in range(NT):
            for mi, (EXb, EXn, prer, prern, Gc, Gcn, Mb, Mn, wc) in enumerate((
                    (EX0, "EX0", pre0r, "pre0r", G0c, "G0c", M0, "M0", 0),
                    (EX1, "EX1", pre1r, "pre1r", G1c, "G1c", M1, "M1", 3))):
                p.op("pe", lambda e, i=i, prer=prer: e.matmul(ps_pre[:, :], k.ones1[:, :],
                                                              prer[:, i * 8:(i + 1) * 8], start=True, stop=True),
                     reads=[k.b("ones1"), k.b(prern)], writes=[k.b("ps_pre")])
                p.op("dve", lambda e, i=i, EXb=EXb: e.tensor_add(out=cf[:, :], in0=EXb[:, i * 8:(i + 1) * 8],
                                                                 in1=ps_pre[:, :]),
                     reads=[k.b(EXn), k.b("ps_pre")], writes=[k.b("cf")])
                p.op("dve", lambda e, i=i, Mb=Mb: e.tensor_mul(out=cf[:, :], in0=cf[:, :],
                                                               in1=Mb[:, i * 8:(i + 1) * 8]),
                     reads=[k.b("cf"), k.b(Mn)], writes=[k.b("cf")])
                p.op("dve", lambda e, wc=wc: e.reduce_sum(out=cols[:, wc:wc + 1], in_=cf[:, :], axis=AX.X),
                     reads=[k.b("cf")], writes=[k.b("cols")])
                p.op("dve", lambda e, wc=wc: e.tensor_scalar(out=cols[:, wc + 1:wc + 2], in0=cols[:, wc:wc + 1],
                                                             scalar1=CAP, scalar2=None, op0=ALU.is_lt),
                     reads=[k.b("cols")], writes=[k.b("cols")])
                p.op("dve", lambda e, i=i, Gc=Gc, wc=wc: e.tensor_mul(out=cols[:, wc + 2:wc + 3],
                                                                     in0=cols[:, wc + 1:wc + 2],
                                                                     in1=Gc[:, i:i + 1]),
                     reads=[k.b("cols"), k.b(Gcn)], writes=[k.b("cols")])
            p.op("dve", lambda e, i=i: e.tensor_scalar_mul(out=wt_t[:, :], in0=M0[:, i * 8:(i + 1) * 8],
                                                           scalar1=cols[:, 2:3]),
                 reads=[k.b("M0"), k.b("cols")], writes=[k.b("wt_t")])
            p.op("dve", lambda e, i=i: e.scalar_tensor_tensor(out=wt_t[:, :], in0=M1[:, i * 8:(i + 1) * 8],
                                                              scalar=1.0, in1=wt_t[:, :],
                                                              op0=ALU.mult, op1=ALU.add),
                 reads=[k.b("M1"), k.b("wt_t"), k.b("cols")], writes=[k.b("wt_t")]) if False else None
            p.op("dve", lambda e, i=i: e.tensor_scalar_mul(out=t8[:, :], in0=M1[:, i * 8:(i + 1) * 8],
                                                           scalar1=cols[:, 5:6]),
                 reads=[k.b("M1"), k.b("cols")], writes=[k.b("t8")])
            p.op("dve", lambda e: e.tensor_add(out=wt_t[:, :], in0=wt_t[:, :], in1=t8[:, :]),
                 reads=[k.b("wt_t"), k.b("t8")], writes=[k.b("wt_t")])
            p.op("pool", lambda e, i=i: e.dma_start(out=wtab_d[i * 128:(i + 1) * 128, :], in_=wt_t[:, :]),
                 reads=[k.b("wt_t")], writes=[k.b("wtab")], dma=True)
        p.op("dve", lambda e: e.engine_nop(), reads=[k.b("wtab")])
        p.build(k.stack)
    return k


def build_moe1(final):
    k = Ker()
    nc, p = k.nc, k.p
    HID = 4096
    NTC = 4
    with k.stack:
        u_d = k.dram_in("u", (TOK, D))
        wt_d = k.dram_in("wtT", (8, TOK))
        xres_d = k.dram_in("xres", (TOK, D))
        w1_d = k.dram_in("w1", (8, D, HID))
        b1t_d = k.dram_in("b1tile", (8, 128, 32))
        w2_d = k.dram_in("w2", (8, HID, D))
        b2_d = k.dram_in("b2", (8, D))
        if final:
            y_d = k.dram_out("y", (TOK, D))
        else:
            y_d = k.dram_out("x6", (TOK, D))
            wh_d = k.dram_in("wh", (D, D))
            bh_d = k.dram_in("bh", (D,))
            wqp_d = k.dram_in("wqp", (D, D))
            bqp_d = k.dram_in("bqp", (D,))
            wkp_d = k.dram_in("wkp", (D, D))
            bkp_d = k.dram_in("bkp", (D,))
            wvp_d = k.dram_in("wvp", (D, D))
            bvp_d = k.dram_in("bvp", (D,))
            qp_d = k.dram_out("qp", (TOK, D))
            kp_d = k.dram_out("kp", (TOK, D))
            vp_d = k.dram_out("vp", (TOK, D))
            hp_d = k.dram_out("hp", (TOK, D))
        uT_dram = k.dram_tmp("uTscr", (KB, 128, TOK), BF16)

        k.prelude_ident()
        ps_r1 = k.ps("ps_r1", (128, 1024), F32)
        ps_a = k.ps("ps_a", (128, 512), F32)
        ps_tp = k.ps("ps_tp", (128, 128), BF16)
        ps_tf = k.ps("ps_tf", (128, 128), F32)

        x_t = k.sb("x_t", (128, D), F32)
        u_t = k.sb("u_t", (128, D), F32)
        tmp = k.sb("tmp", (128, D), F32)
        col = k.sb("col", (128, 8), F32)
        h_bf = k.sb("h_bf", (128, D), BF16)
        hT = k.sb("hT", (128, KB * 128), BF16)
        wt_bf = k.sb("wt_bf", (8, TOK), BF16)
        wt_f = k.sb("wt_f", (8, TOK), F32)
        p.op("sync", lambda e: e.dma_start(out=wt_f[:, :], in_=wt_d[:, :]), writes=[k.b("wt_f")], dma=True)
        p.op("dve", lambda e: e.tensor_copy(out=wt_bf[:, :], in_=wt_f[:, :]),
             reads=[k.b("wt_f")], writes=[k.b("wt_bf")])
        b2_f = k.sb("b2_f", (8, D), F32)
        b2_bf = k.sb("b2_bf", (8, D), BF16)
        p.op("sync", lambda e: e.dma_start(out=b2_f[:, :], in_=b2_d[:, :]), writes=[k.b("b2_f")], dma=True)
        p.op("dve", lambda e: e.tensor_copy(out=b2_bf[:, :], in_=b2_f[:, :]),
             reads=[k.b("b2_f")], writes=[k.b("b2_bf")])

        for tt in range(TT):
            r0 = tt * 128
            p.op("sync", lambda e, r0=r0: e.dma_start(out=u_t[:, :], in_=u_d[r0:r0 + 128, :]),
                 writes=[k.b("u_t")], dma=True)
            p.op("dve", lambda e: e.tensor_copy(out=h_bf[:, :], in_=u_t[:, :]),
                 reads=[k.b("u_t")], writes=[k.b("h_bf")])
            k.tp_blocks(h_bf, "h_bf", ps_tp, "ps_tp", hT, "hT", KB)
            for kb in range(KB):
                p.op("pool", lambda e, kb=kb, r0=r0: e.dma_start(out=uT_dram[kb, :, r0:r0 + 128],
                                                                 in_=hT[:, kb * 128:(kb + 1) * 128]),
                     reads=[k.b("hT")], writes=[k.b("uTscr")], dma=True)

        stag = k.sb("stag", (128, 2048), F32)
        w1h_bf = k.sb("w1h_bf", (128, KB * 2048), BF16)
        w2h_bf = k.sb("w2h_bf", (128, 16 * D), BF16)
        b1c = k.sb("b1c", (128, 32), F32)
        hT_c2 = k.sb("hT_c2", (128, 2 * KB * 512), BF16)
        G_sb = k.sb("G_sb", (128, 16 * 512), BF16)
        acc2 = k.sb("acc2", (128, 2 * KB * 512), F32)
        wb_bf = k.sb("wb_bf", (128, 2 * 512), BF16)
        wrow = k.sb("wrow", (1, 1024), F32)

        for pp in range(2):
            p0 = pp * 1024
            for tci in range(2):
                t0 = p0 + tci * 512
                for kb in range(KB):
                    p.op("sync", lambda e, kb=kb, t0=t0, tci=tci: e.dma_start(
                        out=hT_c2[:, (tci * KB + kb) * 512:(tci * KB + kb + 1) * 512],
                        in_=uT_dram[kb, :, t0:t0 + 512]),
                        reads=[k.b("uTscr")], writes=[k.b("hT_c2")], dma=True)
            for e_i in range(8):
                p.op("sync", lambda e, e_i=e_i: e.dma_start(out=b1c[:, :], in_=b1t_d[e_i, :, :]),
                     writes=[k.b("b1c")], dma=True)
                p.op("sync", lambda e, e_i=e_i, p0=p0: e.dma_start(out=wrow[:, :],
                                                                   in_=wt_d[e_i, p0:p0 + 1024][None, :]),
                     writes=[k.b("wrow")], dma=True)
                for tci in range(2):
                    p.op("pe", lambda e, tci=tci: e.matmul(ps_r1[:, 512:1024], k.ones1[:, :],
                                                           wrow[:, tci * 512:(tci + 1) * 512],
                                                           start=True, stop=True),
                         reads=[k.b("ones1"), k.b("wrow")], writes=[k.b("ps_r1")])
                    p.op("act", lambda e, tci=tci: e.copy(out=wb_bf[:, tci * 512:(tci + 1) * 512],
                                                          in_=ps_r1[:, 512:1024]),
                         reads=[k.b("ps_r1")], writes=[k.b("wb_bf")])
                for jh in range(2):
                    k.load_w_bf("w1h", w1_d[e_i][:, jh * 2048:(jh + 1) * 2048], KB, 2048, stag, "stag",
                                w1h_bf, "w1h_bf")
                    for half in range(2):
                        k.load_w_bf("w2h", w2_d[e_i][jh * 2048 + half * 1024: jh * 2048 + (half + 1) * 1024],
                                    KB, D, stag, "stag", w2h_bf, "w2h_bf", dst_off=half * KB * D)
                    for tci in range(2):
                        t0 = p0 + tci * 512
                        for jt in range(16):
                            jj = jh * 16 + jt
                            for kb in range(KB):
                                p.op("pe", lambda e, jt=jt, kb=kb, tci=tci: e.matmul(
                                    ps_a[:, :], w1h_bf[:, kb * 2048 + jt * 128: kb * 2048 + (jt + 1) * 128],
                                    hT_c2[:, (tci * KB + kb) * 512:(tci * KB + kb + 1) * 512],
                                    start=(kb == 0), stop=(kb == KB - 1)),
                                    reads=[k.b("w1h_bf"), k.b("hT_c2")], writes=[k.b("ps_a")],
                                    quiet=(kb != KB - 1))
                            p.op("act", lambda e, jt=jt, jj=jj: e.activation(out=G_sb[:, jt * 512:(jt + 1) * 512],
                                                                             in_=ps_a[:, :], func=AF.Relu,
                                                                             bias=b1c[:, jj:jj + 1]),
                                 reads=[k.b("ps_a"), k.b("b1c")], writes=[k.b("G_sb")])
                            p.op("dve", lambda e, jt=jt, tci=tci: e.tensor_mul(
                                out=G_sb[:, jt * 512:(jt + 1) * 512],
                                in0=G_sb[:, jt * 512:(jt + 1) * 512],
                                in1=wb_bf[:, tci * 512:(tci + 1) * 512]),
                                reads=[k.b("G_sb"), k.b("wb_bf")], writes=[k.b("G_sb")])
                        for dt in range(KB):
                            if e_i == 0 and jh == 0:
                                p.op("pe", lambda e, dt=dt, t0=t0: e.matmul(
                                    ps_r1[:, 0:512], b2_bf[:, dt * 128:(dt + 1) * 128], wt_bf[:, t0:t0 + 512],
                                    start=True, stop=False),
                                    reads=[k.b("b2_bf"), k.b("wt_bf")], writes=[k.b("ps_r1")], quiet=True)
                            for jb in range(16):
                                p.op("pe", lambda e, dt=dt, jb=jb: e.matmul(
                                    ps_r1[:, 0:512], w2h_bf[:, jb * D + dt * 128: jb * D + (dt + 1) * 128],
                                    G_sb[:, jb * 512:(jb + 1) * 512],
                                    start=(jb == 0 and not (e_i == 0 and jh == 0)), stop=(jb == 15)),
                                    reads=[k.b("w2h_bf"), k.b("G_sb")], writes=[k.b("ps_r1")],
                                    quiet=(jb != 15))
                            ai = (tci * KB + dt) * 512
                            if e_i == 0 and jh == 0:
                                p.op("dve", lambda e, ai=ai: e.tensor_copy(out=acc2[:, ai:ai + 512],
                                                                           in_=ps_r1[:, 0:512]),
                                     reads=[k.b("ps_r1")], writes=[k.b("acc2")])
                            else:
                                p.op("dve", lambda e, ai=ai: e.tensor_add(out=acc2[:, ai:ai + 512],
                                                                          in0=acc2[:, ai:ai + 512],
                                                                          in1=ps_r1[:, 0:512]),
                                     reads=[k.b("acc2"), k.b("ps_r1")], writes=[k.b("acc2")])
            for tci in range(2):
                for ti in range(4):
                    r0 = p0 + tci * 512 + ti * 128
                    p.op("sync", lambda e, r0=r0: e.dma_start(out=u_t[:, :], in_=xres_d[r0:r0 + 128, :]),
                         writes=[k.b("u_t")], dma=True)
                    for dt in range(KB):
                        ai = (tci * KB + dt) * 512 + ti * 128
                        p.op("pe", lambda e, ai=ai: e.transpose(
                            out=ps_tf[:, :], in_=acc2[:, ai:ai + 128],
                            identity=k.id_f32[:, :]),
                            reads=[k.b("acc2"), k.b("id_f32")], writes=[k.b("ps_tf")])
                        p.op("dve", lambda e, dt=dt: e.scalar_tensor_tensor(
                            out=x_t[:, dt * 128:(dt + 1) * 128], in0=ps_tf[:, :], scalar=0.5,
                            in1=u_t[:, dt * 128:(dt + 1) * 128], op0=ALU.mult, op1=ALU.add),
                            reads=[k.b("ps_tf"), k.b("u_t")], writes=[k.b("x_t")])
                    p.op("pool", lambda e, r0=r0: e.dma_start(out=y_d[r0:r0 + 128, :], in_=x_t[:, :]),
                         reads=[k.b("x_t")], writes=[k.b("y" if final else "x6")], dma=True)

        if not final:
            names = [("wqp", "bqp", "qp", True), ("wkp", "bkp", "kp", True),
                     ("wvp", "bvp", "vp", False), ("wh", "bh", "hp", False)]
            wds = {"wqp": wqp_d, "wkp": wkp_d, "wvp": wvp_d, "wh": wh_d}
            bds = {"bqp": bqp_d, "bkp": bkp_d, "bvp": bvp_d, "bh": bh_d}
            outs = {"qp": qp_d, "kp": kp_d, "vp": vp_d, "hp": hp_d}
            for wi, (wn, bn, on, sig) in enumerate(names):
                arena, an, off = (w1h_bf, "w1h_bf", (wi % 2) * KB * D)
                if wi >= 2:
                    arena, an = w2h_bf, "w2h_bf"
                k.load_w_bf(wn, wds[wn], KB, D, stag, "stag", arena, an, dst_off=off)
            brow = k.sb("brow", (1, D), F32)
            out_sb = k.sb("out_sb", (128, 512), F32)
            for tt in range(TT):
                r0 = tt * 128
                p.op("sync", lambda e, r0=r0: e.dma_start(out=x_t[:, :], in_=y_d[r0:r0 + 128, :]),
                     reads=[k.b("x6")], writes=[k.b("x_t")], dma=True)
                k.ln_plain(x_t, "x_t", u_t, "u_t", tmp, "tmp", col, "col")
                p.op("act", lambda e: e.copy(out=h_bf[:, :], in_=u_t[:, :]),
                     reads=[k.b("u_t")], writes=[k.b("h_bf")])
                k.tp_blocks(h_bf, "h_bf", ps_tp, "ps_tp", hT, "hT", KB)
                for wi, (wn, bn, on, sig) in enumerate(names):
                    arena, an, off = (w1h_bf, "w1h_bf", (wi % 2) * KB * D)
                    if wi >= 2:
                        arena, an = w2h_bf, "w2h_bf"
                    p.op("sync", lambda e, bn=bn: e.dma_start(out=brow[:, :], in_=bds[bn][None, :]),
                         writes=[k.b("brow")], dma=True)
                    for jt in range(2):
                        ex = (lambda e, jt=jt: e.matmul(
                            ps_a[:, 0:512], k.ones1[:, :], brow[:, jt * 512:(jt + 1) * 512],
                            start=False, stop=True), [k.b("ones1"), k.b("brow")])
                        k.gemm(hT, "hT", arena, an, ps_a, "ps_a", 512, D, w_off=off + jt * 512, extra=ex)
                        p.op("act", lambda e, sig=sig: e.activation(out=out_sb[:, :], in_=ps_a[:, 0:512],
                                                                    func=(AF.Sigmoid if sig else AF.Copy)),
                             reads=[k.b("ps_a")], writes=[k.b("out_sb")])
                        p.op("pool", lambda e, r0=r0, on=on, jt=jt: e.dma_start(
                            out=outs[on][r0:r0 + 128, jt * 512:(jt + 1) * 512], in_=out_sb[:, :]),
                            reads=[k.b("out_sb")], writes=[k.b(on)], dma=True)
        fin = [k.b("y" if final else "x6")] + ([] if final else [k.b(n) for n in ("qp", "kp", "vp", "hp")])
        p.op("dve", lambda e: e.engine_nop(), reads=fin)
        p.build(k.stack)
    return k



def build_linattn():
    k = Ker()
    nc, p = k.nc, k.p
    NT = T // 128
    with k.stack:
        qp_d = k.dram_in("qp", (TOK, D))
        kp_d = k.dram_in("kpf", (T, D))
        vp_d = k.dram_in("vpf", (T, D))
        hp_d = k.dram_in("hp", (TOK, D))
        x6_d = k.dram_in("x6h", (TOK, D))
        lnw_d = k.dram_in("lnwp", (D,))
        lnb_d = k.dram_in("lnbp", (D,))
        beta_d = k.dram_in("betap", (D,))
        wo_d = k.dram_in("wop", (D, D))
        bo_d = k.dram_in("bop", (D,))
        x7_d = k.dram_out("x7", (TOK, D))
        u7_d = k.dram_out("u7", (TOK, D))

        k.prelude_ident()
        ps_r1 = k.ps("ps_r1", (128, 1024), F32)
        ps_kv = k.ps("ps_kv", (64, 65), F32)
        ps_tp = k.ps("ps_tp", (128, 128), BF16)
        ps_no = k.ps("ps_no", (128, 65), F32)
        ps_a = k.ps("ps_a", (128, 512), F32)
        ps_b = k.ps("ps_b", (128, 512), F32)

        lnw_bc = k.rank1_rows("lnw", lnw_d, D, ps_r1, "ps_r1")
        lnb_bc = k.rank1_rows("lnb", lnb_d, D, ps_r1, "ps_r1")
        beta_bc = k.rank1_rows("beta", beta_d, D, ps_r1, "ps_r1")
        bo_row = k.sb("bo_row", (1, D), F32)
        p.op("sync", lambda e: e.dma_start(out=bo_row[:, :], in_=bo_d[None, :]), writes=[k.b("bo_row")], dma=True)
        stag = k.sb("stag", (128, D), F32)
        wo_bf = k.sb("wo_bf", (128, KB * D), BF16)
        k.load_w_bf("wo", wo_d, KB, D, stag, "stag", wo_bf, "wo_bf")

        stagk = k.sb("stagk", (128, NT * HD), F32)
        kp_bf = k.sb("kp_bf", (128, NT * HD), BF16)
        v65 = k.sb("v65", (128, NT * 65), BF16)
        kv_all = k.sb("kv_all", (64, H * 65), BF16)
        for h in range(H):
            for quarter in range(4):
                p.op("sync", lambda e, h=h, quarter=quarter: e.dma_start(
                    out=stagk.rearrange("p (t d) -> p t d", d=HD)[:, quarter * 8:(quarter + 1) * 8, :],
                    in_=kp_d[:, h * HD:(h + 1) * HD].rearrange("(t p) d -> p t d", p=128)[:, quarter * 8:(quarter + 1) * 8, :]),
                    writes=[k.b("stagk")], dma=True)
            p.op("dve", lambda e: e.tensor_copy(out=kp_bf[:, :], in_=stagk[:, :]),
                 reads=[k.b("stagk")], writes=[k.b("kp_bf")])
            for quarter in range(4):
                p.op("sync", lambda e, h=h, quarter=quarter: e.dma_start(
                    out=stagk.rearrange("p (t d) -> p t d", d=HD)[:, quarter * 8:(quarter + 1) * 8, :],
                    in_=vp_d[:, h * HD:(h + 1) * HD].rearrange("(t p) d -> p t d", p=128)[:, quarter * 8:(quarter + 1) * 8, :]),
                    writes=[k.b("stagk")], dma=True)
            p.op("pool", lambda e: e.memset(v65[:, :], 1.0), writes=[k.b("v65")])
            p.op("dve", lambda e: e.tensor_copy(out=v65.rearrange("p (t d) -> p t d", d=65)[:, :, 0:HD],
                                                in_=stagk.rearrange("p (t d) -> p t d", d=HD)[:, :, :]),
                 reads=[k.b("stagk")], writes=[k.b("v65")])
            for nt in range(NT):
                p.op("pe", lambda e, h=h, nt=nt: e.matmul(ps_kv[:, :], kp_bf[:, nt * HD:(nt + 1) * HD],
                                                          v65[:, nt * 65:(nt + 1) * 65],
                                                          start=(nt == 0), stop=(nt == NT - 1)),
                     reads=[k.b("kp_bf"), k.b("v65")], writes=[k.b("ps_kv")], quiet=(nt != NT - 1))
            p.op("act", lambda e, h=h: e.copy(out=kv_all[:, h * 65:(h + 1) * 65], in_=ps_kv[:, :]),
                 reads=[k.b("ps_kv")], writes=[k.b("kv_all")])

        qp_t = k.sb("qp_t", (128, D), F32)
        qp_bf = k.sb("qp_bf", (128, D), BF16)
        qT_sb = k.sb("qT_sb", (64, 128), BF16)
        attn_sb = k.sb("attn_sb", (128, D), F32)
        hp_t = k.sb("hp_t", (128, D), F32)
        x_t = k.sb("x_t", (128, D), F32)
        u_t = k.sb("u_t", (128, D), F32)
        tmp = k.sb("tmp", (128, D), F32)
        col = k.sb("col", (128, 8), F32)
        g_bf = k.sb("g_bf", (128, D), BF16)
        gT = k.sb("gT", (128, KB * 128), BF16)
        rcol = k.sb("rcol", (128, 1), F32)

        for tt in range(TT):
            r0 = tt * 128
            p.op("sync", lambda e, r0=r0: e.dma_start(out=qp_t[:, :], in_=qp_d[r0:r0 + 128, :]),
                 writes=[k.b("qp_t")], dma=True)
            p.op("dve", lambda e: e.tensor_copy(out=qp_bf[:, :], in_=qp_t[:, :]),
                 reads=[k.b("qp_t")], writes=[k.b("qp_bf")])
            p.op("sync", lambda e, r0=r0: e.dma_start(out=hp_t[:, :], in_=hp_d[r0:r0 + 128, :]),
                 writes=[k.b("hp_t")], dma=True)
            for h in range(H):
                p.op("pe", lambda e, h=h: e.transpose(out=ps_tp[0:HD, :], in_=qp_bf[:, h * HD:(h + 1) * HD],
                                                      identity=k.id_bf[:, :]),
                     reads=[k.b("qp_bf"), k.b("id_bf")], writes=[k.b("ps_tp")])
                p.op("dve", lambda e: e.tensor_copy(out=qT_sb[:, :], in_=ps_tp[0:HD, :]),
                     reads=[k.b("ps_tp")], writes=[k.b("qT_sb")])
                p.op("pe", lambda e, h=h: e.matmul(ps_no[:, :], qT_sb[:, :], kv_all[:, h * 65:(h + 1) * 65],
                                                   start=True, stop=True),
                     reads=[k.b("qT_sb"), k.b("kv_all")], writes=[k.b("ps_no")])
                p.op("dve", lambda e: e.reciprocal(out=rcol[:, :], in_=ps_no[:, 64:65]),
                     reads=[k.b("ps_no")], writes=[k.b("rcol")])
                p.op("dve", lambda e, h=h: e.tensor_scalar_mul(out=attn_sb[:, h * HD:(h + 1) * HD],
                                                               in0=ps_no[:, 0:HD], scalar1=rcol[:, :]),
                     reads=[k.b("ps_no"), k.b("rcol")], writes=[k.b("attn_sb")])
            k.ln_plain(attn_sb, "attn_sb", u_t, "u_t", tmp, "tmp", col, "col")
            p.op("dve", lambda e: e.tensor_mul(out=u_t[:, :], in0=u_t[:, :], in1=lnw_bc[:, :]),
                 reads=[k.b("u_t"), k.b("lnw_bc")], writes=[k.b("u_t")])
            p.op("dve", lambda e: e.tensor_add(out=u_t[:, :], in0=u_t[:, :], in1=lnb_bc[:, :]),
                 reads=[k.b("u_t"), k.b("lnb_bc")], writes=[k.b("u_t")])
            p.op("dve", lambda e: e.tensor_add(out=hp_t[:, :], in0=hp_t[:, :], in1=beta_bc[:, :]),
                 reads=[k.b("hp_t"), k.b("beta_bc")], writes=[k.b("hp_t")])
            p.op("dve", lambda e: e.tensor_mul(out=g_bf[:, :], in0=u_t[:, :], in1=hp_t[:, :]),
                 reads=[k.b("u_t"), k.b("hp_t")], writes=[k.b("g_bf")])
            k.tp_blocks(g_bf, "g_bf", ps_tp, "ps_tp", gT, "gT", KB)
            p.op("sync", lambda e, r0=r0: e.dma_start(out=x_t[:, :], in_=x6_d[r0:r0 + 128, :]),
                 writes=[k.b("x_t")], dma=True)
            for jt in range(2):
                pg, pgn = (ps_a, "ps_a") if jt == 0 else (ps_b, "ps_b")
                ex = (lambda e, jt=jt, pg=pg: e.matmul(pg[:, 0:512], k.ones1[:, :],
                                                       bo_row[:, jt * 512:(jt + 1) * 512], start=False, stop=True),
                      [k.b("ones1"), k.b("bo_row")])
                k.gemm(gT, "gT", wo_bf, "wo_bf", pg, pgn, 512, D, w_off=jt * 512, extra=ex)
                p.op("act", lambda e, pg=pg: e.activation(out=tmp[:, 0:512], in_=pg[:, 0:512], func=AF.Relu),
                     reads=[k.b(pgn)], writes=[k.b("tmp")])
                p.op("dve", lambda e, jt=jt: e.tensor_add(out=x_t[:, jt * 512:(jt + 1) * 512],
                                                          in0=x_t[:, jt * 512:(jt + 1) * 512], in1=tmp[:, 0:512]),
                     reads=[k.b("x_t"), k.b("tmp")], writes=[k.b("x_t")])
            p.op("pool", lambda e, r0=r0: e.dma_start(out=x7_d[r0:r0 + 128, :], in_=x_t[:, :]),
                 reads=[k.b("x_t")], writes=[k.b("x7")], dma=True)
            k.ln_plain(x_t, "x_t", u_t, "u_t", tmp, "tmp", col, "col")
            p.op("pool", lambda e, r0=r0: e.dma_start(out=u7_d[r0:r0 + 128, :], in_=u_t[:, :]),
                 reads=[k.b("u_t")], writes=[k.b("u7")], dma=True)
        p.op("dve", lambda e: e.engine_nop(), reads=[k.b("x7"), k.b("u7")])
        p.build(k.stack)
    return k


_PROGS = {}


def _prog(name, fn, *args):
    if name not in _PROGS:
        _PROGS[name] = fn(*args)
    return _PROGS[name]


def _run(k, in_maps):
    res = run_bass_kernel_spmd(k.nc, in_maps, core_ids=list(range(NCORE)))
    return res.results


def _rows(a):
    return [np.ascontiguousarray(a[c * TOK:(c + 1) * TOK]) for c in range(NCORE)]


def _cat(res, key):
    return np.concatenate([res[c][key] for c in range(NCORE)], axis=0)


def _to_pairs_T(a):
    a4 = a.reshape(B, T, H, HD)
    return [np.ascontiguousarray(np.stack([a4[b, :, 2 * c + hi, :].T
                                           for hi in range(2) for b in range(B)]))
            for c in range(NCORE)]


def _to_pairs_V(a):
    a4 = a.reshape(B, T, H, HD)
    return [np.ascontiguousarray(np.stack([a4[b, :, 2 * c + hi, :]
                                           for hi in range(2) for b in range(B)]))
            for c in range(NCORE)]


def _from_pairs(res, key):
    full = np.empty((B, T, H, HD), np.float32)
    for c in range(NCORE):
        r = res[c][key]
        for hi in range(2):
            for b in range(B):
                full[b, :, 2 * c + hi, :] = r[hi * 4 + b]
    return np.ascontiguousarray(full.reshape(TOKTOT, D))


def kernel(L0_attn_x, L0_attn_Wqkv, L0_attn_bqkv, L0_attn_Wproj, L0_attn_bproj,
           L0_attn_w1, L0_attn_b1, L0_attn_w2, L0_attn_b2, L0_attn_c_param,
           L0_attn_L_multiplier, L0_ffn_ln1_scale, L0_ffn_ln1_shift, L0_ffn_wq,
           L0_ffn_wk, L0_ffn_wv, L0_ffn_wo, L0_ffn_bo, L0_ffn_ln2_scale,
           L0_ffn_ln2_shift, L0_ffn_gate_w, L0_ffn_w1, L0_ffn_w2, L0_ffn_w3,
           L1_attn_Wh, L1_attn_bh, L1_attn_Wq, L1_attn_bq, L1_attn_Wk, L1_attn_bk,
           L1_attn_Wv, L1_attn_bv, L1_attn_beta, L1_attn_ln_w, L1_attn_ln_b,
           L1_attn_Wo, L1_attn_bo, L1_ffn_wg, L1_ffn_w1, L1_ffn_b1, L1_ffn_w2,
           L1_ffn_b2):
    f32 = np.float32
    asf = lambda a: np.ascontiguousarray(np.asarray(a), dtype=f32)
    x0 = asf(L0_attn_x).reshape(TOKTOT, D)
    ident = np.eye(128, dtype=f32)
    LT = np.tril(np.ones((128, 128), f32))
    NEGU_t = ((1 - LT) * NEG).astype(f32)
    UTs = np.triu(np.ones((128, 128), f32), 1)
    iota_t = np.ascontiguousarray(np.broadcast_to(np.arange(T, dtype=f32), (128, T)))

    kb = _prog("bias", build_bias)
    w2bd = np.zeros((128, 64), f32)
    for qs in range(4):
        w2bd[qs * 32:(qs + 1) * 32, qs * 16:(qs + 1) * 16] = asf(L0_attn_w2)
    w1t = np.ascontiguousarray(np.tile(asf(L0_attn_w1)[0], 4)[:, None])
    b1t = np.ascontiguousarray(np.tile(asf(L0_attn_b1), 4)[:, None])
    b2t = np.ascontiguousarray(np.tile(asf(L0_attn_b2), 4)[:, None])
    cpar = asf(L0_attn_c_param).reshape(1, 1)
    lmul = asf(L0_attn_L_multiplier).reshape(1, 1)
    pidx = np.arange(128)
    gidx = np.arange(128)
    maps = []
    for c in range(NCORE):
        qc = (gidx[None, :] * 32 + c * 4 + pidx[:, None] // 32).astype(f32)
        qc64 = (gidx[None, :] * 32 + c * 4 + pidx[:64, None] // 16).astype(f32)
        maps.append({"iota": iota_t, "qcols": np.ascontiguousarray(qc),
                     "qcols64": np.ascontiguousarray(qc64), "w1t": w1t, "b1t": b1t,
                     "w2bd": w2bd, "b2t": b2t, "cpar": cpar, "lmul": lmul, "ident": ident})
    resb = _run(kb, maps)
    bias8 = np.stack([resb[c]["bias8"] for c in range(NCORE)])
    bias_q = bias8.transpose(1, 0, 2, 3, 4).reshape(T, H, T)

    kp = _prog("pre0", build_pre, False)
    xr = _rows(x0)
    com = {"ln1s": asf(L0_ffn_ln1_scale), "ln1b": asf(L0_ffn_ln1_shift),
           "wq": asf(L0_ffn_wq), "wk": asf(L0_ffn_wk), "wv": asf(L0_ffn_wv), "ident": ident}
    res = _run(kp, [dict(com, xin=xr[c]) for c in range(NCORE)])
    u0 = _cat(res, "u")
    q0, k0, v0 = _cat(res, "q"), _cat(res, "k"), _cat(res, "v")

    ka = _prog("attn", build_attn, False)
    qTs, kTs, vs = _to_pairs_T(q0), _to_pairs_T(k0), _to_pairs_V(v0)
    res = _run(ka, [{"qT": qTs[c], "kT": kTs[c], "v": vs[c], "ltmask": LT, "negu": NEGU_t,
                     "ident": ident} for c in range(NCORE)])
    ctx0 = _from_pairs(res, "ctxo")

    kg = _prog("big_qkv", build_big, "qkv")
    cr, ur = _rows(ctx0), _rows(u0)
    com = {"wo": asf(L0_ffn_wo), "bo": asf(L0_ffn_bo), "ln2s": asf(L0_ffn_ln2_scale),
           "ln2b": asf(L0_ffn_ln2_shift), "gw": asf(L0_ffn_gate_w), "w1": asf(L0_ffn_w1),
           "w2": asf(L0_ffn_w2), "w3": asf(L0_ffn_w3), "ident": ident,
           "wqkv": asf(L0_attn_Wqkv), "bqkv": asf(L0_attn_bqkv)}
    res = _run(kg, [dict(com, ctx=cr[c], u=ur[c], xres=xr[c]) for c in range(NCORE)])
    x1 = _cat(res, "xnew")
    qkv = _cat(res, "qkv")

    kab = _prog("attn_bias", build_attn, True)
    q2, k2, v2 = qkv[:, 0:D], qkv[:, D:2 * D], qkv[:, 2 * D:3 * D]
    qTs, kTs, vs = _to_pairs_T(q2), _to_pairs_T(k2), _to_pairs_V(v2)
    res = _run(kab, [{"qT": qTs[c], "kT": kTs[c], "v": vs[c], "ltmask": LT, "negu": NEGU_t,
                      "ident": ident,
                      "bias": np.ascontiguousarray(bias_q[:, 2 * c:2 * c + 2, :].transpose(1, 0, 2))}
                     for c in range(NCORE)])
    ctx2 = _from_pairs(res, "ctxo")

    kp1 = _prog("pre1", build_pre, True)
    cr, x1r = _rows(ctx2), _rows(x1)
    com = {"wproj": asf(L0_attn_Wproj), "bproj": asf(L0_attn_bproj),
           "ln1s": asf(L0_ffn_ln1_scale), "ln1b": asf(L0_ffn_ln1_shift),
           "wq": asf(L0_ffn_wq), "wk": asf(L0_ffn_wk), "wv": asf(L0_ffn_wv), "ident": ident}
    res = _run(kp1, [dict(com, ctx=cr[c], xbase=x1r[c]) for c in range(NCORE)])
    x3 = _cat(res, "xout")
    u3 = _cat(res, "u")
    q3, k3, v3 = _cat(res, "q"), _cat(res, "k"), _cat(res, "v")

    qTs, kTs, vs = _to_pairs_T(q3), _to_pairs_T(k3), _to_pairs_V(v3)
    res = _run(ka, [{"qT": qTs[c], "kT": kTs[c], "v": vs[c], "ltmask": LT, "negu": NEGU_t,
                     "ident": ident} for c in range(NCORE)])
    ctx3 = _from_pairs(res, "ctxo")

    kg1 = _prog("big_ln", build_big, "ln")
    cr, ur, x3r = _rows(ctx3), _rows(u3), _rows(x3)
    com = {"wo": asf(L0_ffn_wo), "bo": asf(L0_ffn_bo), "ln2s": asf(L0_ffn_ln2_scale),
           "ln2b": asf(L0_ffn_ln2_shift), "gw": asf(L0_ffn_gate_w), "w1": asf(L0_ffn_w1),
           "w2": asf(L0_ffn_w2), "w3": asf(L0_ffn_w3), "ident": ident}
    res = _run(kg1, [dict(com, ctx=cr[c], u=ur[c], xres=x3r[c]) for c in range(NCORE)])
    x5 = _cat(res, "xnew")
    u5 = _cat(res, "unorm")

    kgt = _prog("gate", build_gate)
    u5T = np.ascontiguousarray(u5.T)
    com = {"uT": u5T, "wg": asf(L1_ffn_wg), "utstrict": UTs, "ident": ident}
    res = _run(kgt, [com for _ in range(NCORE)])
    wtab1 = res[0]["wtab"]

    perm = ((np.arange(D) % 64) * 16 + np.arange(D) // 64).astype(np.int64)
    km = _prog("moeA", build_moe1, False)
    wtT1 = np.ascontiguousarray(wtab1.T)
    b1tile = np.ascontiguousarray(asf(L1_ffn_b1).reshape(8, 32, 128).transpose(0, 2, 1))
    u5r, x5r = _rows(u5), _rows(x5)
    com = {"w1": asf(L1_ffn_w1), "b1tile": b1tile, "w2": asf(L1_ffn_w2), "b2": asf(L1_ffn_b2),
           "ident": ident,
           "wh": np.ascontiguousarray(asf(L1_attn_Wh)[:, perm]), "bh": np.ascontiguousarray(asf(L1_attn_bh)[perm]),
           "wqp": np.ascontiguousarray(asf(L1_attn_Wq)[:, perm]), "bqp": np.ascontiguousarray(asf(L1_attn_bq)[perm]),
           "wkp": np.ascontiguousarray(asf(L1_attn_Wk)[:, perm]), "bkp": np.ascontiguousarray(asf(L1_attn_bk)[perm]),
           "wvp": np.ascontiguousarray(asf(L1_attn_Wv)[:, perm]), "bvp": np.ascontiguousarray(asf(L1_attn_bv)[perm])}
    res = _run(km, [dict(com, u=u5r[c], wtT=np.ascontiguousarray(wtT1[:, c * TOK:(c + 1) * TOK]),
                         xres=x5r[c]) for c in range(NCORE)])
    x6 = _cat(res, "x6")
    qp, kpp, vpp, hp = _cat(res, "qp"), _cat(res, "kp"), _cat(res, "vp"), _cat(res, "hp")

    kl = _prog("lin", build_linattn)
    com = {"lnwp": np.ascontiguousarray(asf(L1_attn_ln_w)[perm]),
           "lnbp": np.ascontiguousarray(asf(L1_attn_ln_b)[perm]),
           "betap": np.ascontiguousarray(asf(L1_attn_beta)[perm]),
           "wop": np.ascontiguousarray(asf(L1_attn_Wo)[perm, :]), "bop": asf(L1_attn_bo),
           "ident": ident}
    maps = []
    for c in range(NCORE):
        b, half = c // 2, c % 2
        r0 = b * T + half * TOK
        maps.append(dict(com, qp=np.ascontiguousarray(qp[r0:r0 + TOK]),
                         kpf=np.ascontiguousarray(kpp[b * T:(b + 1) * T]),
                         vpf=np.ascontiguousarray(vpp[b * T:(b + 1) * T]),
                         hp=np.ascontiguousarray(hp[r0:r0 + TOK]),
                         x6h=np.ascontiguousarray(x6[r0:r0 + TOK])))
    res = _run(kl, maps)
    x7 = np.empty((TOKTOT, D), f32)
    u7 = np.empty((TOKTOT, D), f32)
    for c in range(NCORE):
        b, half = c // 2, c % 2
        r0 = b * T + half * TOK
        x7[r0:r0 + TOK] = res[c]["x7"]
        u7[r0:r0 + TOK] = res[c]["u7"]

    u7T = np.ascontiguousarray(u7.T)
    com = {"uT": u7T, "wg": asf(L1_ffn_wg), "utstrict": UTs, "ident": ident}
    res = _run(kgt, [com for _ in range(NCORE)])
    wtab2 = res[0]["wtab"]

    kmf = _prog("moeB", build_moe1, True)
    wtT2 = np.ascontiguousarray(wtab2.T)
    u7r, x7r = _rows(u7), _rows(x7)
    com = {"w1": asf(L1_ffn_w1), "b1tile": b1tile, "w2": asf(L1_ffn_w2), "b2": asf(L1_ffn_b2),
           "ident": ident}
    res = _run(kmf, [dict(com, u=u7r[c], wtT=np.ascontiguousarray(wtT2[:, c * TOK:(c + 1) * TOK]),
                          xres=x7r[c]) for c in range(NCORE)])
    y = _cat(res, "y")
    return np.ascontiguousarray(y.reshape(B, T, D)).astype(np.float32)


# version 03 (final)
# speedup vs baseline: 1.0881x; 1.0881x over previous
import numpy as np
import concourse.bass as bass
import concourse.mybir as mybir
from concourse.bass_utils import run_bass_kernel_spmd
from contextlib import ExitStack

F32 = mybir.dt.float32
BF16 = mybir.dt.bfloat16
AF = mybir.ActivationFunctionType
ALU = mybir.AluOpType
AX = mybir.AxisListType

ENGINES = ("pe", "act", "dve", "pool", "sync")
BLOCK_ATTR = {"pe": "tensor", "act": "scalar", "dve": "vector", "pool": "gpsimd", "sync": "sync"}

NCORE = 8
TOKTOT = 16384
TOK = TOKTOT // NCORE
TT = TOK // 128
D = 1024
KB = D // 128
T = 4096
B = 4
H = 16
HD = 64
NEG = -30000.0
NEG8 = -240000.0


class Buf:
    __slots__ = ("writer", "readers", "name")

    def __init__(self, name=""):
        self.writer = None
        self.readers = []
        self.name = name


class Prog:
    def __init__(self, nc):
        self.nc = nc
        self.plan = {e: [] for e in ENGINES}
        self.counts = {e: 0 for e in ENGINES}
        self.dma_counts = {e: 0 for e in ENGINES}
        self.waited = {e: {} for e in ENGINES}
        self.pending = {e: {"r": [], "w": []} for e in ENGINES}
        self.sems = {}

    def op(self, eng, fn, reads=(), writes=(), dma=False, quiet=False):
        toks = []
        for b in reads:
            if b.writer:
                toks.append(b.writer)
        for b in writes:
            if b.writer:
                toks.append(b.writer)
            toks.extend(b.readers)
        need = {}
        for (s, v) in toks:
            if eng == "pe" and s == "pe":
                continue
            need[s] = max(need.get(s, 0), v)
        waits = []
        for s, v in sorted(need.items()):
            if self.waited[eng].get(s, 0) >= v:
                continue
            self.waited[eng][s] = v
            waits.append((s, v))
        if quiet:
            self.pending[eng]["r"].extend(reads)
            self.pending[eng]["w"].extend(writes)
            self.plan[eng].append((waits, fn, None))
            return None
        if dma:
            key = "d_" + (writes[0].name if writes else eng)
            self.dma_counts[key] = self.dma_counts.get(key, 0) + 1
            tok = (key, self.dma_counts[key] * 16)
        else:
            self.counts[eng] += 1
            tok = (eng, self.counts[eng])
        wl = list(writes) + self.pending[eng]["w"]
        rl = list(reads) + self.pending[eng]["r"]
        self.pending[eng]["r"] = []
        self.pending[eng]["w"] = []
        for b in wl:
            b.writer = tok
            b.readers = []
        for b in rl:
            b.readers.append(tok)
        self.plan[eng].append((waits, fn, tok))
        return tok

    def build(self, stack):
        nc = self.nc
        used = set()
        for e in ENGINES:
            for (waits, fn, tok) in self.plan[e]:
                if tok is not None:
                    used.add(tok[0])
                for (s, v) in waits:
                    used.add(s)
        for s in sorted(used):
            self.sems[s] = stack.enter_context(nc.semaphore(name="s_" + s))
        nums = sorted(h.num for h in self.sems.values())
        if nums:
            nc.gpsimd.dma_reset(range(nums[0], nums[-1] + 1))
            for h in self.sems.values():
                nc.gpsimd.sem_clear(h)
        nc.all_engine_barrier()
        block = stack.enter_context(nc.Block())
        for ename in ENGINES:
            items = self.plan[ename]
            if not items:
                continue

            def make(f_items):
                def run(eng):
                    for waits, fn, tok in f_items:
                        for (s, v) in waits:
                            eng.wait_ge(self.sems[s], v)
                        inst = fn(eng)
                        if tok is not None:
                            inst.then_inc(self.sems[tok[0]], 16 if tok[0].startswith("d_") else 1)
                return run

            getattr(block, BLOCK_ATTR[ename])(make(items))


class Ker:

    def __init__(self):
        self.nc = bass.Bass("TRN2", target_bir_lowering=False)
        self.stack = ExitStack()
        self.p = Prog(self.nc)
        self.bufs = {}

    def dram_in(self, name, shape, dt=F32):
        ap = self.nc.dram_tensor(name, list(shape), dt, kind="ExternalInput").ap()
        self.bufs[name] = Buf(name)
        return ap

    def dram_out(self, name, shape, dt=F32):
        ap = self.nc.dram_tensor(name, list(shape), dt, kind="ExternalOutput").ap()
        self.bufs[name] = Buf(name)
        return ap

    def dram_tmp(self, name, shape, dt=F32):
        ap = self.nc.dram_tensor(name, list(shape), dt, kind="Internal").ap()
        self.bufs[name] = Buf(name)
        return ap

    def sb(self, name, shape, dt=F32):
        t = self.stack.enter_context(self.nc.sbuf_tensor(name, list(shape), dt))
        self.bufs[name] = Buf(name)
        return t

    def ps(self, name, shape, dt=F32):
        t = self.stack.enter_context(self.nc.psum_tensor(name, list(shape), dt))
        self.bufs[name] = Buf(name)
        return t

    def b(self, name):
        return self.bufs[name]

    def prelude_ident(self):
        k, p = self, self.p
        self.identd = k.dram_in("ident", (128, 128), F32)
        self.id_f32 = k.sb("id_f32", (128, 128), F32)
        self.id_bf = k.sb("id_bf", (128, 128), BF16)
        self.ones1 = k.sb("ones1", (1, 128), F32)
        p.op("sync", lambda e: e.dma_start(out=self.id_f32[:, :], in_=self.identd[:, :]),
             writes=[k.b("id_f32")], dma=True)
        p.op("dve", lambda e: e.tensor_copy(out=self.id_bf[:, :], in_=self.id_f32[:, :]),
             reads=[k.b("id_f32")], writes=[k.b("id_bf")])
        p.op("pool", lambda e: e.memset(self.ones1[:, :], 1.0), writes=[k.b("ones1")])
        self.epscol = k.sb("epscol", (128, 1), F32)
        self.onecol = k.sb("onecol", (128, 1), F32)
        p.op("pool", lambda e: e.memset(self.epscol[:, :], 1e-5), writes=[k.b("epscol")])
        p.op("pool", lambda e: e.memset(self.onecol[:, :], 1.0), writes=[k.b("onecol")])

    def rank1_rows(self, name, src_dram, n, ps_buf, ps_name, out_dt=F32):
        k, p = self, self.p
        row = k.sb(name + "_row", (1, n), F32)
        out = k.sb(name + "_bc", (128, n), out_dt)
        p.op("sync", lambda e: e.dma_start(out=row[:, :], in_=src_dram[None, :]),
             writes=[k.b(name + "_row")], dma=True)
        nchunk = (n + 511) // 512
        for c in range(nchunk):
            j0, j1 = c * 512, min(n, (c + 1) * 512)
            p.op("pe", lambda e, j0=j0, j1=j1: e.matmul(ps_buf[:, j0:j1], self.ones1[:, :],
                                                        row[:, j0:j1], start=True, stop=True),
                 reads=[k.b("ones1"), k.b(name + "_row")], writes=[k.b(ps_name)],
                 quiet=(c != nchunk - 1))
        p.op("act", lambda e: e.copy(out=out[:, :], in_=ps_buf[:, 0:n]),
             reads=[k.b(ps_name)], writes=[k.b(name + "_bc")])
        return out

    def load_w_bf(self, wname, w_dram, kb, ncols, stag, stag_name, dst, dst_name, dst_off=0,
                  dst_stride=None, col0=0, stag2=None, stag2_name=None):
        k, p = self, self.p
        if dst_stride is None:
            dst_stride = ncols
        for i in range(kb):
            use_b = stag2 is not None and (i % 2 == 1)
            sb_, sn_ = (stag2, stag2_name) if use_b else (stag, stag_name)
            eng = "pool" if use_b else "sync"
            p.op(eng, lambda e, i=i, sb_=sb_: e.dma_start(out=sb_[:, 0:ncols],
                                                          in_=w_dram[i * 128:(i + 1) * 128, col0:col0 + ncols]),
                 writes=[k.b(sn_)], dma=True)
            p.op("dve", lambda e, i=i, sb_=sb_: e.tensor_copy(out=dst[:, dst_off + i * dst_stride: dst_off + i * dst_stride + ncols],
                                                              in_=sb_[:, 0:ncols]),
                 reads=[k.b(sn_)], writes=[k.b(dst_name)])

    def ln_plain(self, x_sb, x_name, out_sb, out_name, tmp, tmp_name, col, col_name, n=D, eps=1e-5):
        k, p = self, self.p
        rx, rc, rt, ro = k.b(x_name), k.b(col_name), k.b(tmp_name), k.b(out_name)
        p.op("dve", lambda e: e.reduce_sum(out=col[:, 0:1], in_=x_sb[:, 0:n], axis=AX.X),
             reads=[rx], writes=[rc])
        p.op("dve", lambda e: e.tensor_scalar_mul(out=col[:, 0:1], in0=col[:, 0:1], scalar1=1.0 / n),
             reads=[rc], writes=[rc])
        p.op("dve", lambda e: e.tensor_scalar_sub(out=out_sb[:, 0:n], in0=x_sb[:, 0:n], scalar1=col[:, 0:1]),
             reads=[rx, rc], writes=[ro])
        p.op("act", lambda e: e.activation(out=tmp[:, 0:n], in_=out_sb[:, 0:n], func=AF.Square),
             reads=[ro], writes=[rt])
        p.op("dve", lambda e: e.reduce_sum(out=col[:, 1:2], in_=tmp[:, 0:n], axis=AX.X),
             reads=[rt], writes=[rc])
        p.op("dve", lambda e: e.tensor_scalar_mul(out=col[:, 1:2], in0=col[:, 1:2], scalar1=1.0 / n),
             reads=[rc], writes=[rc])
        p.op("act", lambda e: e.activation(out=col[:, 2:3], in_=col[:, 1:2], func=AF.Ln,
                                           bias=self.epscol[:, 0:1]),
             reads=[rc, self.b("epscol")], writes=[rc])
        p.op("act", lambda e: e.activation(out=col[:, 3:4], in_=col[:, 2:3], func=AF.Exp, scale=-0.5),
             reads=[rc], writes=[rc])
        p.op("dve", lambda e: e.tensor_scalar_mul(out=out_sb[:, 0:n], in0=out_sb[:, 0:n], scalar1=col[:, 3:4]),
             reads=[ro, rc], writes=[ro])

    def tp_blocks(self, src_sb, src_name, tp_ps, tp_name, dst_sb, dst_name, nblk, dst_off=0,
                  ident=None, ident_name=None, src_off=0):
        k, p = self, self.p
        if ident is None:
            ident, ident_name = self.id_bf, "id_bf"
        for i in range(nblk):
            p.op("pe", lambda e, i=i: e.transpose(out=tp_ps[:, :],
                                                  in_=src_sb[:, src_off + i * 128: src_off + (i + 1) * 128],
                                                  identity=ident[:, :]),
                 reads=[k.b(src_name), k.b(ident_name)], writes=[k.b(tp_name)])
            p.op("dve", lambda e, i=i: e.tensor_copy(out=dst_sb[:, dst_off + i * 128: dst_off + (i + 1) * 128],
                                                     in_=tp_ps[:, :]),
                 reads=[k.b(tp_name)], writes=[k.b(dst_name)])

    def gemm(self, statT, stat_name, w_bf, w_name, out_ps, out_ps_name, ncols, wcols,
             stat_off=0, w_off=0, extra=None):
        k, p = self, self.p
        for kb in range(KB):
            p.op("pe", lambda e, kb=kb: e.matmul(out_ps[:, 0:ncols],
                                                 statT[:, stat_off + kb * 128: stat_off + (kb + 1) * 128],
                                                 w_bf[:, w_off + kb * wcols: w_off + kb * wcols + ncols],
                                                 start=(kb == 0), stop=(kb == KB - 1 and extra is None)),
                 reads=[k.b(stat_name), k.b(w_name)], writes=[k.b(out_ps_name)],
                 quiet=not (kb == KB - 1 and extra is None))
        if extra is not None:
            fn, rbufs = extra
            p.op("pe", fn, reads=rbufs, writes=[k.b(out_ps_name)])



def build_pre(with_proj):
    k = Ker()
    nc, p = k.nc, k.p
    with k.stack:
        if with_proj:
            ctx_d = k.dram_in("ctx", (TOK, D))
            xbase_d = k.dram_in("xbase", (TOK, D))
            wp_d = k.dram_in("wproj", (D, D))
            bp_d = k.dram_in("bproj", (D,))
            xout_d = k.dram_out("xout", (TOK, D))
        else:
            xin_d = k.dram_in("xin", (TOK, D))
        ln1s_d = k.dram_in("ln1s", (D,))
        ln1b_d = k.dram_in("ln1b", (D,))
        wq_d = k.dram_in("wq", (D, D))
        wk_d = k.dram_in("wk", (D, D))
        wv_d = k.dram_in("wv", (D, D))
        u_d = k.dram_out("u", (TOK, D))
        q_d = k.dram_out("q", (TOK, D))
        kk_d = k.dram_out("k", (TOK, D))
        v_d = k.dram_out("v", (TOK, D))
        outs = [q_d, kk_d, v_d]
        outnames = ["q", "k", "v"]

        k.prelude_ident()
        ps_r1 = k.ps("ps_r1", (128, 1024), F32)
        ps_tp = k.ps("ps_tp", (128, 128), BF16)
        ps_g0 = k.ps("ps_g0", (128, 512), F32)
        ps_g1 = k.ps("ps_g1", (128, 512), F32)

        scale_b = k.rank1_rows("lns", ln1s_d, D, ps_r1, "ps_r1")
        shift_b = k.rank1_rows("lnb", ln1b_d, D, ps_r1, "ps_r1")

        stag = k.sb("stag", (128, D), F32)
        w_bf = {}
        for nm, wd in (("wq", wq_d), ("wk", wk_d), ("wv", wv_d)):
            w_bf[nm] = k.sb(nm + "_bf", (128, KB * D), BF16)
            k.load_w_bf(nm, wd, KB, D, stag, "stag", w_bf[nm], nm + "_bf")
        if with_proj:
            wp_bf = k.sb("wp_bf", (128, KB * D), BF16)
            k.load_w_bf("wp", wp_d, KB, D, stag, "stag", wp_bf, "wp_bf")
            bp_row = k.sb("bp_row", (1, D), F32)
            p.op("sync", lambda e: e.dma_start(out=bp_row[:, :], in_=bp_d[None, :]),
                 writes=[k.b("bp_row")], dma=True)

        x_t = k.sb("x_t", (128, D), F32)
        u_t = k.sb("u_t", (128, D), F32)
        tmp = k.sb("tmp", (128, D), F32)
        col = k.sb("col", (128, 4), F32)
        h_bf = k.sb("h_bf", (128, D), BF16)
        hT = k.sb("hT", (128, KB * 128), BF16)
        out_sb = k.sb("out_sb", (128, 512), F32)
        if with_proj:
            ctx_t = k.sb("ctx_t", (128, D), F32)
            ctx_bf = k.sb("ctx_bf", (128, D), BF16)
            cT = k.sb("cT", (128, KB * 128), BF16)

        for tt in range(TT):
            r0 = tt * 128
            if with_proj:
                p.op("sync", lambda e, r0=r0: e.dma_start(out=ctx_t[:, :], in_=ctx_d[r0:r0 + 128, :]),
                     writes=[k.b("ctx_t")], dma=True)
                p.op("dve", lambda e: e.tensor_copy(out=ctx_bf[:, :], in_=ctx_t[:, :]),
                     reads=[k.b("ctx_t")], writes=[k.b("ctx_bf")])
                k.tp_blocks(ctx_bf, "ctx_bf", ps_tp, "ps_tp", cT, "cT", KB)
                p.op("sync", lambda e, r0=r0: e.dma_start(out=x_t[:, :], in_=xbase_d[r0:r0 + 128, :]),
                     writes=[k.b("x_t")], dma=True)
                for jt in range(2):
                    pg, pgn = (ps_g0, "ps_g0") if jt == 0 else (ps_g1, "ps_g1")
                    ex = (lambda e, jt=jt, pg=pg: e.matmul(pg[:, 0:512], k.ones1[:, :],
                                                           bp_row[:, jt * 512:(jt + 1) * 512],
                                                           start=False, stop=True),
                          [k.b("ones1"), k.b("bp_row")])
                    k.gemm(cT, "cT", wp_bf, "wp_bf", pg, pgn, 512, D, w_off=jt * 512, extra=ex)
                    p.op("dve", lambda e, jt=jt, pg=pg: e.tensor_add(out=x_t[:, jt * 512:(jt + 1) * 512],
                                                                     in0=x_t[:, jt * 512:(jt + 1) * 512],
                                                                     in1=pg[:, 0:512]),
                         reads=[k.b("x_t"), k.b(pgn)], writes=[k.b("x_t")])
                p.op("pool", lambda e, r0=r0: e.dma_start(out=xout_d[r0:r0 + 128, :], in_=x_t[:, :]),
                     reads=[k.b("x_t")], writes=[k.b("xout")], dma=True)
            else:
                p.op("sync", lambda e, r0=r0: e.dma_start(out=x_t[:, :], in_=xin_d[r0:r0 + 128, :]),
                     writes=[k.b("x_t")], dma=True)
            k.ln_plain(x_t, "x_t", u_t, "u_t", tmp, "tmp", col, "col")
            p.op("pool", lambda e, r0=r0: e.dma_start(out=u_d[r0:r0 + 128, :], in_=u_t[:, :]),
                 reads=[k.b("u_t")], writes=[k.b("u")], dma=True)
            p.op("dve", lambda e: e.tensor_mul(out=tmp[:, :], in0=u_t[:, :], in1=scale_b[:, :]),
                 reads=[k.b("u_t"), k.b("lns_bc")], writes=[k.b("tmp")])
            p.op("dve", lambda e: e.tensor_add(out=h_bf[:, :], in0=tmp[:, :], in1=shift_b[:, :]),
                 reads=[k.b("tmp"), k.b("lnb_bc")], writes=[k.b("h_bf")])
            k.tp_blocks(h_bf, "h_bf", ps_tp, "ps_tp", hT, "hT", KB)
            for wi, nm in enumerate(outnames):
                for jt in range(2):
                    pg, pgn = (ps_g0, "ps_g0") if jt == 0 else (ps_g1, "ps_g1")
                    k.gemm(hT, "hT", w_bf["w" + nm], "w" + nm + "_bf", pg, pgn, 512, D, w_off=jt * 512)
                    p.op("act", lambda e, pg=pg: e.copy(out=out_sb[:, :], in_=pg[:, 0:512]),
                         reads=[k.b(pgn)], writes=[k.b("out_sb")])
                    p.op("pool", lambda e, r0=r0, wi=wi, jt=jt: e.dma_start(
                        out=outs[wi][r0:r0 + 128, jt * 512:(jt + 1) * 512], in_=out_sb[:, :]),
                        reads=[k.b("out_sb")], writes=[k.b(outnames[wi])], dma=True)
        fin = [k.b(n) for n in outnames] + [k.b("u")] + ([k.b("xout")] if with_proj else [])
        p.op("dve", lambda e: e.engine_nop(), reads=fin)
        p.build(k.stack)
    return k



def build_attn(with_bias):
    k = Ker()
    nc, p = k.nc, k.p
    NP = 8
    with k.stack:
        qT_d = k.dram_in("qT", (NP, HD, T))
        kT_d = k.dram_in("kT", (NP, HD, T))
        v_d = k.dram_in("v", (NP, T, HD))
        lt_d = k.dram_in("ltmask", (128, 128))
        negu_d = k.dram_in("negu", (128, 128))
        if with_bias:
            bias_d = k.dram_in("bias", (2, T, T), BF16)
        ctx_d = k.dram_out("ctxo", (NP, T, HD))

        k.prelude_ident()
        ps_s = [k.ps("ps_s0", (128, 1024), F32), k.ps("ps_s1", (128, 1024), F32)]
        ps_tp2 = [k.ps("ps_tp0", (128, 128), BF16), k.ps("ps_tp1", (128, 128), BF16)]
        ps_c = [k.ps("ps_c0", (128, 64), F32), k.ps("ps_c1", (128, 64), F32)]

        lt_sb = k.sb("lt_sb", (128, 128), F32)
        negu_sb = k.sb("negu_sb", (128, 128), F32)
        p.op("sync", lambda e: e.dma_start(out=lt_sb[:, :], in_=lt_d[:, :]), writes=[k.b("lt_sb")], dma=True)
        p.op("sync", lambda e: e.dma_start(out=negu_sb[:, :], in_=negu_d[:, :]), writes=[k.b("negu_sb")], dma=True)

        stag1 = k.sb("stag1", (HD, T), F32)
        stag2 = k.sb("stag2", (128, TT * 2 * HD), F32)
        qT_bf, kT_bf, v_bf = [], [], []
        for b in range(B):
            qT_bf.append(k.sb("qT%d" % b, (HD, T), BF16))
            kT_bf.append(k.sb("kT%d" % b, (HD, T), BF16))
            v_bf.append(k.sb("v%d" % b, (128, (T // 128) * HD), BF16))
        scores = [k.sb("scores0", (128, T), F32), k.sb("scores1", (128, T), F32)]
        p_sb = [k.sb("p_sb0", (128, T), BF16), k.sb("p_sb1", (128, T), BF16)]
        pT_sb = [k.sb("pT_sb0", (128, 128), BF16), k.sb("pT_sb1", (128, 128), BF16)]
        mcol = [k.sb("mcol0", (128, 4), F32), k.sb("mcol1", (128, 4), F32)]
        ctx_sb = [k.sb("ctx_sb0", (128, HD), F32), k.sb("ctx_sb1", (128, HD), F32)]
        if with_bias:
            bias_bf = k.sb("bias_bf", (128, T), BF16)
            bias_f = k.sb("bias_f", (128, T), F32)

        for hi in range(2):
            for b in range(B):
                pair = hi * 4 + b
                for src_d, dst, dstn in ((qT_d, qT_bf[b], "qT%d" % b), (kT_d, kT_bf[b], "kT%d" % b)):
                    p.op("sync", lambda e, pair=pair, src_d=src_d: e.dma_start(out=stag1[:, :], in_=src_d[pair, :, :]),
                         writes=[k.b("stag1")], dma=True)
                    p.op("dve", lambda e, dst=dst: e.tensor_copy(out=dst[:, :], in_=stag1[:, :]),
                         reads=[k.b("stag1")], writes=[k.b(dstn)])
                for half in range(4):
                    nt0 = half * (T // 512)
                    p.op("sync", lambda e, pair=pair, half=half: e.dma_start(
                        out=stag2.rearrange("p (t d) -> p t d", d=HD)[:, half * 8:(half + 1) * 8, :],
                        in_=v_d[pair].rearrange("(t p) d -> p t d", p=128)[:, half * 8:(half + 1) * 8, :]),
                        writes=[k.b("stag2")], dma=True)
                p.op("dve", lambda e, b=b: e.tensor_copy(out=v_bf[b][:, :], in_=stag2[:, :]),
                     reads=[k.b("stag2")], writes=[k.b("v%d" % b)])
            for qt in range(32):
                cov = (qt // 4 + 1) * 512
                if with_bias:
                    p.op("sync", lambda e, hi=hi, qt=qt, cov=cov: e.dma_start(
                        out=bias_bf[:, 0:cov], in_=bias_d[hi, qt * 128:(qt + 1) * 128, 0:cov]),
                        writes=[k.b("bias_bf")], dma=True)
                    p.op("dve", lambda e, cov=cov: e.tensor_copy(out=bias_f[:, 0:cov], in_=bias_bf[:, 0:cov]),
                         reads=[k.b("bias_bf")], writes=[k.b("bias_f")])
                def stage_a(b):
                    pr = b % 2
                    qbn, kbn = "qT%d" % b, "kT%d" % b
                    psn, scn = "ps_s%d" % pr, "scores%d" % pr
                    sc, pss = scores[pr], ps_s[pr]
                    nmb = (cov + 1023) // 1024
                    for mb in range(nmb):
                        c0 = mb * 1024
                        c1 = min(cov, c0 + 1024)
                        nch = (c1 - c0 + 511) // 512
                        for ch in range(nch):
                            s0 = c0 + ch * 512
                            p.op("pe", lambda e, qt=qt, s0=s0, c0=c0, b=b, pss=pss: e.matmul(
                                pss[:, s0 - c0:s0 - c0 + 512],
                                qT_bf[b][:, qt * 128:(qt + 1) * 128],
                                kT_bf[b][:, s0:s0 + 512], start=True, stop=True),
                                reads=[k.b(qbn), k.b(kbn)], writes=[k.b(psn)],
                                quiet=(ch != nch - 1))
                        for ch in range(nch):
                            s0 = c0 + ch * 512
                            if with_bias:
                                p.op("dve", lambda e, s0=s0, c0=c0, sc=sc, pss=pss: e.tensor_add(
                                    out=sc[:, s0:s0 + 512], in0=pss[:, s0 - c0:s0 - c0 + 512],
                                    in1=bias_f[:, s0:s0 + 512]),
                                    reads=[k.b(psn), k.b("bias_f")], writes=[k.b(scn)])
                            else:
                                p.op("act", lambda e, s0=s0, c0=c0, sc=sc, pss=pss: e.copy(
                                    out=sc[:, s0:s0 + 512], in_=pss[:, s0 - c0:s0 - c0 + 512]),
                                    reads=[k.b(psn)], writes=[k.b(scn)])
                    if not with_bias:
                        dc = qt * 128
                        p.op("dve", lambda e, dc=dc, sc=sc: e.tensor_mul(out=sc[:, dc:dc + 128],
                                                                         in0=sc[:, dc:dc + 128], in1=lt_sb[:, :]),
                             reads=[k.b(scn), k.b("lt_sb")], writes=[k.b(scn)])
                        p.op("dve", lambda e, dc=dc, sc=sc: e.tensor_add(out=sc[:, dc:dc + 128],
                                                                         in0=sc[:, dc:dc + 128], in1=negu_sb[:, :]),
                             reads=[k.b(scn), k.b("negu_sb")], writes=[k.b(scn)])
                        if cov > dc + 128:
                            p.op("pool", lambda e, dc=dc, cov=cov, sc=sc: e.memset(sc[:, dc + 128:cov], NEG),
                                 writes=[k.b(scn)])

                def stage_b(b):
                    pr = b % 2
                    scn, pbn, mcn = "scores%d" % pr, "p_sb%d" % pr, "mcol%d" % pr
                    sc, pb, mc = scores[pr], p_sb[pr], mcol[pr]
                    p.op("dve", lambda e, cov=cov, sc=sc, mc=mc: e.reduce_max(out=mc[:, 0:1], in_=sc[:, 0:cov], axis=AX.X),
                         reads=[k.b(scn)], writes=[k.b(mcn)])
                    p.op("dve", lambda e, mc=mc: e.tensor_scalar_mul(out=mc[:, 1:2], in0=mc[:, 0:1], scalar1=-0.125),
                         reads=[k.b(mcn)], writes=[k.b(mcn)])
                    p.op("act", lambda e, cov=cov, sc=sc, pb=pb, mc=mc: e.activation(out=pb[:, 0:cov], in_=sc[:, 0:cov],
                                                                                     func=AF.Exp, bias=mc[:, 1:2], scale=0.125),
                         reads=[k.b(scn), k.b(mcn)], writes=[k.b(pbn)])
                    p.op("dve", lambda e, cov=cov, pb=pb, mc=mc: e.reduce_sum(out=mc[:, 2:3], in_=pb[:, 0:cov], axis=AX.X),
                         reads=[k.b(pbn)], writes=[k.b(mcn)])
                    p.op("dve", lambda e, mc=mc: e.reciprocal(out=mc[:, 3:4], in_=mc[:, 2:3]),
                         reads=[k.b(mcn)], writes=[k.b(mcn)])

                def stage_c(b):
                    pair = hi * 4 + b
                    pr = b % 2
                    vbn = "v%d" % b
                    pbn, tpn, ptn = "p_sb%d" % pr, "ps_tp%d" % pr, "pT_sb%d" % pr
                    mcn, ctn, pcn = "mcol%d" % pr, "ctx_sb%d" % pr, "ps_c%d" % pr
                    pb, mc, ct = p_sb[pr], mcol[pr], ctx_sb[pr]
                    pstp, psc, pts = ps_tp2[pr], ps_c[pr], pT_sb[pr]
                    nblk = qt + 1
                    for sb in range(nblk):
                        p.op("pe", lambda e, sb=sb, pb=pb, pstp=pstp: e.transpose(out=pstp[:, :], in_=pb[:, sb * 128:(sb + 1) * 128],
                                                                                  identity=k.id_bf[:, :]),
                             reads=[k.b(pbn), k.b("id_bf")], writes=[k.b(tpn)])
                        p.op("act", lambda e, pts=pts, pstp=pstp: e.copy(out=pts[:, :], in_=pstp[:, :]),
                             reads=[k.b(tpn)], writes=[k.b(ptn)])
                        p.op("pe", lambda e, sb=sb, b=b, pts=pts, psc=psc: e.matmul(psc[:, :], pts[:, :],
                                                                                    v_bf[b][:, sb * HD:(sb + 1) * HD],
                                                                                    start=(sb == 0), stop=(sb == nblk - 1)),
                             reads=[k.b(ptn), k.b(vbn)], writes=[k.b(pcn)],
                             quiet=(sb != nblk - 1))
                    p.op("dve", lambda e, ct=ct, psc=psc, mc=mc: e.tensor_scalar_mul(out=ct[:, :], in0=psc[:, :], scalar1=mc[:, 3:4]),
                         reads=[k.b(pcn), k.b(mcn)], writes=[k.b(ctn)])
                    p.op("pool", lambda e, pair=pair, qt=qt, ct=ct: e.dma_start(
                        out=ctx_d[pair, qt * 128:(qt + 1) * 128, :], in_=ct[:, :]),
                        reads=[k.b(ctn)], writes=[k.b("ctxo")], dma=True)

                stage_a(0)
                stage_a(1)
                stage_b(0)
                stage_c(0)
                stage_a(2)
                stage_b(1)
                stage_c(1)
                stage_a(3)
                stage_b(2)
                stage_c(2)
                stage_b(3)
                stage_c(3)
        p.op("dve", lambda e: e.engine_nop(), reads=[k.b("ctxo")])
        p.build(k.stack)
    return k



def build_big(tail):
    k = Ker()
    nc, p = k.nc, k.p
    HID = 2048
    JT = HID // 128
    NTC = 4
    with k.stack:
        ctx_d = k.dram_in("ctx", (TOK, D))
        u_d = k.dram_in("u", (TOK, D))
        xres_d = k.dram_in("xres", (TOK, D))
        wo_d = k.dram_in("wo", (D, D))
        bo_d = k.dram_in("bo", (D,))
        ln2s_d = k.dram_in("ln2s", (D,))
        ln2b_d = k.dram_in("ln2b", (D,))
        gw_d = k.dram_in("gw", (D, 8))
        w1_d = k.dram_in("w1", (8, D, HID))
        w2_d = k.dram_in("w2", (8, D, HID))
        w3_d = k.dram_in("w3", (8, HID, D))
        xnew_d = k.dram_out("xnew", (TOK, D))
        if tail == "qkv":
            wqkv_d = k.dram_in("wqkv", (D, 3 * D))
            bqkv_d = k.dram_in("bqkv", (3 * D,))
            qkv_d = k.dram_out("qkv", (TOK, 3 * D))
        else:
            un_d = k.dram_out("unorm", (TOK, D))
        x1_t_d = k.dram_tmp("x1scr", (TOK, D))
        hT_dram = k.dram_tmp("hTscr", (KB, 128, TOK), BF16)
        wt_dram = k.dram_tmp("wtscr", (8, TOK))

        k.prelude_ident()
        ps_r1 = k.ps("ps_r1", (128, 1024), F32)
        ps_a = k.ps("ps_a", (128, 512), F32)
        ps_b = k.ps("ps_b", (128, 512), F32)
        ps_tp = k.ps("ps_tp", (128, 128), BF16)
        ps_tf = k.ps("ps_tf", (128, 128), F32)
        ps_lg = k.ps("ps_lg", (128, 8), F32)

        scale_b = k.rank1_rows("lns", ln2s_d, D, ps_r1, "ps_r1")
        shift_b = k.rank1_rows("lnb", ln2b_d, D, ps_r1, "ps_r1")
        bo_row = k.sb("bo_row", (1, D), F32)
        p.op("sync", lambda e: e.dma_start(out=bo_row[:, :], in_=bo_d[None, :]), writes=[k.b("bo_row")], dma=True)

        stag = k.sb("stag", (128, 1024), F32)
        stagB = k.sb("stagB", (128, 1024), F32)
        w1h_bf = k.sb("w1h_bf", (128, KB * 1024), BF16)
        wo_bf, _won = w1h_bf, "w1h_bf"
        k.load_w_bf("wo", wo_d, KB, D, stag, "stag", w1h_bf, "w1h_bf", stag2=stagB, stag2_name="stagB")
        gw_sb = k.sb("gw_sb", (128, KB * 8), F32)
        for kb in range(KB):
            p.op("sync", lambda e, kb=kb: e.dma_start(out=gw_sb[:, kb * 8:(kb + 1) * 8],
                                                      in_=gw_d[kb * 128:(kb + 1) * 128, :]),
                 writes=[k.b("gw_sb")], dma=True)

        x_t = k.sb("x_t", (128, D), F32)
        u_t = k.sb("u_t", (128, D), F32)
        tmp = k.sb("tmp", (128, D), F32)
        col = k.sb("col", (128, 8), F32)
        h_f = k.sb("h_f", (128, D), F32)
        h_bf = k.sb("h_bf", (128, D), BF16)
        hT = k.sb("hT", (128, KB * 128), BF16)
        ctx_t = k.sb("ctx_t", (128, D), F32)
        ctx_bf = k.sb("ctx_bf", (128, D), BF16)
        cT = k.sb("cT", (128, KB * 128), BF16)
        statf = k.sb("statf", (128, 128), F32)
        gE = k.sb("gE", (128, 8), F32)
        gC = k.sb("gC", (128, 8), F32)
        wfull = k.sb("wfull", (128, TT * 8), F32)
        wt_sb = k.sb("wt_sb", (8, TOK), F32)
        wrow = k.sb("wrow", (1, 1024), F32)
        wb_bf = k.sb("wb_bf", (128, 1024), BF16)

        for tt in range(TT):
            r0 = tt * 128
            p.op("sync", lambda e, r0=r0: e.dma_start(out=ctx_t[:, :], in_=ctx_d[r0:r0 + 128, :]),
                 writes=[k.b("ctx_t")], dma=True)
            p.op("dve", lambda e: e.tensor_copy(out=ctx_bf[:, :], in_=ctx_t[:, :]),
                 reads=[k.b("ctx_t")], writes=[k.b("ctx_bf")])
            k.tp_blocks(ctx_bf, "ctx_bf", ps_tp, "ps_tp", cT, "cT", KB)
            p.op("sync", lambda e, r0=r0: e.dma_start(out=x_t[:, :], in_=u_d[r0:r0 + 128, :]),
                 writes=[k.b("x_t")], dma=True)
            for jt in range(2):
                pg, pgn = (ps_a, "ps_a") if jt == 0 else (ps_b, "ps_b")
                ex = (lambda e, jt=jt, pg=pg: e.matmul(pg[:, 0:512], k.ones1[:, :],
                                                       bo_row[:, jt * 512:(jt + 1) * 512], start=False, stop=True),
                      [k.b("ones1"), k.b("bo_row")])
                k.gemm(cT, "cT", wo_bf, "w1h_bf", pg, pgn, 512, D, w_off=jt * 512, extra=ex)
                p.op("dve", lambda e, jt=jt, pg=pg: e.tensor_add(out=x_t[:, jt * 512:(jt + 1) * 512],
                                                                 in0=x_t[:, jt * 512:(jt + 1) * 512],
                                                                 in1=pg[:, 0:512]),
                     reads=[k.b("x_t"), k.b(pgn)], writes=[k.b("x_t")])
            p.op("pool", lambda e, r0=r0: e.dma_start(out=x1_t_d[r0:r0 + 128, :], in_=x_t[:, :]),
                 reads=[k.b("x_t")], writes=[k.b("x1scr")], dma=True)
            k.ln_plain(x_t, "x_t", u_t, "u_t", tmp, "tmp", col, "col")
            p.op("dve", lambda e: e.tensor_mul(out=h_f[:, :], in0=u_t[:, :], in1=scale_b[:, :]),
                 reads=[k.b("u_t"), k.b("lns_bc")], writes=[k.b("h_f")])
            p.op("dve", lambda e: e.tensor_add(out=h_f[:, :], in0=h_f[:, :], in1=shift_b[:, :]),
                 reads=[k.b("h_f"), k.b("lnb_bc")], writes=[k.b("h_f")])
            p.op("act", lambda e: e.copy(out=h_bf[:, :], in_=h_f[:, :]),
                 reads=[k.b("h_f")], writes=[k.b("h_bf")])
            k.tp_blocks(h_bf, "h_bf", ps_tp, "ps_tp", hT, "hT", KB)
            for kb in range(KB):
                p.op("pool", lambda e, kb=kb, r0=r0: e.dma_start(out=hT_dram[kb, :, r0:r0 + 128],
                                                                 in_=hT[:, kb * 128:(kb + 1) * 128]),
                     reads=[k.b("hT")], writes=[k.b("hTscr")], dma=True)
            for kb in range(KB):
                p.op("pe", lambda e, kb=kb: e.transpose(out=ps_tf[:, :], in_=h_f[:, kb * 128:(kb + 1) * 128],
                                                        identity=k.id_f32[:, :]),
                     reads=[k.b("h_f"), k.b("id_f32")], writes=[k.b("ps_tf")])
                p.op("dve", lambda e: e.tensor_copy(out=statf[:, :], in_=ps_tf[:, :]),
                     reads=[k.b("ps_tf")], writes=[k.b("statf")])
                p.op("pe", lambda e, kb=kb: e.matmul(ps_lg[:, :], statf[:, :], gw_sb[:, kb * 8:(kb + 1) * 8],
                                                     start=(kb == 0), stop=(kb == KB - 1)),
                     reads=[k.b("statf"), k.b("gw_sb")], writes=[k.b("ps_lg")],
                     quiet=(kb != KB - 1))
            p.op("dve", lambda e: e.reduce_max(out=col[:, 0:1], in_=ps_lg[:, :], axis=AX.X),
                 reads=[k.b("ps_lg")], writes=[k.b("col")])
            p.op("dve", lambda e: e.tensor_scalar(out=gC[:, :], in0=ps_lg[:, :], scalar1=col[:, 0:1],
                                                  scalar2=None, op0=ALU.is_ge),
                 reads=[k.b("ps_lg"), k.b("col")], writes=[k.b("gC")])
            p.op("dve", lambda e: e.scalar_tensor_tensor(out=gE[:, :], in0=gC[:, :], scalar=-1e30,
                                                         in1=ps_lg[:, :], op0=ALU.mult, op1=ALU.add),
                 reads=[k.b("gC"), k.b("ps_lg")], writes=[k.b("gE")])
            p.op("dve", lambda e: e.reduce_max(out=col[:, 2:3], in_=gE[:, :], axis=AX.X),
                 reads=[k.b("gE")], writes=[k.b("col")])
            p.op("dve", lambda e: e.tensor_scalar_mul(out=col[:, 1:2], in0=col[:, 0:1], scalar1=-1.0),
                 reads=[k.b("col")], writes=[k.b("col")])
            p.op("act", lambda e: e.activation(out=col[:, 3:4], in_=col[:, 2:3], func=AF.Exp, bias=col[:, 1:2]),
                 reads=[k.b("col")], writes=[k.b("col")])
            p.op("dve", lambda e: e.tensor_scalar_add(out=col[:, 4:5], in0=col[:, 3:4], scalar1=1.0),
                 reads=[k.b("col")], writes=[k.b("col")])
            p.op("dve", lambda e: e.reciprocal(out=col[:, 4:5], in_=col[:, 4:5]),
                 reads=[k.b("col")], writes=[k.b("col")])
            p.op("act", lambda e: e.activation(out=gE[:, :], in_=ps_lg[:, :], func=AF.Exp, bias=col[:, 1:2]),
                 reads=[k.b("ps_lg"), k.b("col")], writes=[k.b("gE")])
            p.op("dve", lambda e: e.tensor_scalar(out=gC[:, :], in0=ps_lg[:, :], scalar1=col[:, 2:3],
                                                  scalar2=None, op0=ALU.is_ge),
                 reads=[k.b("ps_lg"), k.b("col")], writes=[k.b("gC")])
            p.op("dve", lambda e: e.tensor_mul(out=gE[:, :], in0=gE[:, :], in1=gC[:, :]),
                 reads=[k.b("gE"), k.b("gC")], writes=[k.b("gE")])
            p.op("dve", lambda e, tt=tt: e.tensor_scalar_mul(out=wfull[:, tt * 8:(tt + 1) * 8], in0=gE[:, :],
                                                             scalar1=col[:, 4:5]),
                 reads=[k.b("gE"), k.b("col")], writes=[k.b("wfull")])
        for tt in range(TT):
            p.op("pe", lambda e, tt=tt: e.transpose(out=ps_tf[0:8, 0:128], in_=wfull[:, tt * 8:(tt + 1) * 8],
                                                    identity=k.id_f32[:, :]),
                 reads=[k.b("wfull"), k.b("id_f32")], writes=[k.b("ps_tf")])
            p.op("dve", lambda e, tt=tt: e.tensor_copy(out=wt_sb[0:8, tt * 128:(tt + 1) * 128],
                                                       in_=ps_tf[0:8, 0:128]),
                 reads=[k.b("ps_tf")], writes=[k.b("wt_sb")])
        p.op("pool", lambda e: e.dma_start(out=wt_dram[:, :], in_=wt_sb[:, :]),
             reads=[k.b("wt_sb")], writes=[k.b("wtscr")], dma=True)

        w2h_bf = k.sb("w2h_bf", (128, KB * 1024), BF16)
        w3h_bf = k.sb("w3h_bf", (128, KB * D), BF16)
        hT_c2 = k.sb("hT_c2", (128, 2 * KB * 512), BF16)
        G_sb = k.sb("G_sb", (128, 8 * 512), BF16)
        acc2 = k.sb("acc2", (128, 2 * KB * 512), F32)
        silu_t = k.sb("silu_t", (128, 512), F32)
        for pp in range(2):
            p0 = pp * 1024
            for tci in range(2):
                t0 = p0 + tci * 512
                for kb in range(KB):
                    p.op("sync", lambda e, kb=kb, t0=t0, tci=tci: e.dma_start(
                        out=hT_c2[:, (tci * KB + kb) * 512:(tci * KB + kb + 1) * 512],
                        in_=hT_dram[kb, :, t0:t0 + 512]),
                        reads=[k.b("hTscr")], writes=[k.b("hT_c2")], dma=True)
            for e_i in range(8):
                p.op("sync", lambda e, e_i=e_i, p0=p0: e.dma_start(out=wrow[:, :],
                                                                   in_=wt_dram[e_i, p0:p0 + 1024][None, :]),
                     reads=[k.b("wtscr")], writes=[k.b("wrow")], dma=True)
                for tci in range(2):
                    p.op("pe", lambda e, tci=tci: e.matmul(ps_r1[:, 512:1024], k.ones1[:, :],
                                                           wrow[:, tci * 512:(tci + 1) * 512],
                                                           start=True, stop=True),
                         reads=[k.b("ones1"), k.b("wrow")], writes=[k.b("ps_r1")])
                    p.op("act", lambda e, tci=tci: e.copy(out=wb_bf[:, tci * 512:(tci + 1) * 512],
                                                          in_=ps_r1[:, 512:1024]),
                         reads=[k.b("ps_r1")], writes=[k.b("wb_bf")])
                for hh in range(2):
                    k.load_w_bf("w1e", w1_d[e_i][:, hh * 1024:(hh + 1) * 1024], KB, 1024, stag, "stag",
                                w1h_bf, "w1h_bf", stag2=stagB, stag2_name="stagB")
                    k.load_w_bf("w2e", w2_d[e_i][:, hh * 1024:(hh + 1) * 1024], KB, 1024, stag, "stag",
                                w2h_bf, "w2h_bf", stag2=stagB, stag2_name="stagB")
                    k.load_w_bf("w3e", w3_d[e_i][hh * 1024:(hh + 1) * 1024], KB, D, stag, "stag",
                                w3h_bf, "w3h_bf", stag2=stagB, stag2_name="stagB")
                    for tci in range(2):
                        for jt in range(8):
                            for kb in range(KB):
                                p.op("pe", lambda e, jt=jt, kb=kb, tci=tci: e.matmul(
                                    ps_a[:, :], w1h_bf[:, kb * 1024 + jt * 128: kb * 1024 + (jt + 1) * 128],
                                    hT_c2[:, (tci * KB + kb) * 512:(tci * KB + kb + 1) * 512],
                                    start=(kb == 0), stop=(kb == KB - 1)),
                                    reads=[k.b("w1h_bf"), k.b("hT_c2")], writes=[k.b("ps_a")],
                                    quiet=(kb != KB - 1))
                            for kb in range(KB):
                                p.op("pe", lambda e, jt=jt, kb=kb, tci=tci: e.matmul(
                                    ps_b[:, :], w2h_bf[:, kb * 1024 + jt * 128: kb * 1024 + (jt + 1) * 128],
                                    hT_c2[:, (tci * KB + kb) * 512:(tci * KB + kb + 1) * 512],
                                    start=(kb == 0), stop=(kb == KB - 1)),
                                    reads=[k.b("w2h_bf"), k.b("hT_c2")], writes=[k.b("ps_b")],
                                    quiet=(kb != KB - 1))
                            p.op("act", lambda e: e.activation(out=silu_t[:, :], in_=ps_a[:, :], func=AF.Silu),
                                 reads=[k.b("ps_a")], writes=[k.b("silu_t")])
                            p.op("dve", lambda e, jt=jt: e.tensor_mul(out=G_sb[:, jt * 512:(jt + 1) * 512],
                                                                      in0=silu_t[:, :], in1=ps_b[:, :]),
                                 reads=[k.b("silu_t"), k.b("ps_b")], writes=[k.b("G_sb")])
                            p.op("dve", lambda e, jt=jt, tci=tci: e.tensor_mul(
                                out=G_sb[:, jt * 512:(jt + 1) * 512],
                                in0=G_sb[:, jt * 512:(jt + 1) * 512],
                                in1=wb_bf[:, tci * 512:(tci + 1) * 512]),
                                reads=[k.b("G_sb"), k.b("wb_bf")], writes=[k.b("G_sb")])
                        for dt in range(KB):
                            for jb in range(KB):
                                p.op("pe", lambda e, dt=dt, jb=jb: e.matmul(
                                    ps_r1[:, 0:512], w3h_bf[:, jb * D + dt * 128: jb * D + (dt + 1) * 128],
                                    G_sb[:, jb * 512:(jb + 1) * 512], start=(jb == 0), stop=(jb == KB - 1)),
                                    reads=[k.b("w3h_bf"), k.b("G_sb")], writes=[k.b("ps_r1")],
                                    quiet=(jb != KB - 1))
                            ai = (tci * KB + dt) * 512
                            if e_i == 0 and hh == 0:
                                p.op("dve", lambda e, ai=ai: e.tensor_copy(out=acc2[:, ai:ai + 512],
                                                                           in_=ps_r1[:, 0:512]),
                                     reads=[k.b("ps_r1")], writes=[k.b("acc2")])
                            else:
                                p.op("dve", lambda e, ai=ai: e.tensor_add(out=acc2[:, ai:ai + 512],
                                                                          in0=acc2[:, ai:ai + 512],
                                                                          in1=ps_r1[:, 0:512]),
                                     reads=[k.b("acc2"), k.b("ps_r1")], writes=[k.b("acc2")])
            for tci in range(2):
                for ti in range(4):
                    r0 = p0 + tci * 512 + ti * 128
                    p.op("sync", lambda e, r0=r0: e.dma_start(out=x_t[:, :], in_=x1_t_d[r0:r0 + 128, :]),
                         reads=[k.b("x1scr")], writes=[k.b("x_t")], dma=True)
                    p.op("sync", lambda e, r0=r0: e.dma_start(out=u_t[:, :], in_=xres_d[r0:r0 + 128, :]),
                         writes=[k.b("u_t")], dma=True)
                    for dt in range(KB):
                        ai = (tci * KB + dt) * 512 + ti * 128
                        p.op("pe", lambda e, ai=ai: e.transpose(
                            out=ps_tf[:, :], in_=acc2[:, ai:ai + 128],
                            identity=k.id_f32[:, :]),
                            reads=[k.b("acc2"), k.b("id_f32")], writes=[k.b("ps_tf")])
                        p.op("dve", lambda e, dt=dt: e.tensor_add(out=x_t[:, dt * 128:(dt + 1) * 128],
                                                                  in0=x_t[:, dt * 128:(dt + 1) * 128],
                                                                  in1=ps_tf[:, :]),
                             reads=[k.b("x_t"), k.b("ps_tf")], writes=[k.b("x_t")])
                    p.op("dve", lambda e: e.scalar_tensor_tensor(out=x_t[:, :], in0=x_t[:, :], scalar=0.5,
                                                                 in1=u_t[:, :], op0=ALU.mult, op1=ALU.add),
                         reads=[k.b("x_t"), k.b("u_t")], writes=[k.b("x_t")])
                    p.op("pool", lambda e, r0=r0: e.dma_start(out=xnew_d[r0:r0 + 128, :], in_=x_t[:, :]),
                         reads=[k.b("x_t")], writes=[k.b("xnew")], dma=True)

        qkv_arenas = [(w1h_bf, "w1h_bf"), (w2h_bf, "w2h_bf"), (w3h_bf, "w3h_bf")]
        if tail == "qkv":
            for part in range(3):
                ar, an = qkv_arenas[part]
                k.load_w_bf("wqkv" + str(part), wqkv_d, KB, 1024, stag, "stag", ar, an,
                            col0=part * 1024, stag2=stagB, stag2_name="stagB")
        for tt in range(TT):
            r0 = tt * 128
            p.op("sync", lambda e, r0=r0: e.dma_start(out=x_t[:, :], in_=xnew_d[r0:r0 + 128, :]),
                 reads=[k.b("xnew")], writes=[k.b("x_t")], dma=True)
            k.ln_plain(x_t, "x_t", u_t, "u_t", tmp, "tmp", col, "col")
            if tail == "ln":
                p.op("pool", lambda e, r0=r0: e.dma_start(out=un_d[r0:r0 + 128, :], in_=u_t[:, :]),
                     reads=[k.b("u_t")], writes=[k.b("unorm")], dma=True)
            else:
                p.op("act", lambda e: e.copy(out=h_bf[:, :], in_=u_t[:, :]),
                     reads=[k.b("u_t")], writes=[k.b("h_bf")])
                k.tp_blocks(h_bf, "h_bf", ps_tp, "ps_tp", hT, "hT", KB)
                for jt in range(6):
                    pg, pgn = (ps_a, "ps_a") if jt % 2 == 0 else (ps_b, "ps_b")
                    p.op("sync", lambda e, jt=jt: e.dma_start(out=wrow[:, 0:512],
                                                              in_=bqkv_d[jt * 512:(jt + 1) * 512][None, :]),
                         writes=[k.b("wrow")], dma=True)
                    ex = (lambda e, jt=jt, pg=pg: e.matmul(
                        pg[:, 0:512], k.ones1[:, :], wrow[:, 0:512],
                        start=False, stop=True), [k.b("ones1"), k.b("wrow")])
                    ar, an = qkv_arenas[jt // 2]
                    k.gemm(hT, "hT", ar, an, pg, pgn, 512, D, w_off=(jt % 2) * 512, extra=ex)
                    p.op("act", lambda e, pg=pg: e.copy(out=silu_t[:, :], in_=pg[:, 0:512]),
                         reads=[k.b(pgn)], writes=[k.b("silu_t")])
                    p.op("pool", lambda e, r0=r0, jt=jt: e.dma_start(
                        out=qkv_d[r0:r0 + 128, jt * 512:(jt + 1) * 512], in_=silu_t[:, :]),
                        reads=[k.b("silu_t")], writes=[k.b("qkv")], dma=True)
        fin = [k.b("xnew")] + ([k.b("qkv")] if tail == "qkv" else [k.b("unorm")])
        p.op("dve", lambda e: e.engine_nop(), reads=fin)
        p.build(k.stack)
    return k


def build_bias():
    k = Ker()
    nc, p = k.nc, k.p
    with k.stack:
        iota_d = k.dram_in("iota", (128, T))
        qcols_d = k.dram_in("qcols", (128, 128))
        qcols64_d = k.dram_in("qcols64", (64, 128))
        w1t_d = k.dram_in("w1t", (128, 1))
        b1t_d = k.dram_in("b1t", (128, 1))
        w2bd_d = k.dram_in("w2bd", (128, 64))
        b2t_d = k.dram_in("b2t", (64, 1))
        cpar_d = k.dram_in("cpar", (1, 1))
        lmul_d = k.dram_in("lmul", (1, 1))
        bias_d = k.dram_out("bias8", (128, 4, 16, T), BF16)

        k.prelude_ident()
        ps_r1 = k.ps("ps_r1", (128, 8), F32)
        ps_bb = k.ps("ps_bb", (64, 512), F32)

        iota = k.sb("iota_s", (128, T), F32)
        qcols = k.sb("qcols_s", (128, 128), F32)
        qcols64 = k.sb("qcols64_s", (64, 128), F32)
        w1t = k.sb("w1t_s", (128, 1), F32)
        b1t = k.sb("b1t_s", (128, 1), F32)
        w2bd_f = k.sb("w2bd_f", (128, 64), F32)
        w2bd = k.sb("w2bd_s", (128, 64), BF16)
        b2t = k.sb("b2t_s", (64, 1), F32)
        cpar = k.sb("cpar_s", (1, 1), F32)
        lmul = k.sb("lmul_s", (1, 1), F32)
        for nm, dst, src in (("iota_s", iota, iota_d), ("qcols_s", qcols, qcols_d), ("qcols64_s", qcols64, qcols64_d),
                             ("w1t_s", w1t, w1t_d), ("b1t_s", b1t, b1t_d), ("w2bd_f", w2bd_f, w2bd_d),
                             ("b2t_s", b2t, b2t_d), ("cpar_s", cpar, cpar_d), ("lmul_s", lmul, lmul_d)):
            p.op("sync", lambda e, dst=dst, src=src: e.dma_start(out=dst[:, :], in_=src[:, :]),
                 writes=[k.b(nm)], dma=True)
        p.op("dve", lambda e: e.tensor_copy(out=w2bd[:, :], in_=w2bd_f[:, :]),
             reads=[k.b("w2bd_f")], writes=[k.b("w2bd_s")])
        cc = k.sb("cc", (128, 4), F32)
        p.op("pe", lambda e: e.matmul(ps_r1[:, 0:1], k.ones1[:, :], cpar[:, :], start=True, stop=True),
             reads=[k.b("ones1"), k.b("cpar_s")], writes=[k.b("ps_r1")])
        p.op("act", lambda e: e.copy(out=cc[:, 0:1], in_=ps_r1[:, 0:1]), reads=[k.b("ps_r1")], writes=[k.b("cc")])
        p.op("pe", lambda e: e.matmul(ps_r1[:, 1:2], k.ones1[:, :], lmul[:, :], start=True, stop=True),
             reads=[k.b("ones1"), k.b("lmul_s")], writes=[k.b("ps_r1")])
        p.op("act", lambda e: e.activation(out=cc[:, 1:2], in_=ps_r1[:, 1:2], func=AF.Abs, scale=512.0),
             reads=[k.b("ps_r1")], writes=[k.b("cc")])
        b2x8 = k.sb("b2x8", (64, 1), F32)
        p.op("act", lambda e: e.activation(out=b2x8[:, :], in_=b2t[:, :], func=AF.Copy, scale=8.0),
             reads=[k.b("b2t_s")], writes=[k.b("b2x8")])

        ic = k.sb("ic", (128, 4), F32)
        rel_t = k.sb("rel_t", (128, 512), F32)
        ft = k.sb("ft", (128, 512), BF16)
        bf_f = k.sb("bf_f", (64, 512), F32)
        cmp_t = k.sb("cmp_t", (64, 512), F32)
        out_bf = k.sb("out_bf", (64, 512), BF16)

        for g in range(128):
            p.op("dve", lambda e, g=g: e.tensor_max(out=ic[:, 0:1], in0=qcols[:, g:g + 1], in1=cc[:, 1:2]),
                 reads=[k.b("qcols_s"), k.b("cc")], writes=[k.b("ic")])
            p.op("act", lambda e: e.activation(out=ic[:, 1:2], in_=ic[:, 0:1], func=AF.Abs, scale=cc[:, 0:1]),
                 reads=[k.b("ic"), k.b("cc")], writes=[k.b("ic")])
            p.op("act", lambda e: e.activation(out=ic[:, 2:3], in_=ic[:, 1:2], func=AF.Ln,
                                               bias=k.onecol[:, 0:1]),
                 reads=[k.b("ic"), k.b("onecol")], writes=[k.b("ic")])
            p.op("dve", lambda e: e.tensor_scalar_add(out=ic[:, 2:3], in0=ic[:, 2:3], scalar1=1e-6),
                 reads=[k.b("ic")], writes=[k.b("ic")])
            p.op("dve", lambda e: e.reciprocal(out=ic[:, 3:4], in_=ic[:, 2:3]),
                 reads=[k.b("ic")], writes=[k.b("ic")])
            trips = min(8, (g * 32 + 31) // 512 + 1)
            for st in range(trips):
                s0 = st * 512
                p.op("dve", lambda e, s0=s0, g=g: e.tensor_scalar_sub(out=rel_t[:, :], in0=iota[:, s0:s0 + 512],
                                                                      scalar1=qcols[:, g:g + 1]),
                     reads=[k.b("iota_s"), k.b("qcols_s")], writes=[k.b("rel_t")])
                p.op("act", lambda e: e.activation(out=rel_t[:, :], in_=rel_t[:, :], func=AF.Abs, scale=cc[:, 0:1]),
                     reads=[k.b("rel_t"), k.b("cc")], writes=[k.b("rel_t")])
                p.op("act", lambda e: e.activation(out=rel_t[:, :], in_=rel_t[:, :], func=AF.Ln,
                                                   bias=k.onecol[:, 0:1]),
                     reads=[k.b("rel_t"), k.b("onecol")], writes=[k.b("rel_t")])
                p.op("dve", lambda e: e.tensor_scalar_mul(out=rel_t[:, :], in0=rel_t[:, :], scalar1=ic[:, 3:4]),
                     reads=[k.b("rel_t"), k.b("ic")], writes=[k.b("rel_t")])
                p.op("act", lambda e: e.activation(out=ft[:, :], in_=rel_t[:, :], func=AF.Relu,
                                                   scale=w1t[:, 0:1], bias=b1t[:, 0:1]),
                     reads=[k.b("rel_t"), k.b("w1t_s"), k.b("b1t_s")], writes=[k.b("ft")])
                p.op("pe", lambda e: e.matmul(ps_bb[:, :], w2bd[:, :], ft[:, :], start=True, stop=True),
                     reads=[k.b("w2bd_s"), k.b("ft")], writes=[k.b("ps_bb")])
                p.op("act", lambda e: e.activation(out=bf_f[:, :], in_=ps_bb[:, :], func=AF.Copy,
                                                   scale=8.0),
                     reads=[k.b("ps_bb")], writes=[k.b("bf_f")])
                p.op("dve", lambda e: e.tensor_scalar_add(out=bf_f[:, :], in0=bf_f[:, :],
                                                          scalar1=b2x8[:, 0:1]),
                     reads=[k.b("bf_f"), k.b("b2x8")], writes=[k.b("bf_f")])
                if st >= (g * 32) // 512:
                    p.op("dve", lambda e, s0=s0, g=g: e.tensor_scalar(out=cmp_t[:, :], in0=iota[0:64, s0:s0 + 512],
                                                                      scalar1=qcols64[:, g:g + 1], scalar2=None,
                                                                      op0=ALU.is_gt),
                         reads=[k.b("iota_s"), k.b("qcols64_s")], writes=[k.b("cmp_t")])
                    p.op("dve", lambda e: e.tensor_scalar_sub(out=bf_f[:, :], in0=bf_f[:, :], scalar1=NEG8),
                         reads=[k.b("bf_f")], writes=[k.b("bf_f")])
                    p.op("dve", lambda e: e.scalar_tensor_tensor(out=cmp_t[:, :], in0=cmp_t[:, :], scalar=-1.0,
                                                                 in1=bf_f[:, :], op0=ALU.mult, op1=ALU.mult),
                         reads=[k.b("cmp_t"), k.b("bf_f")], writes=[k.b("cmp_t")])
                    p.op("dve", lambda e: e.tensor_add(out=bf_f[:, :], in0=bf_f[:, :], in1=cmp_t[:, :]),
                         reads=[k.b("bf_f"), k.b("cmp_t")], writes=[k.b("bf_f")])
                    p.op("dve", lambda e: e.tensor_scalar_add(out=bf_f[:, :], in0=bf_f[:, :], scalar1=NEG8),
                         reads=[k.b("bf_f")], writes=[k.b("bf_f")])
                p.op("dve", lambda e: e.tensor_copy(out=out_bf[:, :], in_=bf_f[:, :]),
                     reads=[k.b("bf_f")], writes=[k.b("out_bf")])
                p.op("pool", lambda e, g=g, s0=s0: e.dma_start(out=bias_d[g, :, :, s0:s0 + 512], in_=out_bf[:, :]),
                     reads=[k.b("out_bf")], writes=[k.b("bias8")], dma=True)
        p.op("dve", lambda e: e.engine_nop(), reads=[k.b("bias8")])
        p.build(k.stack)
    return k



def build_gate():
    k = Ker()
    nc, p = k.nc, k.p
    NT = TOKTOT // 128
    CAP = 4096.0
    with k.stack:
        uT_d = k.dram_in("uT", (D, TOKTOT))
        wg_d = k.dram_in("wg", (D, 8))
        uts_d = k.dram_in("utstrict", (128, 128))
        wtab_d = k.dram_out("wtab", (TOKTOT, 8))
        tots_d = k.dram_tmp("totscr", (2, 128, 8))
        pre_d = k.dram_tmp("prescr", (2, 128, 8))

        k.prelude_ident()
        ps_lg = k.ps("ps_lg", (128, 8), F32)
        ps_cs = k.ps("ps_cs", (128, 8), F32)
        ps_tot = k.ps("ps_tot", (1, 8), F32)
        ps_pre = k.ps("ps_pre", (128, 8), F32)

        uts = k.sb("uts", (128, 128), F32)
        p.op("sync", lambda e: e.dma_start(out=uts[:, :], in_=uts_d[:, :]), writes=[k.b("uts")], dma=True)
        onesc = k.sb("onesc", (128, 1), F32)
        p.op("pool", lambda e: e.memset(onesc[:, :], 1.0), writes=[k.b("onesc")])
        wg_sb = k.sb("wg_sb", (128, KB * 8), F32)
        for kb in range(KB):
            p.op("sync", lambda e, kb=kb: e.dma_start(out=wg_sb[:, kb * 8:(kb + 1) * 8],
                                                      in_=wg_d[kb * 128:(kb + 1) * 128, :]),
                 writes=[k.b("wg_sb")], dma=True)
        statq = k.sb("statq", (128, KB * 128), F32)
        p_sb = k.sb("p_sb", (128, 8), F32)
        t8 = k.sb("t8", (128, 8), F32)
        cols = k.sb("cols", (128, 8), F32)
        M0 = k.sb("M0", (128, NT * 8), F32)
        M1 = k.sb("M1", (128, NT * 8), F32)
        EX0 = k.sb("EX0", (128, NT * 8), F32)
        EX1 = k.sb("EX1", (128, NT * 8), F32)
        G0c = k.sb("G0c", (128, NT), F32)
        G1c = k.sb("G1c", (128, NT), F32)
        tsm = k.sb("tsm", (1, 8), F32)
        t0_sb = k.sb("t0_sb", (128, 8), F32)
        t1_sb = k.sb("t1_sb", (128, 8), F32)
        tot0 = k.sb("tot0", (1, 8), F32)
        pre_sb = k.sb("pre_sb", (128, 8), F32)
        pre0r = k.sb("pre0r", (1, NT * 8), F32)
        pre1r = k.sb("pre1r", (1, NT * 8), F32)
        cf = k.sb("cf", (128, 8), F32)
        wt_t = k.sb("wt_t", (128, 8), F32)

        for i in range(NT):
            c0 = i * 128
            for kb in range(KB):
                p.op("sync", lambda e, kb=kb, c0=c0: e.dma_start(out=statq[:, kb * 128:(kb + 1) * 128],
                                                                 in_=uT_d[kb * 128:(kb + 1) * 128, c0:c0 + 128]),
                     writes=[k.b("statq")], dma=True)
            for kb in range(KB):
                p.op("pe", lambda e, kb=kb: e.matmul(ps_lg[:, :], statq[:, kb * 128:(kb + 1) * 128],
                                                     wg_sb[:, kb * 8:(kb + 1) * 8],
                                                     start=(kb == 0), stop=(kb == KB - 1)),
                     reads=[k.b("statq"), k.b("wg_sb")], writes=[k.b("ps_lg")], quiet=(kb != KB - 1))
            p.op("dve", lambda e: e.reduce_max(out=cols[:, 0:1], in_=ps_lg[:, :], axis=AX.X),
                 reads=[k.b("ps_lg")], writes=[k.b("cols")])
            p.op("dve", lambda e: e.tensor_scalar_mul(out=cols[:, 1:2], in0=cols[:, 0:1], scalar1=-1.0),
                 reads=[k.b("cols")], writes=[k.b("cols")])
            p.op("act", lambda e: e.activation(out=p_sb[:, :], in_=ps_lg[:, :], func=AF.Exp, bias=cols[:, 1:2]),
                 reads=[k.b("ps_lg"), k.b("cols")], writes=[k.b("p_sb")])
            p.op("dve", lambda e: e.reduce_sum(out=cols[:, 2:3], in_=p_sb[:, :], axis=AX.X),
                 reads=[k.b("p_sb")], writes=[k.b("cols")])
            p.op("dve", lambda e: e.reciprocal(out=cols[:, 3:4], in_=cols[:, 2:3]),
                 reads=[k.b("cols")], writes=[k.b("cols")])
            p.op("dve", lambda e: e.tensor_scalar_mul(out=p_sb[:, :], in0=p_sb[:, :], scalar1=cols[:, 3:4]),
                 reads=[k.b("p_sb"), k.b("cols")], writes=[k.b("p_sb")])
            p.op("dve", lambda e: e.reduce_max(out=cols[:, 4:5], in_=p_sb[:, :], axis=AX.X),
                 reads=[k.b("p_sb")], writes=[k.b("cols")])
            p.op("dve", lambda e, i=i: e.tensor_scalar(out=M0[:, i * 8:(i + 1) * 8], in0=p_sb[:, :],
                                                       scalar1=cols[:, 4:5], scalar2=None, op0=ALU.is_ge),
                 reads=[k.b("p_sb"), k.b("cols")], writes=[k.b("M0")])
            p.op("dve", lambda e, i=i: e.scalar_tensor_tensor(out=t8[:, :], in0=M0[:, i * 8:(i + 1) * 8],
                                                              scalar=-1e30, in1=p_sb[:, :],
                                                              op0=ALU.mult, op1=ALU.add),
                 reads=[k.b("M0"), k.b("p_sb")], writes=[k.b("t8")])
            p.op("dve", lambda e: e.reduce_max(out=cols[:, 5:6], in_=t8[:, :], axis=AX.X),
                 reads=[k.b("t8")], writes=[k.b("cols")])
            p.op("dve", lambda e: e.tensor_scalar(out=t8[:, :], in0=p_sb[:, :], scalar1=cols[:, 5:6],
                                                  scalar2=None, op0=ALU.is_ge),
                 reads=[k.b("p_sb"), k.b("cols")], writes=[k.b("t8")])
            p.op("dve", lambda e, i=i: e.tensor_sub(out=M1[:, i * 8:(i + 1) * 8], in0=t8[:, :],
                                                    in1=M0[:, i * 8:(i + 1) * 8]),
                 reads=[k.b("t8"), k.b("M0")], writes=[k.b("M1")])
            p.op("dve", lambda e: e.tensor_add(out=cols[:, 6:7], in0=cols[:, 4:5], in1=cols[:, 5:6]),
                 reads=[k.b("cols")], writes=[k.b("cols")])
            p.op("dve", lambda e: e.reciprocal(out=cols[:, 6:7], in_=cols[:, 6:7]),
                 reads=[k.b("cols")], writes=[k.b("cols")])
            p.op("dve", lambda e, i=i: e.tensor_mul(out=G0c[:, i:i + 1], in0=cols[:, 4:5], in1=cols[:, 6:7]),
                 reads=[k.b("cols")], writes=[k.b("G0c")])
            p.op("dve", lambda e, i=i: e.tensor_mul(out=G1c[:, i:i + 1], in0=cols[:, 5:6], in1=cols[:, 6:7]),
                 reads=[k.b("cols")], writes=[k.b("G1c")])
            for mi, (Mb, EXb, Mn, EXn) in enumerate(((M0, EX0, "M0", "EX0"), (M1, EX1, "M1", "EX1"))):
                p.op("pe", lambda e, i=i, Mb=Mb: e.matmul(ps_cs[:, :], uts[:, :], Mb[:, i * 8:(i + 1) * 8],
                                                          start=True, stop=True),
                     reads=[k.b("uts"), k.b(Mn)], writes=[k.b("ps_cs")])
                p.op("act", lambda e, i=i, EXb=EXb: e.copy(out=EXb[:, i * 8:(i + 1) * 8], in_=ps_cs[:, :]),
                     reads=[k.b("ps_cs")], writes=[k.b(EXn)])
                p.op("pe", lambda e, i=i, Mb=Mb: e.matmul(ps_tot[:, :], onesc[:, :], Mb[:, i * 8:(i + 1) * 8],
                                                          start=True, stop=True),
                     reads=[k.b("onesc"), k.b(Mn)], writes=[k.b("ps_tot")])
                p.op("act", lambda e: e.copy(out=tsm[:, :], in_=ps_tot[:, :]),
                     reads=[k.b("ps_tot")], writes=[k.b("tsm")])
                p.op("pool", lambda e, mi=mi, i=i: e.dma_start(out=tots_d[mi, i, :][None, :], in_=tsm[:, :]),
                     reads=[k.b("tsm")], writes=[k.b("totscr")], dma=True)
        p.op("sync", lambda e: e.dma_start(out=t0_sb[:, :], in_=tots_d[0]), reads=[k.b("totscr")],
             writes=[k.b("t0_sb")], dma=True)
        p.op("sync", lambda e: e.dma_start(out=t1_sb[:, :], in_=tots_d[1]), reads=[k.b("totscr")],
             writes=[k.b("t1_sb")], dma=True)
        p.op("pe", lambda e: e.matmul(ps_pre[:, :], uts[:, :], t0_sb[:, :], start=True, stop=True),
             reads=[k.b("uts"), k.b("t0_sb")], writes=[k.b("ps_pre")])
        p.op("act", lambda e: e.copy(out=pre_sb[:, :], in_=ps_pre[:, :]), reads=[k.b("ps_pre")], writes=[k.b("pre_sb")])
        p.op("pool", lambda e: e.dma_start(out=pre_d[0], in_=pre_sb[:, :]), reads=[k.b("pre_sb")],
             writes=[k.b("prescr")], dma=True)
        p.op("pe", lambda e: e.matmul(ps_tot[:, :], onesc[:, :], t0_sb[:, :], start=True, stop=True),
             reads=[k.b("onesc"), k.b("t0_sb")], writes=[k.b("ps_tot")])
        p.op("act", lambda e: e.copy(out=tot0[:, :], in_=ps_tot[:, :]), reads=[k.b("ps_tot")], writes=[k.b("tot0")])
        p.op("pe", lambda e: e.matmul(ps_pre[:, :], uts[:, :], t1_sb[:, :], start=True, stop=False),
             reads=[k.b("uts"), k.b("t1_sb")], writes=[k.b("ps_pre")], quiet=True)
        p.op("pe", lambda e: e.matmul(ps_pre[:, :], k.ones1[:, :], tot0[:, :], start=False, stop=True),
             reads=[k.b("ones1"), k.b("tot0")], writes=[k.b("ps_pre")])
        p.op("act", lambda e: e.copy(out=pre_sb[:, :], in_=ps_pre[:, :]), reads=[k.b("ps_pre")], writes=[k.b("pre_sb")])
        p.op("pool", lambda e: e.dma_start(out=pre_d[1], in_=pre_sb[:, :]), reads=[k.b("pre_sb")],
             writes=[k.b("prescr")], dma=True)
        p.op("sync", lambda e: e.dma_start(out=pre0r[:, :], in_=pre_d[0].rearrange("a b -> (a b)")[None, :]),
             reads=[k.b("prescr")], writes=[k.b("pre0r")], dma=True)
        p.op("sync", lambda e: e.dma_start(out=pre1r[:, :], in_=pre_d[1].rearrange("a b -> (a b)")[None, :]),
             reads=[k.b("prescr")], writes=[k.b("pre1r")], dma=True)
        for i in range(NT):
            for mi, (EXb, EXn, prer, prern, Gc, Gcn, Mb, Mn, wc) in enumerate((
                    (EX0, "EX0", pre0r, "pre0r", G0c, "G0c", M0, "M0", 0),
                    (EX1, "EX1", pre1r, "pre1r", G1c, "G1c", M1, "M1", 3))):
                p.op("pe", lambda e, i=i, prer=prer: e.matmul(ps_pre[:, :], k.ones1[:, :],
                                                              prer[:, i * 8:(i + 1) * 8], start=True, stop=True),
                     reads=[k.b("ones1"), k.b(prern)], writes=[k.b("ps_pre")])
                p.op("dve", lambda e, i=i, EXb=EXb: e.tensor_add(out=cf[:, :], in0=EXb[:, i * 8:(i + 1) * 8],
                                                                 in1=ps_pre[:, :]),
                     reads=[k.b(EXn), k.b("ps_pre")], writes=[k.b("cf")])
                p.op("dve", lambda e, i=i, Mb=Mb: e.tensor_mul(out=cf[:, :], in0=cf[:, :],
                                                               in1=Mb[:, i * 8:(i + 1) * 8]),
                     reads=[k.b("cf"), k.b(Mn)], writes=[k.b("cf")])
                p.op("dve", lambda e, wc=wc: e.reduce_sum(out=cols[:, wc:wc + 1], in_=cf[:, :], axis=AX.X),
                     reads=[k.b("cf")], writes=[k.b("cols")])
                p.op("dve", lambda e, wc=wc: e.tensor_scalar(out=cols[:, wc + 1:wc + 2], in0=cols[:, wc:wc + 1],
                                                             scalar1=CAP, scalar2=None, op0=ALU.is_lt),
                     reads=[k.b("cols")], writes=[k.b("cols")])
                p.op("dve", lambda e, i=i, Gc=Gc, wc=wc: e.tensor_mul(out=cols[:, wc + 2:wc + 3],
                                                                     in0=cols[:, wc + 1:wc + 2],
                                                                     in1=Gc[:, i:i + 1]),
                     reads=[k.b("cols"), k.b(Gcn)], writes=[k.b("cols")])
            p.op("dve", lambda e, i=i: e.tensor_scalar_mul(out=wt_t[:, :], in0=M0[:, i * 8:(i + 1) * 8],
                                                           scalar1=cols[:, 2:3]),
                 reads=[k.b("M0"), k.b("cols")], writes=[k.b("wt_t")])
            p.op("dve", lambda e, i=i: e.scalar_tensor_tensor(out=wt_t[:, :], in0=M1[:, i * 8:(i + 1) * 8],
                                                              scalar=1.0, in1=wt_t[:, :],
                                                              op0=ALU.mult, op1=ALU.add),
                 reads=[k.b("M1"), k.b("wt_t"), k.b("cols")], writes=[k.b("wt_t")]) if False else None
            p.op("dve", lambda e, i=i: e.tensor_scalar_mul(out=t8[:, :], in0=M1[:, i * 8:(i + 1) * 8],
                                                           scalar1=cols[:, 5:6]),
                 reads=[k.b("M1"), k.b("cols")], writes=[k.b("t8")])
            p.op("dve", lambda e: e.tensor_add(out=wt_t[:, :], in0=wt_t[:, :], in1=t8[:, :]),
                 reads=[k.b("wt_t"), k.b("t8")], writes=[k.b("wt_t")])
            p.op("pool", lambda e, i=i: e.dma_start(out=wtab_d[i * 128:(i + 1) * 128, :], in_=wt_t[:, :]),
                 reads=[k.b("wt_t")], writes=[k.b("wtab")], dma=True)
        p.op("dve", lambda e: e.engine_nop(), reads=[k.b("wtab")])
        p.build(k.stack)
    return k


def build_moe1(final):
    k = Ker()
    nc, p = k.nc, k.p
    HID = 4096
    NTC = 4
    with k.stack:
        u_d = k.dram_in("u", (TOK, D))
        wt_d = k.dram_in("wtT", (8, TOK))
        xres_d = k.dram_in("xres", (TOK, D))
        w1_d = k.dram_in("w1", (8, D, HID))
        b1t_d = k.dram_in("b1tile", (8, 128, 32))
        w2_d = k.dram_in("w2", (8, HID, D))
        b2_d = k.dram_in("b2", (8, D))
        if final:
            y_d = k.dram_out("y", (TOK, D))
        else:
            y_d = k.dram_out("x6", (TOK, D))
            wh_d = k.dram_in("wh", (D, D))
            bh_d = k.dram_in("bh", (D,))
            wqp_d = k.dram_in("wqp", (D, D))
            bqp_d = k.dram_in("bqp", (D,))
            wkp_d = k.dram_in("wkp", (D, D))
            bkp_d = k.dram_in("bkp", (D,))
            wvp_d = k.dram_in("wvp", (D, D))
            bvp_d = k.dram_in("bvp", (D,))
            qp_d = k.dram_out("qp", (TOK, D))
            kp_d = k.dram_out("kp", (TOK, D))
            vp_d = k.dram_out("vp", (TOK, D))
            hp_d = k.dram_out("hp", (TOK, D))
        uT_dram = k.dram_tmp("uTscr", (KB, 128, TOK), BF16)

        k.prelude_ident()
        ps_r1 = k.ps("ps_r1", (128, 1024), F32)
        ps_a = k.ps("ps_a", (128, 512), F32)
        ps_tp = k.ps("ps_tp", (128, 128), BF16)
        ps_tf = k.ps("ps_tf", (128, 128), F32)

        x_t = k.sb("x_t", (128, D), F32)
        u_t = k.sb("u_t", (128, D), F32)
        tmp = k.sb("tmp", (128, D), F32)
        col = k.sb("col", (128, 8), F32)
        h_bf = k.sb("h_bf", (128, D), BF16)
        hT = k.sb("hT", (128, KB * 128), BF16)
        wt_bf = k.sb("wt_bf", (8, TOK), BF16)
        wt_f = k.sb("wt_f", (8, TOK), F32)
        p.op("sync", lambda e: e.dma_start(out=wt_f[:, :], in_=wt_d[:, :]), writes=[k.b("wt_f")], dma=True)
        p.op("dve", lambda e: e.tensor_copy(out=wt_bf[:, :], in_=wt_f[:, :]),
             reads=[k.b("wt_f")], writes=[k.b("wt_bf")])
        b2_f = k.sb("b2_f", (8, D), F32)
        b2_bf = k.sb("b2_bf", (8, D), BF16)
        p.op("sync", lambda e: e.dma_start(out=b2_f[:, :], in_=b2_d[:, :]), writes=[k.b("b2_f")], dma=True)
        p.op("dve", lambda e: e.tensor_copy(out=b2_bf[:, :], in_=b2_f[:, :]),
             reads=[k.b("b2_f")], writes=[k.b("b2_bf")])

        for tt in range(TT):
            r0 = tt * 128
            p.op("sync", lambda e, r0=r0: e.dma_start(out=u_t[:, :], in_=u_d[r0:r0 + 128, :]),
                 writes=[k.b("u_t")], dma=True)
            p.op("dve", lambda e: e.tensor_copy(out=h_bf[:, :], in_=u_t[:, :]),
                 reads=[k.b("u_t")], writes=[k.b("h_bf")])
            k.tp_blocks(h_bf, "h_bf", ps_tp, "ps_tp", hT, "hT", KB)
            for kb in range(KB):
                p.op("pool", lambda e, kb=kb, r0=r0: e.dma_start(out=uT_dram[kb, :, r0:r0 + 128],
                                                                 in_=hT[:, kb * 128:(kb + 1) * 128]),
                     reads=[k.b("hT")], writes=[k.b("uTscr")], dma=True)

        stag = k.sb("stag", (128, 1024), F32)
        stagB = k.sb("stagB", (128, 1024), F32)
        w1h_bf = k.sb("w1h_bf", (128, KB * 2048), BF16)
        w2h_bf = k.sb("w2h_bf", (128, 16 * D), BF16)
        b1c = k.sb("b1c", (128, 32), F32)
        hT_c2 = k.sb("hT_c2", (128, 2 * KB * 512), BF16)
        G_sb = k.sb("G_sb", (128, 16 * 512), BF16)
        acc2 = k.sb("acc2", (128, 2 * KB * 512), F32)
        wb_bf = k.sb("wb_bf", (128, 2 * 512), BF16)
        wrow = k.sb("wrow", (1, 1024), F32)

        for pp in range(2):
            p0 = pp * 1024
            for tci in range(2):
                t0 = p0 + tci * 512
                for kb in range(KB):
                    p.op("sync", lambda e, kb=kb, t0=t0, tci=tci: e.dma_start(
                        out=hT_c2[:, (tci * KB + kb) * 512:(tci * KB + kb + 1) * 512],
                        in_=uT_dram[kb, :, t0:t0 + 512]),
                        reads=[k.b("uTscr")], writes=[k.b("hT_c2")], dma=True)
            for e_i in range(8):
                p.op("sync", lambda e, e_i=e_i: e.dma_start(out=b1c[:, :], in_=b1t_d[e_i, :, :]),
                     writes=[k.b("b1c")], dma=True)
                p.op("sync", lambda e, e_i=e_i, p0=p0: e.dma_start(out=wrow[:, :],
                                                                   in_=wt_d[e_i, p0:p0 + 1024][None, :]),
                     writes=[k.b("wrow")], dma=True)
                for tci in range(2):
                    p.op("pe", lambda e, tci=tci: e.matmul(ps_r1[:, 512:1024], k.ones1[:, :],
                                                           wrow[:, tci * 512:(tci + 1) * 512],
                                                           start=True, stop=True),
                         reads=[k.b("ones1"), k.b("wrow")], writes=[k.b("ps_r1")])
                    p.op("act", lambda e, tci=tci: e.copy(out=wb_bf[:, tci * 512:(tci + 1) * 512],
                                                          in_=ps_r1[:, 512:1024]),
                         reads=[k.b("ps_r1")], writes=[k.b("wb_bf")])
                for jh in range(2):
                    for qq in range(2):
                        k.load_w_bf("w1h", w1_d[e_i][:, jh * 2048:(jh + 1) * 2048], KB, 1024, stag, "stag",
                                    w1h_bf, "w1h_bf", dst_off=qq * 1024, dst_stride=2048, col0=qq * 1024,
                                    stag2=stagB, stag2_name="stagB")
                    for half in range(2):
                        k.load_w_bf("w2h", w2_d[e_i][jh * 2048 + half * 1024: jh * 2048 + (half + 1) * 1024],
                                    KB, D, stag, "stag", w2h_bf, "w2h_bf", dst_off=half * KB * D,
                                    stag2=stagB, stag2_name="stagB")
                    for tci in range(2):
                        t0 = p0 + tci * 512
                        for jt in range(16):
                            jj = jh * 16 + jt
                            for kb in range(KB):
                                p.op("pe", lambda e, jt=jt, kb=kb, tci=tci: e.matmul(
                                    ps_a[:, :], w1h_bf[:, kb * 2048 + jt * 128: kb * 2048 + (jt + 1) * 128],
                                    hT_c2[:, (tci * KB + kb) * 512:(tci * KB + kb + 1) * 512],
                                    start=(kb == 0), stop=(kb == KB - 1)),
                                    reads=[k.b("w1h_bf"), k.b("hT_c2")], writes=[k.b("ps_a")],
                                    quiet=(kb != KB - 1))
                            p.op("act", lambda e, jt=jt, jj=jj: e.activation(out=G_sb[:, jt * 512:(jt + 1) * 512],
                                                                             in_=ps_a[:, :], func=AF.Relu,
                                                                             bias=b1c[:, jj:jj + 1]),
                                 reads=[k.b("ps_a"), k.b("b1c")], writes=[k.b("G_sb")])
                            p.op("dve", lambda e, jt=jt, tci=tci: e.tensor_mul(
                                out=G_sb[:, jt * 512:(jt + 1) * 512],
                                in0=G_sb[:, jt * 512:(jt + 1) * 512],
                                in1=wb_bf[:, tci * 512:(tci + 1) * 512]),
                                reads=[k.b("G_sb"), k.b("wb_bf")], writes=[k.b("G_sb")])
                        for dt in range(KB):
                            if e_i == 0 and jh == 0:
                                p.op("pe", lambda e, dt=dt, t0=t0: e.matmul(
                                    ps_r1[:, 0:512], b2_bf[:, dt * 128:(dt + 1) * 128], wt_bf[:, t0:t0 + 512],
                                    start=True, stop=False),
                                    reads=[k.b("b2_bf"), k.b("wt_bf")], writes=[k.b("ps_r1")], quiet=True)
                            for jb in range(16):
                                p.op("pe", lambda e, dt=dt, jb=jb: e.matmul(
                                    ps_r1[:, 0:512], w2h_bf[:, jb * D + dt * 128: jb * D + (dt + 1) * 128],
                                    G_sb[:, jb * 512:(jb + 1) * 512],
                                    start=(jb == 0 and not (e_i == 0 and jh == 0)), stop=(jb == 15)),
                                    reads=[k.b("w2h_bf"), k.b("G_sb")], writes=[k.b("ps_r1")],
                                    quiet=(jb != 15))
                            ai = (tci * KB + dt) * 512
                            if e_i == 0 and jh == 0:
                                p.op("dve", lambda e, ai=ai: e.tensor_copy(out=acc2[:, ai:ai + 512],
                                                                           in_=ps_r1[:, 0:512]),
                                     reads=[k.b("ps_r1")], writes=[k.b("acc2")])
                            else:
                                p.op("dve", lambda e, ai=ai: e.tensor_add(out=acc2[:, ai:ai + 512],
                                                                          in0=acc2[:, ai:ai + 512],
                                                                          in1=ps_r1[:, 0:512]),
                                     reads=[k.b("acc2"), k.b("ps_r1")], writes=[k.b("acc2")])
            for tci in range(2):
                for ti in range(4):
                    r0 = p0 + tci * 512 + ti * 128
                    p.op("sync", lambda e, r0=r0: e.dma_start(out=u_t[:, :], in_=xres_d[r0:r0 + 128, :]),
                         writes=[k.b("u_t")], dma=True)
                    for dt in range(KB):
                        ai = (tci * KB + dt) * 512 + ti * 128
                        p.op("pe", lambda e, ai=ai: e.transpose(
                            out=ps_tf[:, :], in_=acc2[:, ai:ai + 128],
                            identity=k.id_f32[:, :]),
                            reads=[k.b("acc2"), k.b("id_f32")], writes=[k.b("ps_tf")])
                        p.op("dve", lambda e, dt=dt: e.scalar_tensor_tensor(
                            out=x_t[:, dt * 128:(dt + 1) * 128], in0=ps_tf[:, :], scalar=0.5,
                            in1=u_t[:, dt * 128:(dt + 1) * 128], op0=ALU.mult, op1=ALU.add),
                            reads=[k.b("ps_tf"), k.b("u_t")], writes=[k.b("x_t")])
                    p.op("pool", lambda e, r0=r0: e.dma_start(out=y_d[r0:r0 + 128, :], in_=x_t[:, :]),
                         reads=[k.b("x_t")], writes=[k.b("y" if final else "x6")], dma=True)

        if not final:
            names = [("wqp", "bqp", "qp", True), ("wkp", "bkp", "kp", True),
                     ("wvp", "bvp", "vp", False), ("wh", "bh", "hp", False)]
            wds = {"wqp": wqp_d, "wkp": wkp_d, "wvp": wvp_d, "wh": wh_d}
            bds = {"bqp": bqp_d, "bkp": bkp_d, "bvp": bvp_d, "bh": bh_d}
            outs = {"qp": qp_d, "kp": kp_d, "vp": vp_d, "hp": hp_d}
            for wi, (wn, bn, on, sig) in enumerate(names):
                arena, an, off = (w1h_bf, "w1h_bf", (wi % 2) * KB * D)
                if wi >= 2:
                    arena, an = w2h_bf, "w2h_bf"
                k.load_w_bf(wn, wds[wn], KB, D, stag, "stag", arena, an, dst_off=off,
                            stag2=stagB, stag2_name="stagB")
            brow = k.sb("brow", (1, D), F32)
            out_sb = k.sb("out_sb", (128, 512), F32)
            for tt in range(TT):
                r0 = tt * 128
                p.op("sync", lambda e, r0=r0: e.dma_start(out=x_t[:, :], in_=y_d[r0:r0 + 128, :]),
                     reads=[k.b("x6")], writes=[k.b("x_t")], dma=True)
                k.ln_plain(x_t, "x_t", u_t, "u_t", tmp, "tmp", col, "col")
                p.op("act", lambda e: e.copy(out=h_bf[:, :], in_=u_t[:, :]),
                     reads=[k.b("u_t")], writes=[k.b("h_bf")])
                k.tp_blocks(h_bf, "h_bf", ps_tp, "ps_tp", hT, "hT", KB)
                for wi, (wn, bn, on, sig) in enumerate(names):
                    arena, an, off = (w1h_bf, "w1h_bf", (wi % 2) * KB * D)
                    if wi >= 2:
                        arena, an = w2h_bf, "w2h_bf"
                    p.op("sync", lambda e, bn=bn: e.dma_start(out=brow[:, :], in_=bds[bn][None, :]),
                         writes=[k.b("brow")], dma=True)
                    for jt in range(2):
                        ex = (lambda e, jt=jt: e.matmul(
                            ps_a[:, 0:512], k.ones1[:, :], brow[:, jt * 512:(jt + 1) * 512],
                            start=False, stop=True), [k.b("ones1"), k.b("brow")])
                        k.gemm(hT, "hT", arena, an, ps_a, "ps_a", 512, D, w_off=off + jt * 512, extra=ex)
                        p.op("act", lambda e, sig=sig: e.activation(out=out_sb[:, :], in_=ps_a[:, 0:512],
                                                                    func=(AF.Sigmoid if sig else AF.Copy)),
                             reads=[k.b("ps_a")], writes=[k.b("out_sb")])
                        p.op("pool", lambda e, r0=r0, on=on, jt=jt: e.dma_start(
                            out=outs[on][r0:r0 + 128, jt * 512:(jt + 1) * 512], in_=out_sb[:, :]),
                            reads=[k.b("out_sb")], writes=[k.b(on)], dma=True)
        fin = [k.b("y" if final else "x6")] + ([] if final else [k.b(n) for n in ("qp", "kp", "vp", "hp")])
        p.op("dve", lambda e: e.engine_nop(), reads=fin)
        p.build(k.stack)
    return k



def build_linattn():
    k = Ker()
    nc, p = k.nc, k.p
    NT = T // 128
    with k.stack:
        qp_d = k.dram_in("qp", (TOK, D))
        kp_d = k.dram_in("kpf", (T, D))
        vp_d = k.dram_in("vpf", (T, D))
        hp_d = k.dram_in("hp", (TOK, D))
        x6_d = k.dram_in("x6h", (TOK, D))
        lnw_d = k.dram_in("lnwp", (D,))
        lnb_d = k.dram_in("lnbp", (D,))
        beta_d = k.dram_in("betap", (D,))
        wo_d = k.dram_in("wop", (D, D))
        bo_d = k.dram_in("bop", (D,))
        x7_d = k.dram_out("x7", (TOK, D))
        u7_d = k.dram_out("u7", (TOK, D))

        k.prelude_ident()
        ps_r1 = k.ps("ps_r1", (128, 1024), F32)
        ps_kv = k.ps("ps_kv", (64, 65), F32)
        ps_tp = k.ps("ps_tp", (128, 128), BF16)
        ps_no = k.ps("ps_no", (128, 65), F32)
        ps_a = k.ps("ps_a", (128, 512), F32)
        ps_b = k.ps("ps_b", (128, 512), F32)

        lnw_bc = k.rank1_rows("lnw", lnw_d, D, ps_r1, "ps_r1")
        lnb_bc = k.rank1_rows("lnb", lnb_d, D, ps_r1, "ps_r1")
        beta_bc = k.rank1_rows("beta", beta_d, D, ps_r1, "ps_r1")
        bo_row = k.sb("bo_row", (1, D), F32)
        p.op("sync", lambda e: e.dma_start(out=bo_row[:, :], in_=bo_d[None, :]), writes=[k.b("bo_row")], dma=True)
        stag = k.sb("stag", (128, D), F32)
        wo_bf = k.sb("wo_bf", (128, KB * D), BF16)
        k.load_w_bf("wo", wo_d, KB, D, stag, "stag", wo_bf, "wo_bf")

        stagk = k.sb("stagk", (128, NT * HD), F32)
        kp_bf = k.sb("kp_bf", (128, NT * HD), BF16)
        v65 = k.sb("v65", (128, NT * 65), BF16)
        kv_all = k.sb("kv_all", (64, H * 65), BF16)
        for h in range(H):
            for quarter in range(4):
                p.op("sync", lambda e, h=h, quarter=quarter: e.dma_start(
                    out=stagk.rearrange("p (t d) -> p t d", d=HD)[:, quarter * 8:(quarter + 1) * 8, :],
                    in_=kp_d[:, h * HD:(h + 1) * HD].rearrange("(t p) d -> p t d", p=128)[:, quarter * 8:(quarter + 1) * 8, :]),
                    writes=[k.b("stagk")], dma=True)
            p.op("dve", lambda e: e.tensor_copy(out=kp_bf[:, :], in_=stagk[:, :]),
                 reads=[k.b("stagk")], writes=[k.b("kp_bf")])
            for quarter in range(4):
                p.op("sync", lambda e, h=h, quarter=quarter: e.dma_start(
                    out=stagk.rearrange("p (t d) -> p t d", d=HD)[:, quarter * 8:(quarter + 1) * 8, :],
                    in_=vp_d[:, h * HD:(h + 1) * HD].rearrange("(t p) d -> p t d", p=128)[:, quarter * 8:(quarter + 1) * 8, :]),
                    writes=[k.b("stagk")], dma=True)
            p.op("pool", lambda e: e.memset(v65[:, :], 1.0), writes=[k.b("v65")])
            p.op("dve", lambda e: e.tensor_copy(out=v65.rearrange("p (t d) -> p t d", d=65)[:, :, 0:HD],
                                                in_=stagk.rearrange("p (t d) -> p t d", d=HD)[:, :, :]),
                 reads=[k.b("stagk")], writes=[k.b("v65")])
            for nt in range(NT):
                p.op("pe", lambda e, h=h, nt=nt: e.matmul(ps_kv[:, :], kp_bf[:, nt * HD:(nt + 1) * HD],
                                                          v65[:, nt * 65:(nt + 1) * 65],
                                                          start=(nt == 0), stop=(nt == NT - 1)),
                     reads=[k.b("kp_bf"), k.b("v65")], writes=[k.b("ps_kv")], quiet=(nt != NT - 1))
            p.op("act", lambda e, h=h: e.copy(out=kv_all[:, h * 65:(h + 1) * 65], in_=ps_kv[:, :]),
                 reads=[k.b("ps_kv")], writes=[k.b("kv_all")])

        qp_t = k.sb("qp_t", (128, D), F32)
        qp_bf = k.sb("qp_bf", (128, D), BF16)
        qT_sb = k.sb("qT_sb", (64, 128), BF16)
        attn_sb = k.sb("attn_sb", (128, D), F32)
        hp_t = k.sb("hp_t", (128, D), F32)
        x_t = k.sb("x_t", (128, D), F32)
        u_t = k.sb("u_t", (128, D), F32)
        tmp = k.sb("tmp", (128, D), F32)
        col = k.sb("col", (128, 8), F32)
        g_bf = k.sb("g_bf", (128, D), BF16)
        gT = k.sb("gT", (128, KB * 128), BF16)
        rcol = k.sb("rcol", (128, 1), F32)

        for tt in range(TT):
            r0 = tt * 128
            p.op("sync", lambda e, r0=r0: e.dma_start(out=qp_t[:, :], in_=qp_d[r0:r0 + 128, :]),
                 writes=[k.b("qp_t")], dma=True)
            p.op("dve", lambda e: e.tensor_copy(out=qp_bf[:, :], in_=qp_t[:, :]),
                 reads=[k.b("qp_t")], writes=[k.b("qp_bf")])
            p.op("sync", lambda e, r0=r0: e.dma_start(out=hp_t[:, :], in_=hp_d[r0:r0 + 128, :]),
                 writes=[k.b("hp_t")], dma=True)
            for h in range(H):
                p.op("pe", lambda e, h=h: e.transpose(out=ps_tp[0:HD, :], in_=qp_bf[:, h * HD:(h + 1) * HD],
                                                      identity=k.id_bf[:, :]),
                     reads=[k.b("qp_bf"), k.b("id_bf")], writes=[k.b("ps_tp")])
                p.op("dve", lambda e: e.tensor_copy(out=qT_sb[:, :], in_=ps_tp[0:HD, :]),
                     reads=[k.b("ps_tp")], writes=[k.b("qT_sb")])
                p.op("pe", lambda e, h=h: e.matmul(ps_no[:, :], qT_sb[:, :], kv_all[:, h * 65:(h + 1) * 65],
                                                   start=True, stop=True),
                     reads=[k.b("qT_sb"), k.b("kv_all")], writes=[k.b("ps_no")])
                p.op("dve", lambda e: e.reciprocal(out=rcol[:, :], in_=ps_no[:, 64:65]),
                     reads=[k.b("ps_no")], writes=[k.b("rcol")])
                p.op("dve", lambda e, h=h: e.tensor_scalar_mul(out=attn_sb[:, h * HD:(h + 1) * HD],
                                                               in0=ps_no[:, 0:HD], scalar1=rcol[:, :]),
                     reads=[k.b("ps_no"), k.b("rcol")], writes=[k.b("attn_sb")])
            k.ln_plain(attn_sb, "attn_sb", u_t, "u_t", tmp, "tmp", col, "col")
            p.op("dve", lambda e: e.tensor_mul(out=u_t[:, :], in0=u_t[:, :], in1=lnw_bc[:, :]),
                 reads=[k.b("u_t"), k.b("lnw_bc")], writes=[k.b("u_t")])
            p.op("dve", lambda e: e.tensor_add(out=u_t[:, :], in0=u_t[:, :], in1=lnb_bc[:, :]),
                 reads=[k.b("u_t"), k.b("lnb_bc")], writes=[k.b("u_t")])
            p.op("dve", lambda e: e.tensor_add(out=hp_t[:, :], in0=hp_t[:, :], in1=beta_bc[:, :]),
                 reads=[k.b("hp_t"), k.b("beta_bc")], writes=[k.b("hp_t")])
            p.op("dve", lambda e: e.tensor_mul(out=g_bf[:, :], in0=u_t[:, :], in1=hp_t[:, :]),
                 reads=[k.b("u_t"), k.b("hp_t")], writes=[k.b("g_bf")])
            k.tp_blocks(g_bf, "g_bf", ps_tp, "ps_tp", gT, "gT", KB)
            p.op("sync", lambda e, r0=r0: e.dma_start(out=x_t[:, :], in_=x6_d[r0:r0 + 128, :]),
                 writes=[k.b("x_t")], dma=True)
            for jt in range(2):
                pg, pgn = (ps_a, "ps_a") if jt == 0 else (ps_b, "ps_b")
                ex = (lambda e, jt=jt, pg=pg: e.matmul(pg[:, 0:512], k.ones1[:, :],
                                                       bo_row[:, jt * 512:(jt + 1) * 512], start=False, stop=True),
                      [k.b("ones1"), k.b("bo_row")])
                k.gemm(gT, "gT", wo_bf, "wo_bf", pg, pgn, 512, D, w_off=jt * 512, extra=ex)
                p.op("act", lambda e, pg=pg: e.activation(out=tmp[:, 0:512], in_=pg[:, 0:512], func=AF.Relu),
                     reads=[k.b(pgn)], writes=[k.b("tmp")])
                p.op("dve", lambda e, jt=jt: e.tensor_add(out=x_t[:, jt * 512:(jt + 1) * 512],
                                                          in0=x_t[:, jt * 512:(jt + 1) * 512], in1=tmp[:, 0:512]),
                     reads=[k.b("x_t"), k.b("tmp")], writes=[k.b("x_t")])
            p.op("pool", lambda e, r0=r0: e.dma_start(out=x7_d[r0:r0 + 128, :], in_=x_t[:, :]),
                 reads=[k.b("x_t")], writes=[k.b("x7")], dma=True)
            k.ln_plain(x_t, "x_t", u_t, "u_t", tmp, "tmp", col, "col")
            p.op("pool", lambda e, r0=r0: e.dma_start(out=u7_d[r0:r0 + 128, :], in_=u_t[:, :]),
                 reads=[k.b("u_t")], writes=[k.b("u7")], dma=True)
        p.op("dve", lambda e: e.engine_nop(), reads=[k.b("x7"), k.b("u7")])
        p.build(k.stack)
    return k


_PROGS = {}


def _prog(name, fn, *args):
    if name not in _PROGS:
        _PROGS[name] = fn(*args)
    return _PROGS[name]


def _run(k, in_maps):
    res = run_bass_kernel_spmd(k.nc, in_maps, core_ids=list(range(NCORE)))
    return res.results


def _rows(a):
    return [np.ascontiguousarray(a[c * TOK:(c + 1) * TOK]) for c in range(NCORE)]


def _cat(res, key):
    return np.concatenate([res[c][key] for c in range(NCORE)], axis=0)


def _to_pairs_T(a):
    a4 = a.reshape(B, T, H, HD)
    return [np.ascontiguousarray(np.stack([a4[b, :, 2 * c + hi, :].T
                                           for hi in range(2) for b in range(B)]))
            for c in range(NCORE)]


def _to_pairs_V(a):
    a4 = a.reshape(B, T, H, HD)
    return [np.ascontiguousarray(np.stack([a4[b, :, 2 * c + hi, :]
                                           for hi in range(2) for b in range(B)]))
            for c in range(NCORE)]


def _from_pairs(res, key):
    full = np.empty((B, T, H, HD), np.float32)
    for c in range(NCORE):
        r = res[c][key]
        for hi in range(2):
            for b in range(B):
                full[b, :, 2 * c + hi, :] = r[hi * 4 + b]
    return np.ascontiguousarray(full.reshape(TOKTOT, D))


def kernel(L0_attn_x, L0_attn_Wqkv, L0_attn_bqkv, L0_attn_Wproj, L0_attn_bproj,
           L0_attn_w1, L0_attn_b1, L0_attn_w2, L0_attn_b2, L0_attn_c_param,
           L0_attn_L_multiplier, L0_ffn_ln1_scale, L0_ffn_ln1_shift, L0_ffn_wq,
           L0_ffn_wk, L0_ffn_wv, L0_ffn_wo, L0_ffn_bo, L0_ffn_ln2_scale,
           L0_ffn_ln2_shift, L0_ffn_gate_w, L0_ffn_w1, L0_ffn_w2, L0_ffn_w3,
           L1_attn_Wh, L1_attn_bh, L1_attn_Wq, L1_attn_bq, L1_attn_Wk, L1_attn_bk,
           L1_attn_Wv, L1_attn_bv, L1_attn_beta, L1_attn_ln_w, L1_attn_ln_b,
           L1_attn_Wo, L1_attn_bo, L1_ffn_wg, L1_ffn_w1, L1_ffn_b1, L1_ffn_w2,
           L1_ffn_b2):
    f32 = np.float32
    asf = lambda a: np.ascontiguousarray(np.asarray(a), dtype=f32)
    x0 = asf(L0_attn_x).reshape(TOKTOT, D)
    ident = np.eye(128, dtype=f32)
    LT = np.tril(np.ones((128, 128), f32))
    NEGU_t = ((1 - LT) * NEG).astype(f32)
    UTs = np.triu(np.ones((128, 128), f32), 1)
    iota_t = np.ascontiguousarray(np.broadcast_to(np.arange(T, dtype=f32), (128, T)))

    kb = _prog("bias", build_bias)
    w2bd = np.zeros((128, 64), f32)
    for qs in range(4):
        w2bd[qs * 32:(qs + 1) * 32, qs * 16:(qs + 1) * 16] = asf(L0_attn_w2)
    w1t = np.ascontiguousarray(np.tile(asf(L0_attn_w1)[0], 4)[:, None])
    b1t = np.ascontiguousarray(np.tile(asf(L0_attn_b1), 4)[:, None])
    b2t = np.ascontiguousarray(np.tile(asf(L0_attn_b2), 4)[:, None])
    cpar = asf(L0_attn_c_param).reshape(1, 1)
    lmul = asf(L0_attn_L_multiplier).reshape(1, 1)
    pidx = np.arange(128)
    gidx = np.arange(128)
    maps = []
    for c in range(NCORE):
        qc = (gidx[None, :] * 32 + c * 4 + pidx[:, None] // 32).astype(f32)
        qc64 = (gidx[None, :] * 32 + c * 4 + pidx[:64, None] // 16).astype(f32)
        maps.append({"iota": iota_t, "qcols": np.ascontiguousarray(qc),
                     "qcols64": np.ascontiguousarray(qc64), "w1t": w1t, "b1t": b1t,
                     "w2bd": w2bd, "b2t": b2t, "cpar": cpar, "lmul": lmul, "ident": ident})
    resb = _run(kb, maps)
    bias8 = np.stack([resb[c]["bias8"] for c in range(NCORE)])
    bias_q = bias8.transpose(1, 0, 2, 3, 4).reshape(T, H, T)

    kp = _prog("pre0", build_pre, False)
    xr = _rows(x0)
    com = {"ln1s": asf(L0_ffn_ln1_scale), "ln1b": asf(L0_ffn_ln1_shift),
           "wq": asf(L0_ffn_wq), "wk": asf(L0_ffn_wk), "wv": asf(L0_ffn_wv), "ident": ident}
    res = _run(kp, [dict(com, xin=xr[c]) for c in range(NCORE)])
    u0 = _cat(res, "u")
    q0, k0, v0 = _cat(res, "q"), _cat(res, "k"), _cat(res, "v")

    ka = _prog("attn", build_attn, False)
    qTs, kTs, vs = _to_pairs_T(q0), _to_pairs_T(k0), _to_pairs_V(v0)
    res = _run(ka, [{"qT": qTs[c], "kT": kTs[c], "v": vs[c], "ltmask": LT, "negu": NEGU_t,
                     "ident": ident} for c in range(NCORE)])
    ctx0 = _from_pairs(res, "ctxo")

    kg = _prog("big_qkv", build_big, "qkv")
    cr, ur = _rows(ctx0), _rows(u0)
    com = {"wo": asf(L0_ffn_wo), "bo": asf(L0_ffn_bo), "ln2s": asf(L0_ffn_ln2_scale),
           "ln2b": asf(L0_ffn_ln2_shift), "gw": asf(L0_ffn_gate_w), "w1": asf(L0_ffn_w1),
           "w2": asf(L0_ffn_w2), "w3": asf(L0_ffn_w3), "ident": ident,
           "wqkv": asf(L0_attn_Wqkv), "bqkv": asf(L0_attn_bqkv)}
    res = _run(kg, [dict(com, ctx=cr[c], u=ur[c], xres=xr[c]) for c in range(NCORE)])
    x1 = _cat(res, "xnew")
    qkv = _cat(res, "qkv")

    kab = _prog("attn_bias", build_attn, True)
    q2, k2, v2 = qkv[:, 0:D], qkv[:, D:2 * D], qkv[:, 2 * D:3 * D]
    qTs, kTs, vs = _to_pairs_T(q2), _to_pairs_T(k2), _to_pairs_V(v2)
    res = _run(kab, [{"qT": qTs[c], "kT": kTs[c], "v": vs[c], "ltmask": LT, "negu": NEGU_t,
                      "ident": ident,
                      "bias": np.ascontiguousarray(bias_q[:, 2 * c:2 * c + 2, :].transpose(1, 0, 2))}
                     for c in range(NCORE)])
    ctx2 = _from_pairs(res, "ctxo")

    kp1 = _prog("pre1", build_pre, True)
    cr, x1r = _rows(ctx2), _rows(x1)
    com = {"wproj": asf(L0_attn_Wproj), "bproj": asf(L0_attn_bproj),
           "ln1s": asf(L0_ffn_ln1_scale), "ln1b": asf(L0_ffn_ln1_shift),
           "wq": asf(L0_ffn_wq), "wk": asf(L0_ffn_wk), "wv": asf(L0_ffn_wv), "ident": ident}
    res = _run(kp1, [dict(com, ctx=cr[c], xbase=x1r[c]) for c in range(NCORE)])
    x3 = _cat(res, "xout")
    u3 = _cat(res, "u")
    q3, k3, v3 = _cat(res, "q"), _cat(res, "k"), _cat(res, "v")

    qTs, kTs, vs = _to_pairs_T(q3), _to_pairs_T(k3), _to_pairs_V(v3)
    res = _run(ka, [{"qT": qTs[c], "kT": kTs[c], "v": vs[c], "ltmask": LT, "negu": NEGU_t,
                     "ident": ident} for c in range(NCORE)])
    ctx3 = _from_pairs(res, "ctxo")

    kg1 = _prog("big_ln", build_big, "ln")
    cr, ur, x3r = _rows(ctx3), _rows(u3), _rows(x3)
    com = {"wo": asf(L0_ffn_wo), "bo": asf(L0_ffn_bo), "ln2s": asf(L0_ffn_ln2_scale),
           "ln2b": asf(L0_ffn_ln2_shift), "gw": asf(L0_ffn_gate_w), "w1": asf(L0_ffn_w1),
           "w2": asf(L0_ffn_w2), "w3": asf(L0_ffn_w3), "ident": ident}
    res = _run(kg1, [dict(com, ctx=cr[c], u=ur[c], xres=x3r[c]) for c in range(NCORE)])
    x5 = _cat(res, "xnew")
    u5 = _cat(res, "unorm")

    kgt = _prog("gate", build_gate)
    u5T = np.ascontiguousarray(u5.T)
    com = {"uT": u5T, "wg": asf(L1_ffn_wg), "utstrict": UTs, "ident": ident}
    res = _run(kgt, [com for _ in range(NCORE)])
    wtab1 = res[0]["wtab"]

    perm = ((np.arange(D) % 64) * 16 + np.arange(D) // 64).astype(np.int64)
    km = _prog("moeA", build_moe1, False)
    wtT1 = np.ascontiguousarray(wtab1.T)
    b1tile = np.ascontiguousarray(asf(L1_ffn_b1).reshape(8, 32, 128).transpose(0, 2, 1))
    u5r, x5r = _rows(u5), _rows(x5)
    com = {"w1": asf(L1_ffn_w1), "b1tile": b1tile, "w2": asf(L1_ffn_w2), "b2": asf(L1_ffn_b2),
           "ident": ident,
           "wh": np.ascontiguousarray(asf(L1_attn_Wh)[:, perm]), "bh": np.ascontiguousarray(asf(L1_attn_bh)[perm]),
           "wqp": np.ascontiguousarray(asf(L1_attn_Wq)[:, perm]), "bqp": np.ascontiguousarray(asf(L1_attn_bq)[perm]),
           "wkp": np.ascontiguousarray(asf(L1_attn_Wk)[:, perm]), "bkp": np.ascontiguousarray(asf(L1_attn_bk)[perm]),
           "wvp": np.ascontiguousarray(asf(L1_attn_Wv)[:, perm]), "bvp": np.ascontiguousarray(asf(L1_attn_bv)[perm])}
    res = _run(km, [dict(com, u=u5r[c], wtT=np.ascontiguousarray(wtT1[:, c * TOK:(c + 1) * TOK]),
                         xres=x5r[c]) for c in range(NCORE)])
    x6 = _cat(res, "x6")
    qp, kpp, vpp, hp = _cat(res, "qp"), _cat(res, "kp"), _cat(res, "vp"), _cat(res, "hp")

    kl = _prog("lin", build_linattn)
    com = {"lnwp": np.ascontiguousarray(asf(L1_attn_ln_w)[perm]),
           "lnbp": np.ascontiguousarray(asf(L1_attn_ln_b)[perm]),
           "betap": np.ascontiguousarray(asf(L1_attn_beta)[perm]),
           "wop": np.ascontiguousarray(asf(L1_attn_Wo)[perm, :]), "bop": asf(L1_attn_bo),
           "ident": ident}
    maps = []
    for c in range(NCORE):
        b, half = c // 2, c % 2
        r0 = b * T + half * TOK
        maps.append(dict(com, qp=np.ascontiguousarray(qp[r0:r0 + TOK]),
                         kpf=np.ascontiguousarray(kpp[b * T:(b + 1) * T]),
                         vpf=np.ascontiguousarray(vpp[b * T:(b + 1) * T]),
                         hp=np.ascontiguousarray(hp[r0:r0 + TOK]),
                         x6h=np.ascontiguousarray(x6[r0:r0 + TOK])))
    res = _run(kl, maps)
    x7 = np.empty((TOKTOT, D), f32)
    u7 = np.empty((TOKTOT, D), f32)
    for c in range(NCORE):
        b, half = c // 2, c % 2
        r0 = b * T + half * TOK
        x7[r0:r0 + TOK] = res[c]["x7"]
        u7[r0:r0 + TOK] = res[c]["u7"]

    u7T = np.ascontiguousarray(u7.T)
    com = {"uT": u7T, "wg": asf(L1_ffn_wg), "utstrict": UTs, "ident": ident}
    res = _run(kgt, [com for _ in range(NCORE)])
    wtab2 = res[0]["wtab"]

    kmf = _prog("moeB", build_moe1, True)
    wtT2 = np.ascontiguousarray(wtab2.T)
    u7r, x7r = _rows(u7), _rows(x7)
    com = {"w1": asf(L1_ffn_w1), "b1tile": b1tile, "w2": asf(L1_ffn_w2), "b2": asf(L1_ffn_b2),
           "ident": ident}
    res = _run(kmf, [dict(com, u=u7r[c], wtT=np.ascontiguousarray(wtT2[:, c * TOK:(c + 1) * TOK]),
                          xres=x7r[c]) for c in range(NCORE)])
    y = _cat(res, "y")
    return np.ascontiguousarray(y.reshape(B, T, D)).astype(np.float32)
